# version 05
# baseline (speedup 1.0000x reference)
import math
import os
import numpy as np
from contextlib import ExitStack

import concourse.bass as bass
import concourse.mybir as mybir
from concourse.bass_utils import run_bass_kernel_spmd

F32 = mybir.dt.float32
BF16 = mybir.dt.bfloat16
I32 = mybir.dt.int32
AF = mybir.ActivationFunctionType
ALU = mybir.AluOpType
AX = mybir.AxisListType

D = 2048
DK = 16
TOK = 1024
HALO = 256
WINT = TOK + HALO
MEM = 4096
I2 = 8192
U_MODE = 8
HEADS_MODE = 5
SQD = 1.0 / math.sqrt(D)
SQH = 1.0 / math.sqrt(128.0)
EPS = 1e-5


class KB:
    def __init__(self, debug_taps=(), upto=99):
        self.taps = tuple(debug_taps)
        self.upto = upto
        nc = bass.Bass("TRN2", target_bir_lowering=False, dynamic_dma_scratch_size=16384,
                       detect_race_conditions=False)
        self.nc = nc
        self.ctx = ExitStack()
        e = self.ctx.enter_context

        dt = nc.dram_tensor
        self.xs = dt("xs", [D, HALO + 2048], F32, kind="ExternalInput").ap()
        self.wqP = dt("wqP", [1024, 4096], F32, kind="ExternalInput").ap()
        self.wkP = dt("wkP", [1024, 4096], F32, kind="ExternalInput").ap()
        self.wvP = dt("wvP", [1024, 4096], F32, kind="ExternalInput").ap()
        self.woP = dt("woP", [1024, 4096], F32, kind="ExternalInput").ap()
        self.wkU = dt("wkU", [1024, 4096], F32, kind="ExternalInput").ap()
        self.mbP = dt("mbP", [2048, 4096], F32, kind="ExternalInput").ap()
        self.mb = dt("mb", [MEM, D], F32, kind="ExternalInput").ap()
        self.gwT = dt("gwT", [128, 16], F32, kind="ExternalInput").ap()
        self.gb = dt("gb", [1, 1], F32, kind="ExternalInput").ap()
        self.wgP = dt("wgP", [8192, 2048], F32, kind="ExternalInput").ap()
        self.wuP = dt("wuP", [8192, 2048], F32, kind="ExternalInput").ap()
        self.wdP = dt("wdP", [2048, 8192], F32, kind="ExternalInput").ap()
        self.masks_d = dt("masks", [128, 3 * 384], F32, kind="ExternalInput").ap()
        self.out = dt("out", [D, 2048], F32, kind="ExternalOutput").ap()
        self.dbg = {}
        for t in self.taps:
            self.dbg[t] = dt("dbg_" + t, [128, 20480], F32, kind="ExternalOutput").ap()

        self.A32 = e(nc.sbuf_tensor("A32", [128, 16384], F32))
        self.LNB = e(nc.sbuf_tensor("LNB", [128, 20480], BF16))
        self.CTX = e(nc.sbuf_tensor("CTXB", [128, 16384], BF16))
        self.UB = e(nc.sbuf_tensor("UBB", [128, 16384], BF16))
        self.SS = e(nc.sbuf_tensor("SSB", [128, 8192], BF16))

        self.masks = e(nc.sbuf_tensor("masks_s", [128, 3 * 384], F32))
        self.colidx = e(nc.sbuf_tensor("colidx", [128, 256], I32))
        self.identb = e(nc.sbuf_tensor("identb", [128, 128], BF16))
        self.identf = e(nc.sbuf_tensor("identf", [128, 128], F32))
        self.onescol = e(nc.sbuf_tensor("onescol", [128, 1], BF16))
        self.onesrow = e(nc.sbuf_tensor("onesrow", [1, 128], BF16))
        self.onesrowf = e(nc.sbuf_tensor("onesrowf", [1, 128], F32))
        ubf = self.UB[:, :].bitcast(F32)
        self.ms = [ubf[:, 2048 + 384 * i:2048 + 384 * (i + 1)] for i in range(2)]
        self.probs = [ubf[:, 2816 + 384 * i:2816 + 384 * (i + 1)] for i in range(2)]
        self.attn = [self.UB[:, 7168 + 384 * i:7168 + 384 * (i + 1)] for i in range(2)]
        self.attnT = [self.UB[:, 7936 + 384 * i:7936 + 384 * (i + 1)] for i in range(2)]
        self.nmax = [e(nc.sbuf_tensor(f"nmax{i}", [128, 1], F32)) for i in range(2)]
        self.esum = [e(nc.sbuf_tensor(f"esum{i}", [128, 1], F32)) for i in range(2)]
        self.ersum = [e(nc.sbuf_tensor(f"ersum{i}", [128, 1], F32)) for i in range(2)]
        self.packed = e(nc.sbuf_tensor("packed", [128, 256], F32))
        self.candA = e(nc.sbuf_tensor("candA", [128, 8 * 128], F32))
        self.run8 = e(nc.sbuf_tensor("run8", [128, 8], F32))
        self.vals8 = e(nc.sbuf_tensor("vals8", [128, 8], F32))
        self.ev8 = e(nc.sbuf_tensor("ev8", [128, 8], F32))
        self.idxA = e(nc.sbuf_tensor("idxA", [128, 64], I32))
        self.w8A = e(nc.sbuf_tensor("w8A", [128, 64], F32))
        self.xbs = [e(nc.sbuf_tensor(f"xbs{i}", [128, 512], BF16)) for i in range(2)]
        self.sqs = [e(nc.sbuf_tensor(f"sqs{i}", [128, 512], BF16)) for i in range(2)]
        self.nmr = ubf[0:1, 0:512]
        self.vrr = ubf[0:1, 512:1024]
        self.m2r = ubf[0:1, 1024:1536]
        self.rsr = ubf[0:1, 1536:2048]
        self.nmb = self.UB[0:1, 4096:4608]
        self.rsb = self.UB[0:1, 4608:5120]
        self.tmpf = e(nc.sbuf_tensor("tmpf", [128, 512], F32))
        ssf = self.SS[:, :].bitcast(F32)
        self.grow = ssf[0:1, 0:1024]
        self.gtmp = ssf[0:1, 1024:2048]
        self.neggb = e(nc.sbuf_tensor("neggb", [1, 1], F32))
        self.bypB = e(nc.sbuf_tensor("bypB", [128, 1024], F32))
        self.gwTs = e(nc.sbuf_tensor("gwTs", [128, 16], F32))
        self.xrest = self.candA[:, :]
        self.gbs = e(nc.sbuf_tensor("gbs", [1, 1], F32))

        self.pA = e(nc.psum_tensor("pA", [128, 512], F32))
        self.pB = e(nc.psum_tensor("pB", [128, 512], F32))
        self.pC = e(nc.psum_tensor("pC", [128, 512], F32))
        self.pD = e(nc.psum_tensor("pD", [128, 512], F32))
        self.pE = e(nc.psum_tensor("pE", [128, 512], F32))
        self.pS = e(nc.psum_tensor("pS", [1, 512], F32))
        self.pBc1 = e(nc.psum_tensor("pBc1", [128, 512], F32))
        self.pBc2 = e(nc.psum_tensor("pBc2", [128, 512], F32))

        self.sem = {}
        self.cnt = {}
        for s in ("dw", "dx", "dg", "do", "pe", "dv", "ac", "pl",
                  "dwA", "dwB", "dwC", "dwD", "dwE", "dgA", "dgB", "dxA", "dxB"):
            self.sem[s] = e(nc.semaphore(s))
            self.cnt[s] = 0
        self.bankc = {}
        self.store_done = 0
        assert nc.sbuf_bytes_remaining >= 0, nc.sbuf_bytes_remaining

    def open_block(self):
        if getattr(self, "blk", None) is not None:
            self.close_block()
        self.blk = self.nc.Block().__enter__()
        self.blk.vector(lambda e: e.nop())
        self.blk.scalar(lambda e: e.nop())
        self.blk.tensor(lambda e: e.nop())
        self.blk.gpsimd(lambda e: e.nop())
        self.blk.sync(lambda e: e.nop())

    def close_block(self):
        if self.blk is not None:
            self.blk.__exit__(None, None, None)
            self.blk = None

    def V(self, f):
        self.blk.vector(f)

    def S(self, f):
        self.blk.scalar(f)

    def T(self, f):
        self.blk.tensor(f)

    def G(self, f):
        self.blk.gpsimd(f)

    def Q(self, f):
        self.blk.sync(f)

    def inc(self, ins, s, n=1):
        ins.then_inc(self.sem[s], n)
        self.cnt[s] += n
        return self.cnt[s]

    def wv(self, eng, s, tgt):
        if tgt > 0:
            eng.wait_ge(self.sem[s], tgt)

    def bank_wait(self, eng, bank):
        c = self.bankc.get(bank.name)
        if c:
            self.wv(eng, c[0], c[1])

    def bank_set(self, bank, s):
        self.bankc[bank.name] = (s, self.cnt[s])

    def Abf(self, c0, c1):
        return self.A32[:, :].bitcast(BF16)[:, c0:c1]

    def Uf32(self, c0, c1):
        return self.UB[:, :].bitcast(F32)[:, c0:c1]

    def wsrc(self, wdram, c0, cols, k0=0, nk=DK):
        return wdram.rearrange("(k p) o -> p k o", p=128)[:, k0:k0 + nk, c0:c0 + cols]

    def wdst(self, tile_ap, cols):
        return tile_ap.rearrange("p (k f) -> p k f", f=cols)

    def dma(self, q, dst, src, s):
        return self.inc(q.dma_start(out=dst, in_=src), s, 16)

    def tap(self, name, srcs):
        if name not in self.taps:
            return
        me = self
        dst = self.dbg[name]
        if True:
            self.open_block()

            def _(g):
                col = 0
                for ap, ncols in srcs:
                    me.dma(g, dst[:, col:col + ncols], ap, "dg")
                    col += ncols
                g.wait_ge(me.sem["dg"], me.cnt["dg"])
            self.G(_)

    def ph_setup(self):
        me = self
        if True:
            self.open_block()

            def _(q):
                me.dma(q, me.masks[:, :], me.masks_d[:, :], "dxA")
                me.dma(q, me.gwTs[:, :], me.gwT[:, :], "dxA")
                me.dma(q, me.gbs[:, :], me.gb[:, :], "dxA")
            self.Q(_)

            def _(g):
                g.iota(me.colidx[:, :], [[1, 256]], base=0, channel_multiplier=0)
                g.memset(me.identb[:, :], 1.0)
                g.affine_select(me.identb[:, :], me.identb[:, :], [[1, 128]],
                                ALU.is_equal, 0.0, base=0, channel_multiplier=-1)
                g.memset(me.identf[:, :], 1.0)
                g.affine_select(me.identf[:, :], me.identf[:, :], [[1, 128]],
                                ALU.is_equal, 0.0, base=0, channel_multiplier=-1)
                g.memset(me.onescol[:, :], 1.0)
                g.memset(me.onesrow[:, :], 1.0)
                ins = g.memset(me.onesrowf[:, :], 1.0)
                me.inc(ins, "pl")
            self.G(_)

            def _(v):
                me.wv(v, "dxA", me.cnt["dxA"])
                me.wv(v, "pl", me.cnt["pl"])
                ins = v.tensor_scalar(me.neggb[:, :], me.gbs[:, :], -1.0, None, op0=ALU.mult)
                me.inc(ins, "dv")
            self.V(_)

    def ph_ln(self, mt, src):
        me = self
        base = mt * TOK
        slabs = [(0, 512), (512, 512), (1024, 256)] if src == "x" else [(0, 512), (512, 512)]
        ow = WINT if src == "x" else TOK
        if True:
            self.open_block()
            apply_done = {}
            for si, (s0, sw) in enumerate(slabs):
                pi = si % 2
                lsem = "dxA" if pi == 0 else "dxB"
                if src == "x":
                    def _(q, si=si, s0=s0, sw=sw, pi=pi, lsem=lsem):
                        if si >= 2:
                            me.wv(q, "dv", apply_done[si - 2])
                        if mt == 1:
                            me.wv(q, "do", me.store_done)
                        dst = me.A32[:, pi * 8192:pi * 8192 + 8192]
                        me.dma(q, me.wdst(dst, 512)[:, :, 0:sw],
                               me.xs.rearrange("(k p) t -> p k t", p=128)
                               [:, :, base + s0:base + s0 + sw], lsem)
                    self.Q(_)
                ld_tgt = self.cnt[lsem]

                def chunk(dk):
                    if src == "x":
                        return me.A32[:, pi * 8192 + dk * 512:pi * 8192 + dk * 512 + sw]
                    return me.A32[:, dk * TOK + s0:dk * TOK + s0 + sw]

                pe_at_slab_start = self.cnt["pe"]
                stats_tgt = {}
                for dk in range(DK):
                    par = dk % 2

                    def _(sc, dk=dk, sw=sw, par=par, lsem=lsem):
                        if dk == 0 and src == "x":
                            me.wv(sc, lsem, ld_tgt)
                        me.wv(sc, "pe", stats_tgt[dk - 2] if dk > 1 else pe_at_slab_start)
                        ch = chunk(dk)
                        sc.copy(me.xbs[par][:, 0:sw], ch)
                        ins = sc.square(me.sqs[par][:, 0:sw], ch)
                        me.inc(ins, "ac")
                    self.S(_)

                    def _(t, dk=dk, sw=sw, par=par):
                        me.wv(t, "ac", me.cnt["ac"])
                        if dk == 0:
                            me.bank_wait(t, me.pS)
                            me.bank_wait(t, me.pBc1)
                        t.matmul(me.pS[0:1, 0:sw], me.onescol[:, :], me.xbs[par][:, 0:sw],
                                 start=(dk == 0), stop=(dk == DK - 1))
                        ins = t.matmul(me.pBc1[0:1, 0:sw], me.onescol[:, :], me.sqs[par][:, 0:sw],
                                       start=(dk == 0), stop=(dk == DK - 1))
                        me.inc(ins, "pe")
                        stats_tgt[dk] = me.cnt["pe"]
                    self.T(_)

                def _(v, sw=sw):
                    me.wv(v, "pe", me.cnt["pe"])
                    v.tensor_scalar(me.nmr[0:1, 0:sw], me.pS[0:1, 0:sw], -1.0 / D, None, op0=ALU.mult)
                    v.tensor_scalar(me.vrr[0:1, 0:sw], me.pBc1[0:1, 0:sw], 1.0 / D, None, op0=ALU.mult)
                    v.tensor_mul(me.m2r[0:1, 0:sw], me.nmr[0:1, 0:sw], me.nmr[0:1, 0:sw])
                    v.tensor_sub(me.vrr[0:1, 0:sw], me.vrr[0:1, 0:sw], me.m2r[0:1, 0:sw])
                    ins = v.tensor_scalar(me.vrr[0:1, 0:sw], me.vrr[0:1, 0:sw], EPS, None,
                                          op0=ALU.add)
                    me.inc(ins, "dv")
                self.V(_)
                self.bank_set(self.pS, "dv")

                def _(sc, sw=sw):
                    me.wv(sc, "dv", me.cnt["dv"])
                    ins = sc.activation(me.rsr[0:1, 0:sw], me.vrr[0:1, 0:sw], AF.Sqrt)
                    me.inc(ins, "ac")
                self.S(_)

                def _(v, sw=sw):
                    me.wv(v, "ac", me.cnt["ac"])
                    ins = v.reciprocal(me.vrr[0:1, 0:sw], me.rsr[0:1, 0:sw])
                    me.inc(ins, "dv")
                    v.wait_ge(me.sem["dv"], me.cnt["dv"])
                    v.tensor_copy(me.nmb[0:1, 0:sw], me.nmr[0:1, 0:sw])
                    ins = v.tensor_copy(me.rsb[0:1, 0:sw], me.vrr[0:1, 0:sw])
                    me.inc(ins, "dv")
                self.V(_)
                self.bank_set(self.pBc1, "dv")

                def _(t, sw=sw):
                    me.wv(t, "dv", me.cnt["dv"])
                    me.bank_wait(t, me.pBc2)
                    me.bank_wait(t, me.pD)
                    t.matmul(me.pBc2[:, 0:sw], me.onesrow[:, :], me.nmb[0:1, 0:sw],
                             start=True, stop=True)
                    ins = t.matmul(me.pD[:, 0:sw], me.onesrow[:, :], me.rsb[0:1, 0:sw],
                                   start=True, stop=True)
                    me.inc(ins, "pe")
                self.T(_)
                bc_tgt = self.cnt["pe"]

                for dk in range(DK):
                    def _(v, dk=dk, s0=s0, sw=sw):
                        if dk == 0:
                            me.wv(v, "pe", bc_tgt)
                        ch = chunk(dk)
                        v.tensor_add(me.tmpf[:, 0:sw], ch, me.pBc2[:, 0:sw])
                        ins = v.tensor_mul(me.LNB[:, dk * ow + s0:dk * ow + s0 + sw],
                                           me.tmpf[:, 0:sw], me.pD[:, 0:sw])
                        me.inc(ins, "dv")
                    self.V(_)
                apply_done[si] = self.cnt["dv"]
                self.bank_set(self.pBc2, "dv")
                self.bank_set(self.pD, "dv")

    def ph_heads(self, mt):
        me = self
        if True:
            self.open_block()
            wld = {}
            war = {0: 0, 1: 0}
            proj_banks = [self.pA, self.pB, self.pE]
            tr_banks = [self.pC, self.pD]
            ctx_banks = [self.pBc1, self.pBc2]
            rot = {"p": 0}
            ctx_done = 0
            tr_done = {}
            exp_done = {}
            attn_done = {}
            ctxevac = {}

            for hp in range(8):
                pi = hp % 2
                woff = pi * 12288

                wsem = "dwA" if pi == 0 else "dwB"

                def _(g, hp=hp, pi=pi, woff=woff, wsem=wsem):
                    me.wv(g, "pe", war[pi])
                    if hp == 0:
                        me.wv(g, "do", me.store_done)
                    for w, wo in ((me.wqP, 0), (me.wkP, 4096), (me.wvP, 8192)):
                        me.dma(g, me.Abf(woff + wo, woff + wo + 4096),
                               w[hp * 128:(hp + 1) * 128, :], wsem)
                self.G(_)
                wld[hp] = (wsem, self.cnt[wsem])

                for h2 in range(2):
                    h = hp * 2 + h2
                    if HEADS_MODE < 2:
                        continue
                    for (name, wo, dst_off, spans) in (
                        ("q", 0, 24576, [(HALO, 512), (HALO + 512, 512)]),
                        ("k", 4096, 25600, [(0, 512), (512, 512), (1024, 256)]),
                        ("v", 8192, 26880, [(0, 512), (512, 512), (1024, 256)]),
                    ):
                        for (c0, cw) in spans:
                            bk = proj_banks[rot["p"] % 3]
                            rot["p"] += 1

                            def _(t, bk=bk, wo=wo, c0=c0, cw=cw, woff=woff, h2=h2, hp=hp):
                                me.wv(t, wld[hp][0], wld[hp][1])
                                me.bank_wait(t, bk)
                                for dk in range(DK):
                                    ins = t.matmul(
                                        bk[:, 0:cw],
                                        me.Abf(woff + wo + dk * 256 + h2 * 128,
                                               woff + wo + dk * 256 + h2 * 128 + 128),
                                        me.LNB[:, dk * WINT + c0:dk * WINT + c0 + cw],
                                        start=(dk == 0), stop=(dk == DK - 1))
                                me.inc(ins, "pe")
                            self.T(_)

                            def _(v, bk=bk, dst_off=dst_off, c0=c0, cw=cw, name=name,
                                  ctx_done=ctx_done):
                                me.wv(v, "pe", me.cnt["pe"])
                                me.wv(v, "pe", ctx_done)
                                b0 = c0 if name != "q" else c0 - HALO
                                ins = v.tensor_copy(me.Abf(dst_off + b0, dst_off + b0 + cw),
                                                    bk[:, 0:cw])
                                me.inc(ins, "dv")
                            self.V(_)
                            self.bank_set(bk, "dv")
                    vproj_done = self.cnt["dv"]
                    if h2 == 1:
                        war[pi] = self.cnt["pe"]

                    if HEADS_MODE < 3:
                        continue
                    for bi_, bstart in enumerate((0, 4, 8)):
                        nb = min(4, 10 - bstart)
                        bk = tr_banks[bi_ % 2]

                        def _(t, bk=bk, bstart=bstart, nb=nb, vproj_done=vproj_done):
                            me.wv(t, "dv", vproj_done)
                            me.bank_wait(t, bk)
                            bb = bk[:, :].bitcast(BF16)
                            for i in range(nb):
                                ins = t.transpose(bb[:, i * 128:(i + 1) * 128],
                                                  me.Abf(26880 + (bstart + i) * 128,
                                                         26880 + (bstart + i) * 128 + 128),
                                                  me.identb[:, :])
                            me.inc(ins, "pe")
                        self.T(_)

                        def _(v, bk=bk, bstart=bstart, nb=nb):
                            me.wv(v, "pe", me.cnt["pe"])
                            bb = bk[:, :].bitcast(BF16)
                            ins = v.tensor_copy(me.Abf(28160 + bstart * 128, 28160 + (bstart + nb) * 128),
                                                bb[:, 0:nb * 128])
                            me.inc(ins, "dv")
                        self.V(_)
                        self.bank_set(bk, "dv")

                    if HEADS_MODE < 4:
                        continue
                    for qp in range(4):
                        qt0, qt1 = 2 * qp, 2 * qp + 1
                        sbk0 = proj_banks[rot["p"] % 3]
                        rot["p"] += 1
                        sbk1 = proj_banks[rot["p"] % 3]
                        rot["p"] += 1

                        def mi_of(qt):
                            if mt == 0 and qt == 0:
                                return 1
                            if mt == 0 and qt == 1:
                                return 2
                            return 0

                        def _(t, sbk0=sbk0, sbk1=sbk1, qt0=qt0, qt1=qt1):
                            me.bank_wait(t, sbk0)
                            ins = t.matmul(sbk0[:, 0:384],
                                           me.Abf(24576 + qt0 * 128, 24576 + qt0 * 128 + 128),
                                           me.Abf(25600 + qt0 * 128, 25600 + qt0 * 128 + 384),
                                           start=True, stop=True)
                            me.inc(ins, "pe")
                            me.bank_wait(t, sbk1)
                            ins = t.matmul(sbk1[:, 0:384],
                                           me.Abf(24576 + qt1 * 128, 24576 + qt1 * 128 + 128),
                                           me.Abf(25600 + qt1 * 128, 25600 + qt1 * 128 + 384),
                                           start=True, stop=True)
                            me.inc(ins, "pe")
                        self.T(_)

                        def _(v, sbk0=sbk0, sbk1=sbk1, qt0=qt0, qt1=qt1):
                            me.wv(v, "pe", me.cnt["pe"])
                            me.wv(v, "ac", exp_done.get(0, 0))
                            v.scalar_tensor_tensor(me.ms[0][:, :], sbk0[:, 0:384], SQH,
                                                   me.masks[:, mi_of(qt0) * 384:mi_of(qt0) * 384 + 384],
                                                   op0=ALU.mult, op1=ALU.add)
                            ins = v.scalar_tensor_tensor(me.ms[1][:, :], sbk1[:, 0:384], SQH,
                                                         me.masks[:, mi_of(qt1) * 384:mi_of(qt1) * 384 + 384],
                                                         op0=ALU.mult, op1=ALU.add)
                            me.inc(ins, "dv")
                            ins = v.reduce_max(me.nmax[0][:, :], me.ms[0][:, :],
                                               axis=AX.X, negate=True)
                            me.inc(ins, "dv")
                            ins = v.reduce_max(me.nmax[1][:, :], me.ms[1][:, :],
                                               axis=AX.X, negate=True)
                            me.inc(ins, "dv")
                        self.V(_)
                        self.bank_set(sbk0, "dv")
                        self.bank_set(sbk1, "dv")

                        def _(sc):
                            me.wv(sc, "dv", me.cnt["dv"])
                            me.wv(sc, "dv", attn_done.get(0, 0))
                            sc.activation(me.probs[0][:, :], me.ms[0][:, :], AF.Exp,
                                          bias=me.nmax[0][:, 0:1], scale=1.0,
                                          accum_out=me.esum[0][:, 0:1])
                            ins = sc.activation(me.probs[1][:, :], me.ms[1][:, :], AF.Exp,
                                                bias=me.nmax[1][:, 0:1], scale=1.0,
                                                accum_out=me.esum[1][:, 0:1])
                            me.inc(ins, "ac")
                        self.S(_)
                        exp_done[0] = self.cnt["ac"]

                        def _(v):
                            me.wv(v, "ac", me.cnt["ac"])
                            me.wv(v, "pe", tr_done.get(0, 0))
                            ins = v.reciprocal(me.ersum[0][:, :], me.esum[0][:, :])
                            me.inc(ins, "dv")
                            ins = v.reciprocal(me.ersum[1][:, :], me.esum[1][:, :])
                            me.inc(ins, "dv")
                            v.wait_ge(me.sem["dv"], me.cnt["dv"])
                            v.tensor_scalar(me.attn[0][:, :], me.probs[0][:, :],
                                            me.ersum[0][:, 0:1], None, op0=ALU.mult)
                            ins = v.tensor_scalar(me.attn[1][:, :], me.probs[1][:, :],
                                                  me.ersum[1][:, 0:1], None, op0=ALU.mult)
                            me.inc(ins, "dv")
                        self.V(_)
                        attn_done[0] = self.cnt["dv"]

                        def _(t):
                            me.wv(t, "dv", me.cnt["dv"])
                            me.bank_wait(t, me.pC)
                            me.bank_wait(t, me.pD)
                            b0 = me.pC[:, :].bitcast(BF16)
                            b1 = me.pD[:, :].bitcast(BF16)
                            for kk in range(3):
                                t.transpose(b0[:, kk * 128:kk * 128 + 128],
                                            me.attn[0][:, kk * 128:kk * 128 + 128],
                                            me.identb[:, :])
                            for kk in range(3):
                                ins = t.transpose(b1[:, kk * 128:kk * 128 + 128],
                                                  me.attn[1][:, kk * 128:kk * 128 + 128],
                                                  me.identb[:, :])
                            me.inc(ins, "pe")
                        self.T(_)
                        tr_done[0] = self.cnt["pe"]

                        def _(v):
                            me.wv(v, "pe", me.cnt["pe"])
                            me.wv(v, "pe", ctxevac.get(0, 0))
                            v.tensor_copy(me.attnT[0][:, :], me.pC[:, :].bitcast(BF16)[:, 0:384])
                            ins = v.tensor_copy(me.attnT[1][:, :], me.pD[:, :].bitcast(BF16)[:, 0:384])
                            me.inc(ins, "dv")
                        self.V(_)
                        self.bank_set(me.pC, "dv")
                        self.bank_set(me.pD, "dv")

                        def _(t, qt0=qt0, qt1=qt1):
                            me.wv(t, "dv", me.cnt["dv"])
                            me.bank_wait(t, me.pBc1)
                            me.bank_wait(t, me.pBc2)
                            for kk in range(3):
                                t.matmul(me.pBc1[:, 0:128],
                                         me.Abf(28160 + (qt0 + kk) * 128, 28160 + (qt0 + kk) * 128 + 128),
                                         me.attnT[0][:, kk * 128:kk * 128 + 128],
                                         start=(kk == 0), stop=(kk == 2))
                            for kk in range(3):
                                ins = t.matmul(me.pBc2[:, 0:128],
                                               me.Abf(28160 + (qt1 + kk) * 128, 28160 + (qt1 + kk) * 128 + 128),
                                               me.attnT[1][:, kk * 128:kk * 128 + 128],
                                               start=(kk == 0), stop=(kk == 2))
                            me.inc(ins, "pe")
                        self.T(_)
                        ctxevac[0] = self.cnt["pe"]

                        def _(sc, h=h, qt0=qt0, qt1=qt1):
                            me.wv(sc, "pe", me.cnt["pe"])
                            sc.copy(me.CTX[:, h * TOK + qt0 * 128:h * TOK + qt0 * 128 + 128],
                                    me.pBc1[:, 0:128])
                            ins = sc.copy(me.CTX[:, h * TOK + qt1 * 128:h * TOK + qt1 * 128 + 128],
                                          me.pBc2[:, 0:128])
                            me.inc(ins, "ac")
                        self.S(_)
                        self.bank_set(me.pBc1, "ac")
                        self.bank_set(me.pBc2, "ac")
                    ctx_done = self.cnt["pe"]

    def ph_u(self, mt):
        me = self
        if True:
            self.open_block()
            banks = [self.pA, self.pB]
            war = {0: 0, 1: 0}
            for ec in range(8):
                pi = ec % 2

                wsem = "dwA" if pi == 0 else "dwB"

                def _(g, ec=ec, pi=pi, wsem=wsem):
                    me.wv(g, "pe", war[pi])
                    me.dma(g, me.Abf(pi * 8192, pi * 8192 + 4096),
                           me.wkU[ec * 128:(ec + 1) * 128, :], wsem)
                self.G(_)
                ld = (wsem, self.cnt[wsem])
                for es in range(2):
                    for tp in range(2):
                        bk = banks[(es * 2 + tp) % 2]

                        def _(t, bk=bk, es=es, tp=tp, pi=pi, ld=ld):
                            me.wv(t, ld[0], ld[1])
                            me.bank_wait(t, bk)
                            for dk in range(DK):
                                ins = t.matmul(
                                    bk[:, 0:512],
                                    me.Abf(pi * 8192 + dk * 256 + es * 128,
                                           pi * 8192 + dk * 256 + es * 128 + 128),
                                    me.LNB[:, dk * WINT + HALO + tp * 512:
                                           dk * WINT + HALO + tp * 512 + 512],
                                    start=(dk == 0), stop=(dk == DK - 1))
                            me.inc(ins, "pe")
                        self.T(_)

                        def _(v, bk=bk, ec=ec, es=es, tp=tp):
                            me.wv(v, "pe", me.cnt["pe"])
                            ek = ec * 2 + es
                            ins = v.tensor_copy(me.UB[:, ek * TOK + tp * 512:ek * TOK + tp * 512 + 512],
                                                bk[:, 0:512])
                            me.inc(ins, "dv")
                        self.V(_)
                        self.bank_set(bk, "dv")
                war[pi] = self.cnt["pe"]

    def ph_olocal_gate(self, mt):
        me = self
        if True:
            self.open_block()
            banks = [self.pA, self.pB]
            war = {0: 0, 1: 0}
            gate_banks = {0: self.pS, 1: self.pBc1}
            for oc4 in range(8):
                pi = oc4 % 2

                wsem = "dwA" if pi == 0 else "dwB"

                def _(g, oc4=oc4, pi=pi, wsem=wsem):
                    me.wv(g, "pe", war[pi])
                    me.dma(g, me.LNB[:, pi * 4096:pi * 4096 + 4096],
                           me.woP[oc4 * 128:(oc4 + 1) * 128, :], wsem)
                self.G(_)
                ld = (wsem, self.cnt[wsem])
                for os_ in range(2):
                    oc = oc4 * 2 + os_
                    for tp in range(2):
                        bk = banks[(os_ * 2 + tp) % 2]

                        def _(t, bk=bk, os_=os_, tp=tp, pi=pi, ld=ld):
                            me.wv(t, ld[0], ld[1])
                            me.bank_wait(t, bk)
                            for dk in range(DK):
                                ins = t.matmul(
                                    bk[:, 0:512],
                                    me.LNB[:, pi * 4096 + dk * 256 + os_ * 128:
                                           pi * 4096 + dk * 256 + os_ * 128 + 128],
                                    me.CTX[:, dk * TOK + tp * 512:dk * TOK + tp * 512 + 512],
                                    start=(dk == 0), stop=(dk == DK - 1))
                            me.inc(ins, "pe")
                        self.T(_)

                        def _(v, bk=bk, oc=oc, tp=tp):
                            me.wv(v, "pe", me.cnt["pe"])
                            ins = v.tensor_copy(me.A32[:, oc * TOK + tp * 512:oc * TOK + tp * 512 + 512],
                                                bk[:, 0:512])
                            me.inc(ins, "dv")
                        self.V(_)
                        self.bank_set(bk, "dv")
                war[pi] = self.cnt["pe"]
            olocal_done = self.cnt["dv"]

            for tp in range(2):
                for oc in range(DK):
                    def _(t, oc=oc, tp=tp):
                        if oc == 0:
                            me.wv(t, "dv", olocal_done)
                            me.bank_wait(t, gate_banks[tp])
                        ins = t.matmul(gate_banks[tp][0:1, 0:512], me.gwTs[:, oc:oc + 1],
                                       me.A32[:, oc * TOK + tp * 512:oc * TOK + tp * 512 + 512],
                                       start=(oc == 0), stop=(oc == DK - 1))
                        me.inc(ins, "pe")
                    self.T(_)

            def _(sc):
                me.wv(sc, "pe", me.cnt["pe"])
                sc.activation(me.gtmp[0:1, 0:512], me.pS[0:1, 0:512], AF.Exp,
                              bias=me.neggb[0:1, 0:1], scale=-1.0)
                ins = sc.activation(me.gtmp[0:1, 512:1024], me.pBc1[0:1, 0:512], AF.Exp,
                                    bias=me.neggb[0:1, 0:1], scale=-1.0)
                me.inc(ins, "ac")
            self.S(_)
            self.bank_set(self.pS, "ac")
            self.bank_set(self.pBc1, "ac")

            def _(v):
                me.wv(v, "ac", me.cnt["ac"])
                v.tensor_scalar(me.grow[0:1, :], me.gtmp[0:1, :], 1.0, None, op0=ALU.add)
                ins = v.reciprocal(me.gtmp[0:1, :], me.grow[0:1, :])
                me.inc(ins, "dv")
                v.wait_ge(me.sem["dv"], me.cnt["dv"])
                ins = v.engine_nop()
                me.inc(ins, "dv")
            self.V(_)

            for tp in range(2):
                def _(t, tp=tp):
                    me.wv(t, "dv", me.cnt["dv"])
                    me.bank_wait(t, me.pBc2)
                    ins = t.matmul(me.pBc2[:, 0:512], me.onesrowf[:, :],
                                   me.gtmp[0:1, tp * 512:tp * 512 + 512],
                                   start=True, stop=True)
                    me.inc(ins, "pe")
                self.T(_)

                def _(v, tp=tp):
                    me.wv(v, "pe", me.cnt["pe"])
                    ins = v.tensor_copy(me.bypB[:, tp * 512:tp * 512 + 512], me.pBc2[:, 0:512])
                    me.inc(ins, "dv")
                self.V(_)
                self.bank_set(self.pBc2, "dv")

    def ph_sim_topk(self, mt):
        me = self
        if True:
            self.open_block()
            banks = [self.pA, self.pB]
            war = {0: 0, 1: 0}
            for mc in range(16):
                pi = mc % 2

                wsem = "dwA" if pi == 0 else "dwB"

                def _(g, mc=mc, pi=pi, wsem=wsem):
                    me.wv(g, "pe", war[pi])
                    me.dma(g, me.SS[:, pi * 4096:pi * 4096 + 4096],
                           me.mbP[mc * 128:(mc + 1) * 128, :], wsem)
                self.G(_)
                ld = (wsem, self.cnt[wsem])
                for tt in range(8):
                    bk = banks[tt % 2]

                    def _(t, bk=bk, tt=tt, pi=pi, ld=ld):
                        me.wv(t, ld[0], ld[1])
                        me.bank_wait(t, bk)
                        for ek in range(DK):
                            ins = t.matmul(
                                bk[:, 0:256],
                                me.UB[:, ek * TOK + tt * 128:ek * TOK + tt * 128 + 128],
                                me.SS[:, pi * 4096 + ek * 256:pi * 4096 + ek * 256 + 256],
                                start=(ek == 0), stop=(ek == DK - 1))
                        me.inc(ins, "pe")
                    self.T(_)

                    def _(v, bk=bk, mc=mc, tt=tt):
                        me.wv(v, "pe", me.cnt["pe"])
                        bi = bk[:, :].bitcast(I32)
                        ins = v.tensor_scalar(me.packed[:, :].bitcast(I32), bi[:, 0:256],
                                              -4096, mc << 8, op0=ALU.bitwise_and,
                                              op1=ALU.bitwise_or)
                        me.inc(ins, "dv")
                        v.tensor_tensor(me.packed[:, :].bitcast(I32),
                                        me.packed[:, :].bitcast(I32),
                                        me.colidx[:, :], op=ALU.bitwise_or)
                        ins = v.max(out=me.candA[:, tt * 128 + mc * 8:tt * 128 + mc * 8 + 8],
                                    in_=me.packed[:, :])
                        me.inc(ins, "dv")
                    self.V(_)
                    self.bank_set(bk, "dv")
                war[pi] = self.cnt["pe"]

            for tt in range(8):
                def _(v, tt=tt):
                    me.wv(v, "dv", me.cnt["dv"])
                    ins = v.max(out=me.run8[:, :], in_=me.candA[:, tt * 128:tt * 128 + 128])
                    me.inc(ins, "dv")
                    v.wait_ge(me.sem["dv"], me.cnt["dv"])
                    v.tensor_scalar(me.idxA[:, tt * 8:tt * 8 + 8], me.run8[:, :].bitcast(I32),
                                    4095, None, op0=ALU.bitwise_and)
                    ins = v.tensor_scalar(me.vals8[:, :].bitcast(I32), me.run8[:, :].bitcast(I32),
                                          -4096, None, op0=ALU.bitwise_and)
                    me.inc(ins, "dv")
                    v.wait_ge(me.sem["dv"], me.cnt["dv"])
                    ins = v.tensor_scalar(me.vals8[:, :], me.vals8[:, :], SQD, None, op0=ALU.mult)
                    me.inc(ins, "dv")
                    v.wait_ge(me.sem["dv"], me.cnt["dv"])
                    ins = v.reduce_max(me.nmax[0][:, :], me.vals8[:, :], axis=AX.X, negate=True)
                    me.inc(ins, "dv")
                self.V(_)

                def _(sc):
                    me.wv(sc, "dv", me.cnt["dv"])
                    ins = sc.activation(me.ev8[:, :], me.vals8[:, :], AF.Exp,
                                        bias=me.nmax[0][:, 0:1], scale=1.0,
                                        accum_out=me.esum[0][:, 0:1])
                    me.inc(ins, "ac")
                self.S(_)

                def _(v, tt=tt):
                    me.wv(v, "ac", me.cnt["ac"])
                    ins = v.reciprocal(me.ersum[0][:, :], me.esum[0][:, :])
                    me.inc(ins, "dv")
                    v.wait_ge(me.sem["dv"], me.cnt["dv"])
                    ins = v.tensor_scalar(me.w8A[:, tt * 8:tt * 8 + 8], me.ev8[:, :],
                                          me.ersum[0][:, 0:1], None, op0=ALU.mult)
                    me.inc(ins, "dv")
                self.V(_)

    def ph_mem(self, mt):
        me = self
        base = mt * TOK
        if True:
            self.open_block()
            tr_banks = [self.pC, self.pD]
            banks = [self.pA, self.pB]
            mac_done = {}
            ytm_war = {0: 0, 1: 0}
            for half in range(2):
                for j in range(4):
                    tt = half * 4 + j
                    yo = 2048 + (j % 2) * 2048
                    for k in range(8):
                        seq = tt * 8 + k

                        gsem = "dgA" if k % 2 == 0 else "dgB"

                        def _(g, tt=tt, k=k, seq=seq, gsem=gsem):
                            me.wv(g, "dv", mac_done.get(seq - 2, 0))
                            ins = g.indirect_dma_start(
                                out=me.UB[:, (k % 2) * 2048:(k % 2) * 2048 + 2048],
                                out_offset=None, in_=me.mb[:, :],
                                in_offset=bass.IndirectOffsetOnAxis(
                                    me.idxA[:, tt * 8 + k:tt * 8 + k + 1], axis=0))
                            me.inc(ins, gsem, 16)
                        self.G(_)

                        def _(v, tt=tt, k=k, yo=yo, j=j, gsem=gsem):
                            me.wv(v, gsem, me.cnt[gsem])
                            if k == 0:
                                me.wv(v, "ac", ytm_war[j % 2])
                            gbuf = me.UB[:, (k % 2) * 2048:(k % 2) * 2048 + 2048]
                            ytm = me.Uf32(yo, yo + 2048)
                            if k == 0:
                                ins = v.tensor_scalar(ytm, gbuf, me.w8A[:, tt * 8:tt * 8 + 1],
                                                      None, op0=ALU.mult)
                            else:
                                ins = v.scalar_tensor_tensor(
                                    ytm, gbuf, me.w8A[:, tt * 8 + k:tt * 8 + k + 1],
                                    ytm, op0=ALU.mult, op1=ALU.add)
                            me.inc(ins, "dv")
                        self.V(_)
                        mac_done[seq] = self.cnt["dv"]

                    for bq in range(4):
                        bk = tr_banks[bq % 2]

                        def _(t, bk=bk, bq=bq, yo=yo):
                            me.wv(t, "dv", me.cnt["dv"])
                            me.bank_wait(t, bk)
                            for i in range(4):
                                ins = t.transpose(
                                    bk[:, i * 128:i * 128 + 128],
                                    me.Uf32(yo + (bq * 4 + i) * 128, yo + (bq * 4 + i) * 128 + 128),
                                    me.identf[:, :])
                            me.inc(ins, "pe")
                        self.T(_)

                        def _(sc, bk=bk, bq=bq, half=half, j=j):
                            me.wv(sc, "pe", me.cnt["pe"])
                            for i in range(4):
                                dk = bq * 4 + i
                                ins = sc.copy(
                                    me.CTX[:, dk * TOK + half * 512 + j * 128:
                                           dk * TOK + half * 512 + j * 128 + 128],
                                    bk[:, i * 128:i * 128 + 128])
                            me.inc(ins, "ac")
                        self.S(_)
                        self.bank_set(bk, "ac")
                    ytm_war[j % 2] = self.cnt["ac"]
                ypre_done = self.cnt["ac"]
                self.close_block()
                self.open_block()

                sswar = self.cnt["pe"]
                zwar = {0: sswar, 1: sswar}
                for zc in range(8):
                    pi = zc % 2

                    wsem = "dwA" if pi == 0 else "dwB"

                    def _(g, zc=zc, pi=pi, wsem=wsem):
                        me.wv(g, "pe", zwar[pi])
                        me.dma(g, me.SS[:, pi * 4096:pi * 4096 + 4096],
                               me.wvP[zc * 128:(zc + 1) * 128, :], wsem)
                    self.G(_)
                    ld = (wsem, self.cnt[wsem])
                    for ic2 in range(2):
                        ic = zc * 2 + ic2
                        bk = banks[ic % 2]

                        def _(t, bk=bk, ic2=ic2, pi=pi, ld=ld, half=half):
                            me.wv(t, ld[0], ld[1])
                            me.wv(t, "ac", ypre_done)
                            me.bank_wait(t, bk)
                            for dk in range(DK):
                                ins = t.matmul(
                                    bk[:, 0:512],
                                    me.SS[:, pi * 4096 + dk * 256 + ic2 * 128:
                                           pi * 4096 + dk * 256 + ic2 * 128 + 128],
                                    me.CTX[:, dk * TOK + half * 512:dk * TOK + half * 512 + 512],
                                    start=(dk == 0), stop=(dk == DK - 1))
                            me.inc(ins, "pe")
                        self.T(_)

                        def _(v, bk=bk, ic=ic, half=half):
                            me.wv(v, "pe", me.cnt["pe"])
                            ins = v.tensor_copy(
                                me.LNB[:, ic * TOK + half * 512:ic * TOK + half * 512 + 512],
                                bk[:, 0:512])
                            me.inc(ins, "dv")
                        self.V(_)
                        self.bank_set(bk, "dv")
                    zwar[pi] = self.cnt["pe"]

                self.close_block()
                self.open_block()
                z_done = self.cnt["dv"]
                sswar2 = self.cnt["pe"]
                owar = {0: sswar2, 1: sswar2}
                xres_war = {0: 0, 1: 0}
                for oc2 in range(8):
                    pi = oc2 % 2

                    wsem = "dwA" if pi == 0 else "dwB"

                    def _(g, oc2=oc2, pi=pi, wsem=wsem):
                        me.wv(g, "pe", owar[pi])
                        me.dma(g, me.SS[:, pi * 4096:pi * 4096 + 4096],
                               me.woP[oc2 * 128:(oc2 + 1) * 128, :], wsem)
                    self.G(_)
                    ld = (wsem, self.cnt[wsem])
                    for o2 in range(2):
                        oc = oc2 * 2 + o2
                        bk = banks[oc % 2]
                        xb = (oc % 2) * 512

                        xsem = "dxA" if oc % 2 == 0 else "dxB"

                        def _(q, oc=oc, xb=xb, half=half, xsem=xsem):
                            me.wv(q, "dv", xres_war[oc % 2])
                            src = me.xs.rearrange("(k p) t -> p k t", p=128)[
                                :, oc:oc + 1, HALO + base + half * 512:HALO + base + half * 512 + 512]
                            me.dma(q, me.xrest[:, xb:xb + 512].rearrange("p (a t) -> p a t", a=1),
                                   src, xsem)
                        self.Q(_)
                        xld = (xsem, self.cnt[xsem])

                        def _(t, bk=bk, o2=o2, pi=pi, ld=ld, half=half):
                            me.wv(t, ld[0], ld[1])
                            me.wv(t, "dv", z_done)
                            me.bank_wait(t, bk)
                            for ik in range(DK):
                                ins = t.matmul(
                                    bk[:, 0:512],
                                    me.SS[:, pi * 4096 + ik * 256 + o2 * 128:
                                           pi * 4096 + ik * 256 + o2 * 128 + 128],
                                    me.LNB[:, ik * TOK + half * 512:ik * TOK + half * 512 + 512],
                                    start=(ik == 0), stop=(ik == DK - 1))
                            me.inc(ins, "pe")
                        self.T(_)

                        def _(v, bk=bk, oc=oc, xb=xb, half=half, xld=xld):
                            me.wv(v, "pe", me.cnt["pe"])
                            me.wv(v, xld[0], xld[1])
                            a = me.A32[:, oc * TOK + half * 512:oc * TOK + half * 512 + 512]
                            v.tensor_mul(me.tmpf[:, :], bk[:, 0:512],
                                         me.bypB[:, half * 512:half * 512 + 512])
                            v.tensor_add(me.tmpf[:, :], me.tmpf[:, :], a)
                            ins = v.tensor_add(a, me.tmpf[:, :], me.xrest[:, xb:xb + 512])
                            me.inc(ins, "dv")
                        self.V(_)
                        self.bank_set(bk, "dv")
                        xres_war[oc % 2] = self.cnt["dv"]
                    owar[pi] = self.cnt["pe"]
            self.close_block()

    def ph_ffn(self, mt):
        me = self
        if True:
            self.open_block()
            prev_sp_pl = 0
            prev_sp_dv = 0
            dCD = [self.pC, self.pD]
            QS = 6144
            gupld = {}
            ju_done = {}
            dload = {}
            sp_arr = {}
            down_done = {}

            JSEM = ("dwA", "dwB", "dwC", "dwD")

            def load_gu(sg, j):
                js = JSEM[j]

                def _(g, sg=sg, j=j, js=js):
                    me.wv(g, "pe", ju_done.get((sg - 1, j), 0))
                    r0 = (sg * 4 + j) * 128
                    me.dma(g, me.CTX[:, j * 2048:j * 2048 + 2048],
                           me.wgP[r0:r0 + 128, :], js)
                    me.dma(g, me.CTX[:, 8192 + j * 2048:8192 + j * 2048 + 2048],
                           me.wuP[r0:r0 + 128, :], js)
                self.G(_)
                gupld[(sg, j)] = (js, self.cnt[js])

            def load_down(sg):
                def _(g, sg=sg):
                    me.wv(g, "pe", down_done.get(sg - 1, 0))
                    me.dma(g, me.SS[:, 0:8192], me.wdP[sg * 128:(sg + 1) * 128, :], "dwE")
                self.G(_)
                dload[sg] = self.cnt["dwE"]

            def hsp(par, c0, c1):
                if par == 0:
                    return me.UB[:, c0:c1]
                return me.LNB[:, 16384 + c0:16384 + c1]

            def emit_down(sg):
                par = sg % 2
                for oc in range(16):
                    for tp in range(2):
                        bk = dCD[(oc * 2 + tp) % 2]

                        def _(t, bk=bk, oc=oc, tp=tp, sg=sg, par=par):
                            me.wv(t, "dwE", dload[sg])
                            me.wv(t, "dv", sp_arr[sg])
                            me.bank_wait(t, bk)
                            for ic in range(4):
                                ins = t.matmul(
                                    bk[:, 0:512],
                                    me.SS[:, ic * 2048 + oc * 128:ic * 2048 + oc * 128 + 128],
                                    hsp(par, ic * 1024 + tp * 512, ic * 1024 + tp * 512 + 512),
                                    start=(ic == 0), stop=(ic == 3))
                            me.inc(ins, "pe")
                        self.T(_)

                        def _(v, bk=bk, oc=oc, tp=tp):
                            me.wv(v, "pe", me.cnt["pe"])
                            a = me.A32[:, oc * TOK + tp * 512:oc * TOK + tp * 512 + 512]
                            ins = v.tensor_add(a, a, bk[:, 0:512])
                            me.inc(ins, "dv")
                        self.V(_)
                        self.bank_set(bk, "dv")
                down_done[sg] = self.cnt["pe"]

            for j in range(4):
                load_gu(0, j)
            load_down(0)

            for sg in range(16):
                gbanks = [self.pA, self.pE]
                ubanks = [self.pB, self.pBc1]
                stmp = [QS, QS + 1024]
                stmp_war = {0: prev_sp_dv, 1: prev_sp_dv}
                for j in range(4):
                    for tp in range(2):
                        par = (j * 2 + tp) % 2
                        gbk = gbanks[par]
                        ubk = ubanks[par]

                        def _(t, j=j, tp=tp, gbk=gbk, sg=sg):
                            me.wv(t, gupld[(sg, j)][0], gupld[(sg, j)][1])
                            me.bank_wait(t, gbk)
                            for dk in range(DK):
                                ins = t.matmul(
                                    gbk[:, 0:512],
                                    me.CTX[:, j * 2048 + dk * 128:j * 2048 + dk * 128 + 128],
                                    me.LNB[:, dk * TOK + tp * 512:dk * TOK + tp * 512 + 512],
                                    start=(dk == 0), stop=(dk == DK - 1))
                            me.inc(ins, "pe")
                        self.T(_)
                        gate_pe = self.cnt["pe"]

                        def _(t, j=j, tp=tp, ubk=ubk, sg=sg):
                            me.bank_wait(t, ubk)
                            for dk in range(DK):
                                ins = t.matmul(
                                    ubk[:, 0:512],
                                    me.CTX[:, 8192 + j * 2048 + dk * 128:8192 + j * 2048 + dk * 128 + 128],
                                    me.LNB[:, dk * TOK + tp * 512:dk * TOK + tp * 512 + 512],
                                    start=(dk == 0), stop=(dk == DK - 1))
                            me.inc(ins, "pe")
                        self.T(_)
                        if tp == 1:
                            ju_done[(sg, j)] = self.cnt["pe"]
                            if sg < 15:
                                load_gu(sg + 1, j)

                        def _(sc, j=j, tp=tp, gbk=gbk, par=par, gate_pe=gate_pe):
                            me.wv(sc, "pe", gate_pe)
                            me.wv(sc, "dv", stmp_war[par])
                            if j == 0 and tp == 0:
                                me.wv(sc, "pl", prev_sp_pl)
                            ins = sc.activation(
                                me.Uf32(stmp[par], stmp[par] + 1024)[:, tp * 512:tp * 512 + 512],
                                gbk[:, 0:512], AF.Silu)
                            me.inc(ins, "ac")
                        self.S(_)
                        self.bank_set(gbk, "ac")

                        def _(v, j=j, tp=tp, ubk=ubk, par=par):
                            me.wv(v, "pe", me.cnt["pe"])
                            me.wv(v, "ac", me.cnt["ac"])
                            if j == 0 and tp == 0:
                                me.wv(v, "pl", prev_sp_pl)
                            ins = v.tensor_mul(
                                me.Uf32(2048 + j * 1024, 2048 + j * 1024 + 1024)[:, tp * 512:tp * 512 + 512],
                                me.Uf32(stmp[par], stmp[par] + 1024)[:, tp * 512:tp * 512 + 512],
                                ubk[:, 0:512])
                            me.inc(ins, "dv")
                        self.V(_)
                        self.bank_set(ubk, "dv")
                        stmp_war[par] = self.cnt["dv"]
                p_done = self.cnt["dv"]

                def pq(j, q4):
                    return me.Uf32(2048 + j * 1024 + q4 * 256, 2048 + j * 1024 + q4 * 256 + 256)

                spar = sg % 2
                for q4 in range(4):
                    sj = [me.Uf32(QS + 256 * j, QS + 256 * (j + 1)) for j in range(4)]
                    sA = me.Uf32(QS + 1024, QS + 1280)
                    sB = me.Uf32(QS + 1280, QS + 1536)
                    sC = me.Uf32(QS + 1536, QS + 1792)
                    sD = me.Uf32(QS + 1792, QS + 2048)

                    def _(g, q4=q4, sj=sj):
                        me.wv(g, "dv", p_done if q4 == 0 else me.cnt["dv"])
                        for j in range(4):
                            ins = g.tensor_mul(sj[j], pq(j, q4), pq(j, q4))
                        me.inc(ins, "pl")
                    self.G(_)

                    def _(v, sj=sj, sA=sA, sB=sB, sC=sC, sD=sD):
                        me.wv(v, "pl", me.cnt["pl"])
                        v.tensor_tensor(sA, sj[0], sj[1], op=ALU.max)
                        v.tensor_tensor(sB, sj[0], sj[1], op=ALU.min)
                        v.tensor_tensor(sC, sj[2], sj[3], op=ALU.max)
                        v.tensor_tensor(sD, sj[2], sj[3], op=ALU.min)
                        v.tensor_tensor(sA, sA, sC, op=ALU.min)
                        v.tensor_tensor(sB, sB, sD, op=ALU.max)
                        ins = v.tensor_tensor(sA, sA, sB, op=ALU.max)
                        me.inc(ins, "dv")
                    self.V(_)

                    def _(g, sj=sj, sA=sA):
                        me.wv(g, "dv", me.cnt["dv"])
                        for j in range(4):
                            ins = g.tensor_sub(sj[j], sj[j], sA)
                        me.inc(ins, "pl")
                    self.G(_)

                    def _(v, q4=q4, sj=sj, sg=sg, spar=spar):
                        me.wv(v, "pl", me.cnt["pl"])
                        if q4 == 0:
                            me.wv(v, "pe", down_done.get(sg - 2, 0))
                        for j in range(4):
                            ins = v.scalar_tensor_tensor(
                                hsp(spar, j * 1024 + q4 * 256, j * 1024 + q4 * 256 + 256),
                                sj[j], 0.0, pq(j, q4), op0=ALU.is_ge, op1=ALU.mult)
                        me.inc(ins, "dv")
                    self.V(_)
                sp_arr[sg] = self.cnt["dv"]
                prev_sp_pl = self.cnt["pl"]
                prev_sp_dv = sp_arr[sg]

                if os.environ.get("RACE_A"):
                    if sg >= 1 and sg == 1:
                        pass
                    emit_down(sg)
                    if sg < 15:
                        load_down(sg + 1)
                else:
                    if sg >= 1:
                        emit_down(sg - 1)
                        if sg < 15:
                            load_down(sg)
            if not os.environ.get("RACE_A"):
                load_down(15)
                emit_down(15)

    def ph_store(self, mt):
        me = self
        if True:
            self.open_block()

            def _(q):
                me.dma(q, me.out.rearrange("(k p) t -> p k t", p=128)[:, :, mt * TOK:mt * TOK + TOK],
                       me.wdst(me.A32[:, :], TOK), "do")
            self.Q(_)
            self.store_done = self.cnt["do"]

    def ph_final(self):
        me = self
        if True:
            self.open_block()

            def _(q):
                q.wait_ge(me.sem["do"], me.cnt["do"])
            self.Q(_)

    def build(self):
        self.marks = []

        def mark(name):
            self.marks.append((name, self.nc.n_instructions()))
        self.mark = mark
        mark("setup")
        self.ph_setup()
        for mt in (0, 1):
            u = self.upto
            self.mark(f"ln1.{mt}")
            self.ph_ln(mt, "x")
            self.tap(f"ln1h{mt}", [(self.LNB[:, :], 20480)])
            if u >= 1:
                self.mark(f"heads.{mt}")
                self.ph_heads(mt)
                self.tap(f"ctx{mt}", [(self.CTX[:, :], 16384)])
            if u >= 2:
                self.mark(f"u.{mt}")
                self.ph_u(mt)
                self.tap(f"u{mt}", [(self.UB[:, :], 16384)])
            if u >= 3:
                self.mark(f"olg.{mt}")
                self.ph_olocal_gate(mt)
                self.tap(f"ol{mt}", [(self.A32[:, :], 16384)])
                self.tap(f"byp{mt}", [(self.bypB[:, :], 1024)])
            if u >= 4:
                self.mark(f"sim.{mt}")
                self.ph_sim_topk(mt)
                self.tap(f"idx{mt}", [(self.idxA[:, :], 64), (self.w8A[:, :], 64)])
            if u >= 5:
                self.mark(f"mem.{mt}")
                self.ph_mem(mt)
                self.tap(f"yp{mt}", [(self.CTX[:, :], 16384)])
                self.tap(f"z{mt}", [(self.LNB[:, 0:16384], 16384)])
                self.tap(f"xp{mt}", [(self.A32[:, :], 16384)])
            if u >= 6:
                self.mark(f"ln2.{mt}")
                self.ph_ln(mt, "acc")
                self.tap(f"ln2_{mt}", [(self.LNB[:, 0:16384], 16384)])
            if u >= 7:
                self.mark(f"ffn.{mt}")
                self.ph_ffn(mt)
                self.tap(f"ff{mt}", [(self.A32[:, :], 16384)])
            self.mark(f"store.{mt}")
            self.ph_store(mt)
        self.mark("final")
        self.ph_final()
        self.close_block()
        self.nc.finalize()
        return self.nc


_CACHE = {}


def _build_masks():
    r = np.arange(128)[:, None]
    c = np.arange(384)[None, :]
    band = (c >= r + 1) & (c <= r + 256)
    mg = np.where(band, 0.0, -1.0e30).astype(np.float32)
    m0 = np.where(band & (c >= 256), 0.0, -1.0e30).astype(np.float32)
    m1 = np.where(band & (c >= 128), 0.0, -1.0e30).astype(np.float32)
    return mg, m0, m1


def _chunked(wT, ncol):
    Dr, X = wT.shape
    nch = X // ncol
    return np.ascontiguousarray(
        wT.reshape(16, 128, nch, ncol).transpose(2, 1, 0, 3).reshape(nch * 128, 16 * ncol))


def _wd_chunked(wd):
    return np.ascontiguousarray(
        wd.reshape(16, 4, 128, 2048).transpose(0, 2, 1, 3).reshape(2048, 8192))


def _ffn_perm():
    perm = np.empty(I2, dtype=np.int64)
    idx = 0
    for sg in range(16):
        for j in range(4):
            for gw in range(128):
                perm[idx] = 4 * (sg * 128 + gw) + j
                idx += 1
    return perm


def kernel(L0_attn_hidden_states, L0_attn_wq, L0_attn_wk, L0_attn_wv, L0_attn_wo,
           L0_attn_gate_w, L0_attn_gate_b, L0_attn_mem_bank,
           L0_ffn_w_gate, L0_ffn_w_up, L0_ffn_w_down):
    x = np.asarray(L0_attn_hidden_states, dtype=np.float32)
    f32c = lambda a: np.ascontiguousarray(np.asarray(a, dtype=np.float32))
    wq, wk, wv, wo = map(f32c, (L0_attn_wq, L0_attn_wk, L0_attn_wv, L0_attn_wo))
    gate_w = f32c(L0_attn_gate_w)
    gate_b = f32c(L0_attn_gate_b).reshape(1, 1)
    mbank = f32c(L0_attn_mem_bank)
    wg_, wu_, wd_ = map(f32c, (L0_ffn_w_gate, L0_ffn_w_up, L0_ffn_w_down))

    perm = _ffn_perm()
    shared = {
        "wqP": _chunked(wq.T, 256), "wkP": _chunked(wk.T, 256),
        "wvP": _chunked(wv.T, 256), "woP": _chunked(wo.T, 256),
        "wkU": _chunked(wk, 256), "mbP": _chunked(mbank.T, 256),
        "mb": mbank,
        "gwT": np.ascontiguousarray(gate_w.reshape(16, 128).T),
        "gb": gate_b,
        "wgP": _chunked(wg_[:, perm], 128),
        "wuP": _chunked(wu_[:, perm], 128),
        "wdP": _wd_chunked(wd_[perm, :]),
    }
    mg, m0, m1 = _build_masks()
    masks_h0 = np.ascontiguousarray(np.concatenate([mg, m0, m1], axis=1))
    masks_h1 = np.ascontiguousarray(np.concatenate([mg, mg, mg], axis=1))

    in_maps = []
    for c in range(8):
        b, half = c // 2, c % 2
        xT = x[b].T
        if half == 0:
            xsl = np.concatenate([np.zeros((D, HALO), np.float32), xT[:, 0:2048]], axis=1)
        else:
            xsl = xT[:, 2048 - HALO:4096]
        m = dict(shared)
        m["xs"] = np.ascontiguousarray(xsl)
        m["masks"] = masks_h0 if half == 0 else masks_h1
        in_maps.append(m)

    if "nc" not in _CACHE:
        kb = KB()
        _CACHE["nc"] = kb.build()
    nc = _CACHE["nc"]
    res = run_bass_kernel_spmd(nc, in_maps, core_ids=list(range(8)))

    out = np.empty((4, 4096, D), dtype=np.float32)
    for c in range(8):
        b, half = c // 2, c % 2
        out[b, half * 2048:(half + 1) * 2048, :] = res.results[c]["out"].T
    return out


# version 06
# speedup vs baseline: 1.0167x; 1.0167x over previous
import math
import os
import numpy as np
from contextlib import ExitStack

import concourse.bass as bass
import concourse.mybir as mybir
from concourse.bass_utils import run_bass_kernel_spmd

F32 = mybir.dt.float32
BF16 = mybir.dt.bfloat16
I32 = mybir.dt.int32
AF = mybir.ActivationFunctionType
ALU = mybir.AluOpType
AX = mybir.AxisListType

D = 2048
DK = 16
TOK = 1024
HALO = 256
WINT = TOK + HALO
MEM = 4096
I2 = 8192
U_MODE = 8
HEADS_MODE = 5
SQD = 1.0 / math.sqrt(D)
SQH = 1.0 / math.sqrt(128.0)
EPS = 1e-5


class KB:
    def __init__(self, debug_taps=(), upto=99):
        self.taps = tuple(debug_taps)
        self.upto = upto
        nc = bass.Bass("TRN2", target_bir_lowering=False, dynamic_dma_scratch_size=16384,
                       detect_race_conditions=False)
        self.nc = nc
        self.ctx = ExitStack()
        e = self.ctx.enter_context

        dt = nc.dram_tensor
        self.xs = dt("xs", [D, HALO + 2048], F32, kind="ExternalInput").ap()
        self.wqP = dt("wqP", [1024, 4096], F32, kind="ExternalInput").ap()
        self.wkP = dt("wkP", [1024, 4096], F32, kind="ExternalInput").ap()
        self.wvP = dt("wvP", [1024, 4096], F32, kind="ExternalInput").ap()
        self.woP = dt("woP", [1024, 4096], F32, kind="ExternalInput").ap()
        self.wkU = dt("wkU", [1024, 4096], F32, kind="ExternalInput").ap()
        self.mbP = dt("mbP", [2048, 4096], F32, kind="ExternalInput").ap()
        self.mb = dt("mb", [MEM, D], F32, kind="ExternalInput").ap()
        self.gwT = dt("gwT", [128, 16], F32, kind="ExternalInput").ap()
        self.gb = dt("gb", [1, 1], F32, kind="ExternalInput").ap()
        self.wgP = dt("wgP", [8192, 2048], F32, kind="ExternalInput").ap()
        self.wuP = dt("wuP", [8192, 2048], F32, kind="ExternalInput").ap()
        self.wdP = dt("wdP", [2048, 8192], F32, kind="ExternalInput").ap()
        self.masks_d = dt("masks", [128, 3 * 384], F32, kind="ExternalInput").ap()
        self.out = dt("out", [D, 2048], F32, kind="ExternalOutput").ap()
        self.dbg = {}
        for t in self.taps:
            self.dbg[t] = dt("dbg_" + t, [128, 20480], F32, kind="ExternalOutput").ap()

        self.A32 = e(nc.sbuf_tensor("A32", [128, 16384], F32))
        self.LNB = e(nc.sbuf_tensor("LNB", [128, 20480], BF16))
        self.CTX = e(nc.sbuf_tensor("CTXB", [128, 16384], BF16))
        self.UB = e(nc.sbuf_tensor("UBB", [128, 16384], BF16))
        self.SS = e(nc.sbuf_tensor("SSB", [128, 8192], BF16))

        self.masks = e(nc.sbuf_tensor("masks_s", [128, 3 * 384], F32))
        self.colidx = e(nc.sbuf_tensor("colidx", [128, 256], I32))
        self.identb = e(nc.sbuf_tensor("identb", [128, 128], BF16))
        self.identf = e(nc.sbuf_tensor("identf", [128, 128], F32))
        self.onescol = e(nc.sbuf_tensor("onescol", [128, 1], BF16))
        self.onesrow = e(nc.sbuf_tensor("onesrow", [1, 128], BF16))
        self.onesrowf = e(nc.sbuf_tensor("onesrowf", [1, 128], F32))
        ubf = self.UB[:, :].bitcast(F32)
        self.ms = [ubf[:, 2048 + 384 * i:2048 + 384 * (i + 1)] for i in range(2)]
        self.probs = [ubf[:, 2816 + 384 * i:2816 + 384 * (i + 1)] for i in range(2)]
        self.attn = [self.UB[:, 7168 + 384 * i:7168 + 384 * (i + 1)] for i in range(4)]
        self.attnT = [self.UB[:, 8704 + 384 * i:8704 + 384 * (i + 1)] for i in range(2)]
        self.nmax = [e(nc.sbuf_tensor(f"nmax{i}", [128, 1], F32)) for i in range(2)]
        self.esum = [e(nc.sbuf_tensor(f"esum{i}", [128, 1], F32)) for i in range(2)]
        self.ersum = [e(nc.sbuf_tensor(f"ersum{i}", [128, 1], F32)) for i in range(2)]
        self.packed = e(nc.sbuf_tensor("packed", [128, 256], F32))
        self.candA = e(nc.sbuf_tensor("candA", [128, 8 * 128], F32))
        self.run8 = e(nc.sbuf_tensor("run8", [128, 8], F32))
        self.vals8 = e(nc.sbuf_tensor("vals8", [128, 8], F32))
        self.ev8 = e(nc.sbuf_tensor("ev8", [128, 8], F32))
        self.idxA = e(nc.sbuf_tensor("idxA", [128, 64], I32))
        self.w8A = e(nc.sbuf_tensor("w8A", [128, 64], F32))
        self.xbs = [e(nc.sbuf_tensor(f"xbs{i}", [128, 512], BF16)) for i in range(2)]
        self.sqs = [e(nc.sbuf_tensor(f"sqs{i}", [128, 512], BF16)) for i in range(2)]
        self.nmr = ubf[0:1, 0:512]
        self.vrr = ubf[0:1, 512:1024]
        self.m2r = ubf[0:1, 1024:1536]
        self.rsr = ubf[0:1, 1536:2048]
        self.nmb = self.UB[0:1, 4096:4608]
        self.rsb = self.UB[0:1, 4608:5120]
        self.tmpf = e(nc.sbuf_tensor("tmpf", [128, 512], F32))
        ssf = self.SS[:, :].bitcast(F32)
        self.grow = ssf[0:1, 0:1024]
        self.gtmp = ssf[0:1, 1024:2048]
        self.neggb = e(nc.sbuf_tensor("neggb", [1, 1], F32))
        self.bypB = e(nc.sbuf_tensor("bypB", [128, 1024], F32))
        self.gwTs = e(nc.sbuf_tensor("gwTs", [128, 16], F32))
        self.xrest = self.candA[:, :]
        self.gbs = e(nc.sbuf_tensor("gbs", [1, 1], F32))

        self.pA = e(nc.psum_tensor("pA", [128, 512], F32))
        self.pB = e(nc.psum_tensor("pB", [128, 512], F32))
        self.pC = e(nc.psum_tensor("pC", [128, 512], F32))
        self.pD = e(nc.psum_tensor("pD", [128, 512], F32))
        self.pE = e(nc.psum_tensor("pE", [128, 512], F32))
        self.pS = e(nc.psum_tensor("pS", [1, 512], F32))
        self.pBc1 = e(nc.psum_tensor("pBc1", [128, 512], F32))
        self.pBc2 = e(nc.psum_tensor("pBc2", [128, 512], F32))

        self.sem = {}
        self.cnt = {}
        for s in ("dw", "dx", "dg", "do", "pe", "dv", "ac", "pl",
                  "dwA", "dwB", "dwC", "dwD", "dwE", "dgA", "dgB", "dxA", "dxB"):
            self.sem[s] = e(nc.semaphore(s))
            self.cnt[s] = 0
        self.bankc = {}
        self.store_done = 0
        assert nc.sbuf_bytes_remaining >= 0, nc.sbuf_bytes_remaining

    def open_block(self):
        if getattr(self, "blk", None) is not None:
            self.close_block()
        self.blk = self.nc.Block().__enter__()
        self.blk.vector(lambda e: e.nop())
        self.blk.scalar(lambda e: e.nop())
        self.blk.tensor(lambda e: e.nop())
        self.blk.gpsimd(lambda e: e.nop())
        self.blk.sync(lambda e: e.nop())

    def close_block(self):
        if self.blk is not None:
            self.blk.__exit__(None, None, None)
            self.blk = None

    def V(self, f):
        self.blk.vector(f)

    def S(self, f):
        self.blk.scalar(f)

    def T(self, f):
        self.blk.tensor(f)

    def G(self, f):
        self.blk.gpsimd(f)

    def Q(self, f):
        self.blk.sync(f)

    def inc(self, ins, s, n=1):
        ins.then_inc(self.sem[s], n)
        self.cnt[s] += n
        return self.cnt[s]

    def wv(self, eng, s, tgt):
        if tgt > 0:
            eng.wait_ge(self.sem[s], tgt)

    def bank_wait(self, eng, bank):
        c = self.bankc.get(bank.name)
        if c:
            self.wv(eng, c[0], c[1])

    def bank_set(self, bank, s):
        self.bankc[bank.name] = (s, self.cnt[s])

    def Abf(self, c0, c1):
        return self.A32[:, :].bitcast(BF16)[:, c0:c1]

    def Uf32(self, c0, c1):
        return self.UB[:, :].bitcast(F32)[:, c0:c1]

    def wsrc(self, wdram, c0, cols, k0=0, nk=DK):
        return wdram.rearrange("(k p) o -> p k o", p=128)[:, k0:k0 + nk, c0:c0 + cols]

    def wdst(self, tile_ap, cols):
        return tile_ap.rearrange("p (k f) -> p k f", f=cols)

    def dma(self, q, dst, src, s):
        return self.inc(q.dma_start(out=dst, in_=src), s, 16)

    def tap(self, name, srcs):
        if name not in self.taps:
            return
        me = self
        dst = self.dbg[name]
        if True:
            self.open_block()

            def _(g):
                col = 0
                for ap, ncols in srcs:
                    me.dma(g, dst[:, col:col + ncols], ap, "dg")
                    col += ncols
                g.wait_ge(me.sem["dg"], me.cnt["dg"])
            self.G(_)

    def ph_setup(self):
        me = self
        if True:
            self.open_block()

            def _(q):
                me.dma(q, me.masks[:, :], me.masks_d[:, :], "dxA")
                me.dma(q, me.gwTs[:, :], me.gwT[:, :], "dxA")
                me.dma(q, me.gbs[:, :], me.gb[:, :], "dxA")
            self.Q(_)

            def _(g):
                g.iota(me.colidx[:, :], [[1, 256]], base=0, channel_multiplier=0)
                g.memset(me.identb[:, :], 1.0)
                g.affine_select(me.identb[:, :], me.identb[:, :], [[1, 128]],
                                ALU.is_equal, 0.0, base=0, channel_multiplier=-1)
                g.memset(me.identf[:, :], 1.0)
                g.affine_select(me.identf[:, :], me.identf[:, :], [[1, 128]],
                                ALU.is_equal, 0.0, base=0, channel_multiplier=-1)
                g.memset(me.onescol[:, :], 1.0)
                g.memset(me.onesrow[:, :], 1.0)
                ins = g.memset(me.onesrowf[:, :], 1.0)
                me.inc(ins, "pl")
            self.G(_)

            def _(v):
                me.wv(v, "dxA", me.cnt["dxA"])
                me.wv(v, "pl", me.cnt["pl"])
                ins = v.tensor_scalar(me.neggb[:, :], me.gbs[:, :], -1.0, None, op0=ALU.mult)
                me.inc(ins, "dv")
            self.V(_)

    def ph_ln(self, mt, src):
        me = self
        base = mt * TOK
        slabs = [(0, 512), (512, 512), (1024, 256)] if src == "x" else [(0, 512), (512, 512)]
        ow = WINT if src == "x" else TOK
        if True:
            self.open_block()
            apply_done = {}
            for si, (s0, sw) in enumerate(slabs):
                pi = si % 2
                lsem = "dxA" if pi == 0 else "dxB"
                if src == "x":
                    def _(q, si=si, s0=s0, sw=sw, pi=pi, lsem=lsem):
                        if si >= 2:
                            me.wv(q, "dv", apply_done[si - 2])
                        if mt == 1:
                            me.wv(q, "do", me.store_done)
                        dst = me.A32[:, pi * 8192:pi * 8192 + 8192]
                        me.dma(q, me.wdst(dst, 512)[:, :, 0:sw],
                               me.xs.rearrange("(k p) t -> p k t", p=128)
                               [:, :, base + s0:base + s0 + sw], lsem)
                    self.Q(_)
                ld_tgt = self.cnt[lsem]

                def chunk(dk):
                    if src == "x":
                        return me.A32[:, pi * 8192 + dk * 512:pi * 8192 + dk * 512 + sw]
                    return me.A32[:, dk * TOK + s0:dk * TOK + s0 + sw]

                pe_at_slab_start = self.cnt["pe"]
                stats_tgt = {}
                for dk in range(DK):
                    par = dk % 2

                    def _(sc, dk=dk, sw=sw, par=par, lsem=lsem):
                        if dk == 0 and src == "x":
                            me.wv(sc, lsem, ld_tgt)
                        me.wv(sc, "pe", stats_tgt[dk - 2] if dk > 1 else pe_at_slab_start)
                        ch = chunk(dk)
                        sc.copy(me.xbs[par][:, 0:sw], ch)
                        ins = sc.square(me.sqs[par][:, 0:sw], ch)
                        me.inc(ins, "ac")
                    self.S(_)

                    def _(t, dk=dk, sw=sw, par=par):
                        me.wv(t, "ac", me.cnt["ac"])
                        if dk == 0:
                            me.bank_wait(t, me.pS)
                            me.bank_wait(t, me.pBc1)
                        t.matmul(me.pS[0:1, 0:sw], me.onescol[:, :], me.xbs[par][:, 0:sw],
                                 start=(dk == 0), stop=(dk == DK - 1))
                        ins = t.matmul(me.pBc1[0:1, 0:sw], me.onescol[:, :], me.sqs[par][:, 0:sw],
                                       start=(dk == 0), stop=(dk == DK - 1))
                        me.inc(ins, "pe")
                        stats_tgt[dk] = me.cnt["pe"]
                    self.T(_)

                def _(v, sw=sw):
                    me.wv(v, "pe", me.cnt["pe"])
                    v.tensor_scalar(me.nmr[0:1, 0:sw], me.pS[0:1, 0:sw], -1.0 / D, None, op0=ALU.mult)
                    v.tensor_scalar(me.vrr[0:1, 0:sw], me.pBc1[0:1, 0:sw], 1.0 / D, None, op0=ALU.mult)
                    v.tensor_mul(me.m2r[0:1, 0:sw], me.nmr[0:1, 0:sw], me.nmr[0:1, 0:sw])
                    v.tensor_sub(me.vrr[0:1, 0:sw], me.vrr[0:1, 0:sw], me.m2r[0:1, 0:sw])
                    ins = v.tensor_scalar(me.vrr[0:1, 0:sw], me.vrr[0:1, 0:sw], EPS, None,
                                          op0=ALU.add)
                    me.inc(ins, "dv")
                self.V(_)
                self.bank_set(self.pS, "dv")

                def _(sc, sw=sw):
                    me.wv(sc, "dv", me.cnt["dv"])
                    ins = sc.activation(me.rsr[0:1, 0:sw], me.vrr[0:1, 0:sw], AF.Sqrt)
                    me.inc(ins, "ac")
                self.S(_)

                def _(v, sw=sw):
                    me.wv(v, "ac", me.cnt["ac"])
                    ins = v.reciprocal(me.vrr[0:1, 0:sw], me.rsr[0:1, 0:sw])
                    me.inc(ins, "dv")
                    v.wait_ge(me.sem["dv"], me.cnt["dv"])
                    v.tensor_copy(me.nmb[0:1, 0:sw], me.nmr[0:1, 0:sw])
                    ins = v.tensor_copy(me.rsb[0:1, 0:sw], me.vrr[0:1, 0:sw])
                    me.inc(ins, "dv")
                self.V(_)
                self.bank_set(self.pBc1, "dv")

                def _(t, sw=sw):
                    me.wv(t, "dv", me.cnt["dv"])
                    me.bank_wait(t, me.pBc2)
                    me.bank_wait(t, me.pD)
                    t.matmul(me.pBc2[:, 0:sw], me.onesrow[:, :], me.nmb[0:1, 0:sw],
                             start=True, stop=True)
                    ins = t.matmul(me.pD[:, 0:sw], me.onesrow[:, :], me.rsb[0:1, 0:sw],
                                   start=True, stop=True)
                    me.inc(ins, "pe")
                self.T(_)
                bc_tgt = self.cnt["pe"]

                for dk in range(DK):
                    def _(v, dk=dk, s0=s0, sw=sw):
                        if dk == 0:
                            me.wv(v, "pe", bc_tgt)
                        ch = chunk(dk)
                        v.tensor_add(me.tmpf[:, 0:sw], ch, me.pBc2[:, 0:sw])
                        ins = v.tensor_mul(me.LNB[:, dk * ow + s0:dk * ow + s0 + sw],
                                           me.tmpf[:, 0:sw], me.pD[:, 0:sw])
                        me.inc(ins, "dv")
                    self.V(_)
                apply_done[si] = self.cnt["dv"]
                self.bank_set(self.pBc2, "dv")
                self.bank_set(self.pD, "dv")

    def ph_heads(self, mt):
        me = self
        if True:
            self.open_block()
            wld = {}
            war = {0: 0, 1: 0}
            proj_banks = [self.pA, self.pB, self.pE]
            tr_banks = [self.pC, self.pD]
            ctx_banks = [self.pBc1, self.pBc2]
            rot = {"p": 0}
            ctx_done = 0
            tr_done = {}
            exp_done = {}
            attn_done = {}
            ctxevac = {}

            for hp in range(8):
                pi = hp % 2
                woff = pi * 12288

                wsem = "dwA" if pi == 0 else "dwB"

                def _(g, hp=hp, pi=pi, woff=woff, wsem=wsem):
                    me.wv(g, "pe", war[pi])
                    if hp == 0:
                        me.wv(g, "do", me.store_done)
                    for w, wo in ((me.wqP, 0), (me.wkP, 4096), (me.wvP, 8192)):
                        me.dma(g, me.Abf(woff + wo, woff + wo + 4096),
                               w[hp * 128:(hp + 1) * 128, :], wsem)
                self.G(_)
                wld[hp] = (wsem, self.cnt[wsem])

                for h2 in range(2):
                    h = hp * 2 + h2
                    if HEADS_MODE < 2:
                        continue
                    for (name, wo, dst_off, spans) in (
                        ("q", 0, 24576, [(HALO, 512), (HALO + 512, 512)]),
                        ("k", 4096, 25600, [(0, 512), (512, 512), (1024, 256)]),
                        ("v", 8192, 26880, [(0, 512), (512, 512), (1024, 256)]),
                    ):
                        for (c0, cw) in spans:
                            bk = proj_banks[rot["p"] % 3]
                            rot["p"] += 1

                            def _(t, bk=bk, wo=wo, c0=c0, cw=cw, woff=woff, h2=h2, hp=hp):
                                me.wv(t, wld[hp][0], wld[hp][1])
                                me.bank_wait(t, bk)
                                for dk in range(DK):
                                    ins = t.matmul(
                                        bk[:, 0:cw],
                                        me.Abf(woff + wo + dk * 256 + h2 * 128,
                                               woff + wo + dk * 256 + h2 * 128 + 128),
                                        me.LNB[:, dk * WINT + c0:dk * WINT + c0 + cw],
                                        start=(dk == 0), stop=(dk == DK - 1))
                                me.inc(ins, "pe")
                            self.T(_)

                            def _(v, bk=bk, dst_off=dst_off, c0=c0, cw=cw, name=name,
                                  ctx_done=ctx_done):
                                me.wv(v, "pe", me.cnt["pe"])
                                me.wv(v, "pe", ctx_done)
                                b0 = c0 if name != "q" else c0 - HALO
                                ins = v.tensor_copy(me.Abf(dst_off + b0, dst_off + b0 + cw),
                                                    bk[:, 0:cw])
                                me.inc(ins, "dv")
                            self.V(_)
                            self.bank_set(bk, "dv")
                    vproj_done = self.cnt["dv"]
                    if h2 == 1:
                        war[pi] = self.cnt["pe"]

                    if HEADS_MODE < 3:
                        continue
                    for bi_, bstart in enumerate((0, 4, 8)):
                        nb = min(4, 10 - bstart)
                        bk = tr_banks[bi_ % 2]

                        def _(t, bk=bk, bstart=bstart, nb=nb, vproj_done=vproj_done):
                            me.wv(t, "dv", vproj_done)
                            me.bank_wait(t, bk)
                            bb = bk[:, :].bitcast(BF16)
                            for i in range(nb):
                                ins = t.transpose(bb[:, i * 128:(i + 1) * 128],
                                                  me.Abf(26880 + (bstart + i) * 128,
                                                         26880 + (bstart + i) * 128 + 128),
                                                  me.identb[:, :])
                            me.inc(ins, "pe")
                        self.T(_)

                        def _(v, bk=bk, bstart=bstart, nb=nb):
                            me.wv(v, "pe", me.cnt["pe"])
                            bb = bk[:, :].bitcast(BF16)
                            ins = v.tensor_copy(me.Abf(28160 + bstart * 128, 28160 + (bstart + nb) * 128),
                                                bb[:, 0:nb * 128])
                            me.inc(ins, "dv")
                        self.V(_)
                        self.bank_set(bk, "dv")

                    if HEADS_MODE < 4:
                        continue
                    attn_ready = {}
                    attn_war = {0: 0, 1: 0}

                    def stage_front(qp):
                        qt0, qt1 = 2 * qp, 2 * qp + 1
                        pp = qp % 2
                        a0, a1 = 2 * pp, 2 * pp + 1
                        sbk0 = proj_banks[rot["p"] % 3]
                        rot["p"] += 1
                        sbk1 = proj_banks[rot["p"] % 3]
                        rot["p"] += 1

                        def mi_of(qt):
                            if mt == 0 and qt == 0:
                                return 1
                            if mt == 0 and qt == 1:
                                return 2
                            return 0

                        def _(t, sbk0=sbk0, sbk1=sbk1, qt0=qt0, qt1=qt1):
                            me.bank_wait(t, sbk0)
                            ins = t.matmul(sbk0[:, 0:384],
                                           me.Abf(24576 + qt0 * 128, 24576 + qt0 * 128 + 128),
                                           me.Abf(25600 + qt0 * 128, 25600 + qt0 * 128 + 384),
                                           start=True, stop=True)
                            me.inc(ins, "pe")
                            me.bank_wait(t, sbk1)
                            ins = t.matmul(sbk1[:, 0:384],
                                           me.Abf(24576 + qt1 * 128, 24576 + qt1 * 128 + 128),
                                           me.Abf(25600 + qt1 * 128, 25600 + qt1 * 128 + 384),
                                           start=True, stop=True)
                            me.inc(ins, "pe")
                        self.T(_)

                        def _(v, sbk0=sbk0, sbk1=sbk1, qt0=qt0, qt1=qt1):
                            me.wv(v, "pe", me.cnt["pe"])
                            me.wv(v, "ac", exp_done.get(0, 0))
                            v.scalar_tensor_tensor(me.ms[0][:, :], sbk0[:, 0:384], SQH,
                                                   me.masks[:, mi_of(qt0) * 384:mi_of(qt0) * 384 + 384],
                                                   op0=ALU.mult, op1=ALU.add)
                            ins = v.scalar_tensor_tensor(me.ms[1][:, :], sbk1[:, 0:384], SQH,
                                                         me.masks[:, mi_of(qt1) * 384:mi_of(qt1) * 384 + 384],
                                                         op0=ALU.mult, op1=ALU.add)
                            me.inc(ins, "dv")
                            ins = v.reduce_max(me.nmax[0][:, :], me.ms[0][:, :],
                                               axis=AX.X, negate=True)
                            me.inc(ins, "dv")
                            ins = v.reduce_max(me.nmax[1][:, :], me.ms[1][:, :],
                                               axis=AX.X, negate=True)
                            me.inc(ins, "dv")
                        self.V(_)
                        self.bank_set(sbk0, "dv")
                        self.bank_set(sbk1, "dv")

                        def _(sc):
                            me.wv(sc, "dv", me.cnt["dv"])
                            me.wv(sc, "dv", attn_done.get(0, 0))
                            sc.activation(me.probs[0][:, :], me.ms[0][:, :], AF.Exp,
                                          bias=me.nmax[0][:, 0:1], scale=1.0,
                                          accum_out=me.esum[0][:, 0:1])
                            ins = sc.activation(me.probs[1][:, :], me.ms[1][:, :], AF.Exp,
                                                bias=me.nmax[1][:, 0:1], scale=1.0,
                                                accum_out=me.esum[1][:, 0:1])
                            me.inc(ins, "ac")
                        self.S(_)
                        exp_done[0] = self.cnt["ac"]

                        def _(v, a0=a0, a1=a1, pp=pp):
                            me.wv(v, "ac", me.cnt["ac"])
                            me.wv(v, "pe", attn_war[pp])
                            ins = v.reciprocal(me.ersum[0][:, :], me.esum[0][:, :])
                            me.inc(ins, "dv")
                            ins = v.reciprocal(me.ersum[1][:, :], me.esum[1][:, :])
                            me.inc(ins, "dv")
                            v.wait_ge(me.sem["dv"], me.cnt["dv"])
                            v.tensor_scalar(me.attn[a0][:, :], me.probs[0][:, :],
                                            me.ersum[0][:, 0:1], None, op0=ALU.mult)
                            ins = v.tensor_scalar(me.attn[a1][:, :], me.probs[1][:, :],
                                                  me.ersum[1][:, 0:1], None, op0=ALU.mult)
                            me.inc(ins, "dv")
                        self.V(_)
                        attn_done[0] = self.cnt["dv"]
                        attn_ready[qp] = self.cnt["dv"]

                    def stage_back(qp):
                        qt0, qt1 = 2 * qp, 2 * qp + 1
                        pp = qp % 2
                        a0, a1 = 2 * pp, 2 * pp + 1

                        def _(t, a0=a0, a1=a1, qp=qp):
                            me.wv(t, "dv", attn_ready[qp])
                            me.bank_wait(t, me.pC)
                            me.bank_wait(t, me.pD)
                            b0 = me.pC[:, :].bitcast(BF16)
                            b1 = me.pD[:, :].bitcast(BF16)
                            for kk in range(3):
                                t.transpose(b0[:, kk * 128:kk * 128 + 128],
                                            me.attn[a0][:, kk * 128:kk * 128 + 128],
                                            me.identb[:, :])
                            for kk in range(3):
                                ins = t.transpose(b1[:, kk * 128:kk * 128 + 128],
                                                  me.attn[a1][:, kk * 128:kk * 128 + 128],
                                                  me.identb[:, :])
                            me.inc(ins, "pe")
                        self.T(_)
                        attn_war[pp] = self.cnt["pe"]

                        def _(v):
                            me.wv(v, "pe", me.cnt["pe"])
                            me.wv(v, "pe", ctxevac.get(0, 0))
                            v.tensor_copy(me.attnT[0][:, :], me.pC[:, :].bitcast(BF16)[:, 0:384])
                            ins = v.tensor_copy(me.attnT[1][:, :], me.pD[:, :].bitcast(BF16)[:, 0:384])
                            me.inc(ins, "dv")
                        self.V(_)
                        self.bank_set(me.pC, "dv")
                        self.bank_set(me.pD, "dv")

                        def _(t, qt0=qt0, qt1=qt1):
                            me.wv(t, "dv", me.cnt["dv"])
                            me.bank_wait(t, me.pBc1)
                            me.bank_wait(t, me.pBc2)
                            for kk in range(3):
                                t.matmul(me.pBc1[:, 0:128],
                                         me.Abf(28160 + (qt0 + kk) * 128, 28160 + (qt0 + kk) * 128 + 128),
                                         me.attnT[0][:, kk * 128:kk * 128 + 128],
                                         start=(kk == 0), stop=(kk == 2))
                            for kk in range(3):
                                ins = t.matmul(me.pBc2[:, 0:128],
                                               me.Abf(28160 + (qt1 + kk) * 128, 28160 + (qt1 + kk) * 128 + 128),
                                               me.attnT[1][:, kk * 128:kk * 128 + 128],
                                               start=(kk == 0), stop=(kk == 2))
                            me.inc(ins, "pe")
                        self.T(_)
                        ctxevac[0] = self.cnt["pe"]

                        def _(sc, h=h, qt0=qt0, qt1=qt1):
                            me.wv(sc, "pe", me.cnt["pe"])
                            sc.copy(me.CTX[:, h * TOK + qt0 * 128:h * TOK + qt0 * 128 + 128],
                                    me.pBc1[:, 0:128])
                            ins = sc.copy(me.CTX[:, h * TOK + qt1 * 128:h * TOK + qt1 * 128 + 128],
                                          me.pBc2[:, 0:128])
                            me.inc(ins, "ac")
                        self.S(_)
                        self.bank_set(me.pBc1, "ac")
                        self.bank_set(me.pBc2, "ac")

                    for qp in range(5):
                        if qp < 4:
                            stage_front(qp)
                        if qp >= 1:
                            stage_back(qp - 1)
                    ctx_done = self.cnt["pe"]

    def ph_u(self, mt):
        me = self
        if True:
            self.open_block()
            banks = [self.pA, self.pB]
            war = {0: 0, 1: 0}
            for ec in range(8):
                pi = ec % 2

                wsem = "dwA" if pi == 0 else "dwB"

                def _(g, ec=ec, pi=pi, wsem=wsem):
                    me.wv(g, "pe", war[pi])
                    me.dma(g, me.Abf(pi * 8192, pi * 8192 + 4096),
                           me.wkU[ec * 128:(ec + 1) * 128, :], wsem)
                self.G(_)
                ld = (wsem, self.cnt[wsem])
                for es in range(2):
                    for tp in range(2):
                        bk = banks[(es * 2 + tp) % 2]

                        def _(t, bk=bk, es=es, tp=tp, pi=pi, ld=ld):
                            me.wv(t, ld[0], ld[1])
                            me.bank_wait(t, bk)
                            for dk in range(DK):
                                ins = t.matmul(
                                    bk[:, 0:512],
                                    me.Abf(pi * 8192 + dk * 256 + es * 128,
                                           pi * 8192 + dk * 256 + es * 128 + 128),
                                    me.LNB[:, dk * WINT + HALO + tp * 512:
                                           dk * WINT + HALO + tp * 512 + 512],
                                    start=(dk == 0), stop=(dk == DK - 1))
                            me.inc(ins, "pe")
                        self.T(_)

                        def _(v, bk=bk, ec=ec, es=es, tp=tp):
                            me.wv(v, "pe", me.cnt["pe"])
                            ek = ec * 2 + es
                            ins = v.tensor_copy(me.UB[:, ek * TOK + tp * 512:ek * TOK + tp * 512 + 512],
                                                bk[:, 0:512])
                            me.inc(ins, "dv")
                        self.V(_)
                        self.bank_set(bk, "dv")
                war[pi] = self.cnt["pe"]

    def ph_olocal_gate(self, mt):
        me = self
        if True:
            self.open_block()
            banks = [self.pA, self.pB]
            war = {0: 0, 1: 0}
            gate_banks = {0: self.pS, 1: self.pBc1}
            for oc4 in range(8):
                pi = oc4 % 2

                wsem = "dwA" if pi == 0 else "dwB"

                def _(g, oc4=oc4, pi=pi, wsem=wsem):
                    me.wv(g, "pe", war[pi])
                    me.dma(g, me.LNB[:, pi * 4096:pi * 4096 + 4096],
                           me.woP[oc4 * 128:(oc4 + 1) * 128, :], wsem)
                self.G(_)
                ld = (wsem, self.cnt[wsem])
                for os_ in range(2):
                    oc = oc4 * 2 + os_
                    for tp in range(2):
                        bk = banks[(os_ * 2 + tp) % 2]

                        def _(t, bk=bk, os_=os_, tp=tp, pi=pi, ld=ld):
                            me.wv(t, ld[0], ld[1])
                            me.bank_wait(t, bk)
                            for dk in range(DK):
                                ins = t.matmul(
                                    bk[:, 0:512],
                                    me.LNB[:, pi * 4096 + dk * 256 + os_ * 128:
                                           pi * 4096 + dk * 256 + os_ * 128 + 128],
                                    me.CTX[:, dk * TOK + tp * 512:dk * TOK + tp * 512 + 512],
                                    start=(dk == 0), stop=(dk == DK - 1))
                            me.inc(ins, "pe")
                        self.T(_)

                        def _(v, bk=bk, oc=oc, tp=tp):
                            me.wv(v, "pe", me.cnt["pe"])
                            ins = v.tensor_copy(me.A32[:, oc * TOK + tp * 512:oc * TOK + tp * 512 + 512],
                                                bk[:, 0:512])
                            me.inc(ins, "dv")
                        self.V(_)
                        self.bank_set(bk, "dv")
                war[pi] = self.cnt["pe"]
            olocal_done = self.cnt["dv"]

            for tp in range(2):
                for oc in range(DK):
                    def _(t, oc=oc, tp=tp):
                        if oc == 0:
                            me.wv(t, "dv", olocal_done)
                            me.bank_wait(t, gate_banks[tp])
                        ins = t.matmul(gate_banks[tp][0:1, 0:512], me.gwTs[:, oc:oc + 1],
                                       me.A32[:, oc * TOK + tp * 512:oc * TOK + tp * 512 + 512],
                                       start=(oc == 0), stop=(oc == DK - 1))
                        me.inc(ins, "pe")
                    self.T(_)

            def _(sc):
                me.wv(sc, "pe", me.cnt["pe"])
                sc.activation(me.gtmp[0:1, 0:512], me.pS[0:1, 0:512], AF.Exp,
                              bias=me.neggb[0:1, 0:1], scale=-1.0)
                ins = sc.activation(me.gtmp[0:1, 512:1024], me.pBc1[0:1, 0:512], AF.Exp,
                                    bias=me.neggb[0:1, 0:1], scale=-1.0)
                me.inc(ins, "ac")
            self.S(_)
            self.bank_set(self.pS, "ac")
            self.bank_set(self.pBc1, "ac")

            def _(v):
                me.wv(v, "ac", me.cnt["ac"])
                v.tensor_scalar(me.grow[0:1, :], me.gtmp[0:1, :], 1.0, None, op0=ALU.add)
                ins = v.reciprocal(me.gtmp[0:1, :], me.grow[0:1, :])
                me.inc(ins, "dv")
                v.wait_ge(me.sem["dv"], me.cnt["dv"])
                ins = v.engine_nop()
                me.inc(ins, "dv")
            self.V(_)

            for tp in range(2):
                def _(t, tp=tp):
                    me.wv(t, "dv", me.cnt["dv"])
                    me.bank_wait(t, me.pBc2)
                    ins = t.matmul(me.pBc2[:, 0:512], me.onesrowf[:, :],
                                   me.gtmp[0:1, tp * 512:tp * 512 + 512],
                                   start=True, stop=True)
                    me.inc(ins, "pe")
                self.T(_)

                def _(v, tp=tp):
                    me.wv(v, "pe", me.cnt["pe"])
                    ins = v.tensor_copy(me.bypB[:, tp * 512:tp * 512 + 512], me.pBc2[:, 0:512])
                    me.inc(ins, "dv")
                self.V(_)
                self.bank_set(self.pBc2, "dv")

    def ph_sim_topk(self, mt):
        me = self
        if True:
            self.open_block()
            banks = [self.pA, self.pB]
            war = {0: 0, 1: 0}
            for mc in range(16):
                pi = mc % 2

                wsem = "dwA" if pi == 0 else "dwB"

                def _(g, mc=mc, pi=pi, wsem=wsem):
                    me.wv(g, "pe", war[pi])
                    me.dma(g, me.SS[:, pi * 4096:pi * 4096 + 4096],
                           me.mbP[mc * 128:(mc + 1) * 128, :], wsem)
                self.G(_)
                ld = (wsem, self.cnt[wsem])
                for tt in range(8):
                    bk = banks[tt % 2]

                    def _(t, bk=bk, tt=tt, pi=pi, ld=ld):
                        me.wv(t, ld[0], ld[1])
                        me.bank_wait(t, bk)
                        for ek in range(DK):
                            ins = t.matmul(
                                bk[:, 0:256],
                                me.UB[:, ek * TOK + tt * 128:ek * TOK + tt * 128 + 128],
                                me.SS[:, pi * 4096 + ek * 256:pi * 4096 + ek * 256 + 256],
                                start=(ek == 0), stop=(ek == DK - 1))
                        me.inc(ins, "pe")
                    self.T(_)

                    def _(v, bk=bk, mc=mc, tt=tt):
                        me.wv(v, "pe", me.cnt["pe"])
                        bi = bk[:, :].bitcast(I32)
                        ins = v.tensor_scalar(me.packed[:, :].bitcast(I32), bi[:, 0:256],
                                              -4096, mc << 8, op0=ALU.bitwise_and,
                                              op1=ALU.bitwise_or)
                        me.inc(ins, "dv")
                        v.tensor_tensor(me.packed[:, :].bitcast(I32),
                                        me.packed[:, :].bitcast(I32),
                                        me.colidx[:, :], op=ALU.bitwise_or)
                        ins = v.max(out=me.candA[:, tt * 128 + mc * 8:tt * 128 + mc * 8 + 8],
                                    in_=me.packed[:, :])
                        me.inc(ins, "dv")
                    self.V(_)
                    self.bank_set(bk, "dv")
                war[pi] = self.cnt["pe"]

            for tt in range(8):
                def _(v, tt=tt):
                    me.wv(v, "dv", me.cnt["dv"])
                    ins = v.max(out=me.run8[:, :], in_=me.candA[:, tt * 128:tt * 128 + 128])
                    me.inc(ins, "dv")
                    v.wait_ge(me.sem["dv"], me.cnt["dv"])
                    v.tensor_scalar(me.idxA[:, tt * 8:tt * 8 + 8], me.run8[:, :].bitcast(I32),
                                    4095, None, op0=ALU.bitwise_and)
                    ins = v.tensor_scalar(me.vals8[:, :].bitcast(I32), me.run8[:, :].bitcast(I32),
                                          -4096, None, op0=ALU.bitwise_and)
                    me.inc(ins, "dv")
                    v.wait_ge(me.sem["dv"], me.cnt["dv"])
                    ins = v.tensor_scalar(me.vals8[:, :], me.vals8[:, :], SQD, None, op0=ALU.mult)
                    me.inc(ins, "dv")
                    v.wait_ge(me.sem["dv"], me.cnt["dv"])
                    ins = v.reduce_max(me.nmax[0][:, :], me.vals8[:, :], axis=AX.X, negate=True)
                    me.inc(ins, "dv")
                self.V(_)

                def _(sc):
                    me.wv(sc, "dv", me.cnt["dv"])
                    ins = sc.activation(me.ev8[:, :], me.vals8[:, :], AF.Exp,
                                        bias=me.nmax[0][:, 0:1], scale=1.0,
                                        accum_out=me.esum[0][:, 0:1])
                    me.inc(ins, "ac")
                self.S(_)

                def _(v, tt=tt):
                    me.wv(v, "ac", me.cnt["ac"])
                    ins = v.reciprocal(me.ersum[0][:, :], me.esum[0][:, :])
                    me.inc(ins, "dv")
                    v.wait_ge(me.sem["dv"], me.cnt["dv"])
                    ins = v.tensor_scalar(me.w8A[:, tt * 8:tt * 8 + 8], me.ev8[:, :],
                                          me.ersum[0][:, 0:1], None, op0=ALU.mult)
                    me.inc(ins, "dv")
                self.V(_)

    def ph_mem(self, mt):
        me = self
        base = mt * TOK
        if True:
            self.open_block()
            tr_banks = [self.pC, self.pD]
            banks = [self.pA, self.pB]
            mac_done = {}
            ytm_war = {0: 0, 1: 0}
            for half in range(2):
                for j in range(4):
                    tt = half * 4 + j
                    yo = 2048 + (j % 2) * 2048
                    for k in range(8):
                        seq = tt * 8 + k

                        gsem = "dgA" if k % 2 == 0 else "dgB"

                        def _(g, tt=tt, k=k, seq=seq, gsem=gsem):
                            me.wv(g, "dv", mac_done.get(seq - 2, 0))
                            ins = g.indirect_dma_start(
                                out=me.UB[:, (k % 2) * 2048:(k % 2) * 2048 + 2048],
                                out_offset=None, in_=me.mb[:, :],
                                in_offset=bass.IndirectOffsetOnAxis(
                                    me.idxA[:, tt * 8 + k:tt * 8 + k + 1], axis=0))
                            me.inc(ins, gsem, 16)
                        self.G(_)

                        def _(v, tt=tt, k=k, yo=yo, j=j, gsem=gsem):
                            me.wv(v, gsem, me.cnt[gsem])
                            if k == 0:
                                me.wv(v, "ac", ytm_war[j % 2])
                            gbuf = me.UB[:, (k % 2) * 2048:(k % 2) * 2048 + 2048]
                            ytm = me.Uf32(yo, yo + 2048)
                            if k == 0:
                                ins = v.tensor_scalar(ytm, gbuf, me.w8A[:, tt * 8:tt * 8 + 1],
                                                      None, op0=ALU.mult)
                            else:
                                ins = v.scalar_tensor_tensor(
                                    ytm, gbuf, me.w8A[:, tt * 8 + k:tt * 8 + k + 1],
                                    ytm, op0=ALU.mult, op1=ALU.add)
                            me.inc(ins, "dv")
                        self.V(_)
                        mac_done[seq] = self.cnt["dv"]

                    for bq in range(4):
                        bk = tr_banks[bq % 2]

                        def _(t, bk=bk, bq=bq, yo=yo):
                            me.wv(t, "dv", me.cnt["dv"])
                            me.bank_wait(t, bk)
                            for i in range(4):
                                ins = t.transpose(
                                    bk[:, i * 128:i * 128 + 128],
                                    me.Uf32(yo + (bq * 4 + i) * 128, yo + (bq * 4 + i) * 128 + 128),
                                    me.identf[:, :])
                            me.inc(ins, "pe")
                        self.T(_)

                        def _(sc, bk=bk, bq=bq, half=half, j=j):
                            me.wv(sc, "pe", me.cnt["pe"])
                            for i in range(4):
                                dk = bq * 4 + i
                                ins = sc.copy(
                                    me.CTX[:, dk * TOK + half * 512 + j * 128:
                                           dk * TOK + half * 512 + j * 128 + 128],
                                    bk[:, i * 128:i * 128 + 128])
                            me.inc(ins, "ac")
                        self.S(_)
                        self.bank_set(bk, "ac")
                    ytm_war[j % 2] = self.cnt["ac"]
                ypre_done = self.cnt["ac"]
                self.close_block()
                self.open_block()

                sswar = self.cnt["pe"]
                zwar = {0: sswar, 1: sswar}
                for zc in range(8):
                    pi = zc % 2

                    wsem = "dwA" if pi == 0 else "dwB"

                    def _(g, zc=zc, pi=pi, wsem=wsem):
                        me.wv(g, "pe", zwar[pi])
                        me.dma(g, me.SS[:, pi * 4096:pi * 4096 + 4096],
                               me.wvP[zc * 128:(zc + 1) * 128, :], wsem)
                    self.G(_)
                    ld = (wsem, self.cnt[wsem])
                    for ic2 in range(2):
                        ic = zc * 2 + ic2
                        bk = banks[ic % 2]

                        def _(t, bk=bk, ic2=ic2, pi=pi, ld=ld, half=half):
                            me.wv(t, ld[0], ld[1])
                            me.wv(t, "ac", ypre_done)
                            me.bank_wait(t, bk)
                            for dk in range(DK):
                                ins = t.matmul(
                                    bk[:, 0:512],
                                    me.SS[:, pi * 4096 + dk * 256 + ic2 * 128:
                                           pi * 4096 + dk * 256 + ic2 * 128 + 128],
                                    me.CTX[:, dk * TOK + half * 512:dk * TOK + half * 512 + 512],
                                    start=(dk == 0), stop=(dk == DK - 1))
                            me.inc(ins, "pe")
                        self.T(_)

                        def _(v, bk=bk, ic=ic, half=half):
                            me.wv(v, "pe", me.cnt["pe"])
                            ins = v.tensor_copy(
                                me.LNB[:, ic * TOK + half * 512:ic * TOK + half * 512 + 512],
                                bk[:, 0:512])
                            me.inc(ins, "dv")
                        self.V(_)
                        self.bank_set(bk, "dv")
                    zwar[pi] = self.cnt["pe"]

                self.close_block()
                self.open_block()
                z_done = self.cnt["dv"]
                sswar2 = self.cnt["pe"]
                owar = {0: sswar2, 1: sswar2}
                xres_war = {0: 0, 1: 0}
                for oc2 in range(8):
                    pi = oc2 % 2

                    wsem = "dwA" if pi == 0 else "dwB"

                    def _(g, oc2=oc2, pi=pi, wsem=wsem):
                        me.wv(g, "pe", owar[pi])
                        me.dma(g, me.SS[:, pi * 4096:pi * 4096 + 4096],
                               me.woP[oc2 * 128:(oc2 + 1) * 128, :], wsem)
                    self.G(_)
                    ld = (wsem, self.cnt[wsem])
                    for o2 in range(2):
                        oc = oc2 * 2 + o2
                        bk = banks[oc % 2]
                        xb = (oc % 2) * 512

                        xsem = "dxA" if oc % 2 == 0 else "dxB"

                        def _(q, oc=oc, xb=xb, half=half, xsem=xsem):
                            me.wv(q, "dv", xres_war[oc % 2])
                            src = me.xs.rearrange("(k p) t -> p k t", p=128)[
                                :, oc:oc + 1, HALO + base + half * 512:HALO + base + half * 512 + 512]
                            me.dma(q, me.xrest[:, xb:xb + 512].rearrange("p (a t) -> p a t", a=1),
                                   src, xsem)
                        self.Q(_)
                        xld = (xsem, self.cnt[xsem])

                        def _(t, bk=bk, o2=o2, pi=pi, ld=ld, half=half):
                            me.wv(t, ld[0], ld[1])
                            me.wv(t, "dv", z_done)
                            me.bank_wait(t, bk)
                            for ik in range(DK):
                                ins = t.matmul(
                                    bk[:, 0:512],
                                    me.SS[:, pi * 4096 + ik * 256 + o2 * 128:
                                           pi * 4096 + ik * 256 + o2 * 128 + 128],
                                    me.LNB[:, ik * TOK + half * 512:ik * TOK + half * 512 + 512],
                                    start=(ik == 0), stop=(ik == DK - 1))
                            me.inc(ins, "pe")
                        self.T(_)

                        def _(v, bk=bk, oc=oc, xb=xb, half=half, xld=xld):
                            me.wv(v, "pe", me.cnt["pe"])
                            me.wv(v, xld[0], xld[1])
                            a = me.A32[:, oc * TOK + half * 512:oc * TOK + half * 512 + 512]
                            v.tensor_mul(me.tmpf[:, :], bk[:, 0:512],
                                         me.bypB[:, half * 512:half * 512 + 512])
                            v.tensor_add(me.tmpf[:, :], me.tmpf[:, :], a)
                            ins = v.tensor_add(a, me.tmpf[:, :], me.xrest[:, xb:xb + 512])
                            me.inc(ins, "dv")
                        self.V(_)
                        self.bank_set(bk, "dv")
                        xres_war[oc % 2] = self.cnt["dv"]
                    owar[pi] = self.cnt["pe"]
            self.close_block()

    def ph_ffn(self, mt):
        me = self
        if True:
            self.open_block()
            prev_sp_pl = 0
            prev_sp_dv = 0
            dCD = [self.pC, self.pD]
            QS = 6144
            gupld = {}
            ju_done = {}
            dload = {}
            sp_arr = {}
            down_done = {}

            JSEM = ("dwA", "dwB", "dwC", "dwD")

            def load_gu(sg, j):
                js = JSEM[j]

                def _(g, sg=sg, j=j, js=js):
                    me.wv(g, "pe", ju_done.get((sg - 1, j), 0))
                    r0 = (sg * 4 + j) * 128
                    me.dma(g, me.CTX[:, j * 2048:j * 2048 + 2048],
                           me.wgP[r0:r0 + 128, :], js)
                    me.dma(g, me.CTX[:, 8192 + j * 2048:8192 + j * 2048 + 2048],
                           me.wuP[r0:r0 + 128, :], js)
                self.G(_)
                gupld[(sg, j)] = (js, self.cnt[js])

            def load_down(sg):
                def _(g, sg=sg):
                    me.wv(g, "pe", down_done.get(sg - 1, 0))
                    me.dma(g, me.SS[:, 0:8192], me.wdP[sg * 128:(sg + 1) * 128, :], "dwE")
                self.G(_)
                dload[sg] = self.cnt["dwE"]

            def hsp(par, c0, c1):
                if par == 0:
                    return me.UB[:, c0:c1]
                return me.LNB[:, 16384 + c0:16384 + c1]

            def emit_down(sg):
                par = sg % 2
                for oc in range(16):
                    for tp in range(2):
                        bk = dCD[(oc * 2 + tp) % 2]

                        def _(t, bk=bk, oc=oc, tp=tp, sg=sg, par=par):
                            me.wv(t, "dwE", dload[sg])
                            me.wv(t, "dv", sp_arr[sg])
                            me.bank_wait(t, bk)
                            for ic in range(4):
                                ins = t.matmul(
                                    bk[:, 0:512],
                                    me.SS[:, ic * 2048 + oc * 128:ic * 2048 + oc * 128 + 128],
                                    hsp(par, ic * 1024 + tp * 512, ic * 1024 + tp * 512 + 512),
                                    start=(ic == 0), stop=(ic == 3))
                            me.inc(ins, "pe")
                        self.T(_)

                        def _(v, bk=bk, oc=oc, tp=tp):
                            me.wv(v, "pe", me.cnt["pe"])
                            a = me.A32[:, oc * TOK + tp * 512:oc * TOK + tp * 512 + 512]
                            ins = v.tensor_add(a, a, bk[:, 0:512])
                            me.inc(ins, "dv")
                        self.V(_)
                        self.bank_set(bk, "dv")
                down_done[sg] = self.cnt["pe"]

            for j in range(4):
                load_gu(0, j)
            load_down(0)

            for sg in range(16):
                gbanks = [self.pA, self.pE]
                ubanks = [self.pB, self.pBc1]
                stmp = [QS, QS + 1024]
                stmp_war = {0: prev_sp_dv, 1: prev_sp_dv}
                for j in range(4):
                    for tp in range(2):
                        par = (j * 2 + tp) % 2
                        gbk = gbanks[par]
                        ubk = ubanks[par]

                        def _(t, j=j, tp=tp, gbk=gbk, sg=sg):
                            me.wv(t, gupld[(sg, j)][0], gupld[(sg, j)][1])
                            me.bank_wait(t, gbk)
                            for dk in range(DK):
                                ins = t.matmul(
                                    gbk[:, 0:512],
                                    me.CTX[:, j * 2048 + dk * 128:j * 2048 + dk * 128 + 128],
                                    me.LNB[:, dk * TOK + tp * 512:dk * TOK + tp * 512 + 512],
                                    start=(dk == 0), stop=(dk == DK - 1))
                            me.inc(ins, "pe")
                        self.T(_)
                        gate_pe = self.cnt["pe"]

                        def _(t, j=j, tp=tp, ubk=ubk, sg=sg):
                            me.bank_wait(t, ubk)
                            for dk in range(DK):
                                ins = t.matmul(
                                    ubk[:, 0:512],
                                    me.CTX[:, 8192 + j * 2048 + dk * 128:8192 + j * 2048 + dk * 128 + 128],
                                    me.LNB[:, dk * TOK + tp * 512:dk * TOK + tp * 512 + 512],
                                    start=(dk == 0), stop=(dk == DK - 1))
                            me.inc(ins, "pe")
                        self.T(_)
                        if tp == 1:
                            ju_done[(sg, j)] = self.cnt["pe"]
                            if sg < 15:
                                load_gu(sg + 1, j)

                        def _(sc, j=j, tp=tp, gbk=gbk, par=par, gate_pe=gate_pe):
                            me.wv(sc, "pe", gate_pe)
                            me.wv(sc, "dv", stmp_war[par])
                            if j == 0 and tp == 0:
                                me.wv(sc, "pl", prev_sp_pl)
                            ins = sc.activation(
                                me.Uf32(stmp[par], stmp[par] + 1024)[:, tp * 512:tp * 512 + 512],
                                gbk[:, 0:512], AF.Silu)
                            me.inc(ins, "ac")
                        self.S(_)
                        self.bank_set(gbk, "ac")

                        def _(v, j=j, tp=tp, ubk=ubk, par=par):
                            me.wv(v, "pe", me.cnt["pe"])
                            me.wv(v, "ac", me.cnt["ac"])
                            if j == 0 and tp == 0:
                                me.wv(v, "pl", prev_sp_pl)
                            ins = v.tensor_mul(
                                me.Uf32(2048 + j * 1024, 2048 + j * 1024 + 1024)[:, tp * 512:tp * 512 + 512],
                                me.Uf32(stmp[par], stmp[par] + 1024)[:, tp * 512:tp * 512 + 512],
                                ubk[:, 0:512])
                            me.inc(ins, "dv")
                        self.V(_)
                        self.bank_set(ubk, "dv")
                        stmp_war[par] = self.cnt["dv"]
                p_done = self.cnt["dv"]

                if sg >= 1:
                    emit_down(sg - 1)
                    if sg < 15:
                        load_down(sg)

                def pq(j, q4):
                    return me.Uf32(2048 + j * 1024 + q4 * 256, 2048 + j * 1024 + q4 * 256 + 256)

                spar = sg % 2
                for q4 in range(4):
                    sj = [me.Uf32(QS + 256 * j, QS + 256 * (j + 1)) for j in range(4)]
                    sA = me.Uf32(QS + 1024, QS + 1280)
                    sB = me.Uf32(QS + 1280, QS + 1536)
                    sC = me.Uf32(QS + 1536, QS + 1792)
                    sD = me.Uf32(QS + 1792, QS + 2048)

                    def _(g, q4=q4, sj=sj):
                        me.wv(g, "dv", p_done if q4 == 0 else me.cnt["dv"])
                        for j in range(4):
                            ins = g.tensor_mul(sj[j], pq(j, q4), pq(j, q4))
                        me.inc(ins, "pl")
                    self.G(_)

                    def _(v, sj=sj, sA=sA, sB=sB, sC=sC, sD=sD):
                        me.wv(v, "pl", me.cnt["pl"])
                        v.tensor_tensor(sA, sj[0], sj[1], op=ALU.max)
                        v.tensor_tensor(sB, sj[0], sj[1], op=ALU.min)
                        v.tensor_tensor(sC, sj[2], sj[3], op=ALU.max)
                        v.tensor_tensor(sD, sj[2], sj[3], op=ALU.min)
                        v.tensor_tensor(sA, sA, sC, op=ALU.min)
                        v.tensor_tensor(sB, sB, sD, op=ALU.max)
                        ins = v.tensor_tensor(sA, sA, sB, op=ALU.max)
                        me.inc(ins, "dv")
                    self.V(_)

                    def _(g, sj=sj, sA=sA):
                        me.wv(g, "dv", me.cnt["dv"])
                        for j in range(4):
                            ins = g.tensor_sub(sj[j], sj[j], sA)
                        me.inc(ins, "pl")
                    self.G(_)

                    def _(v, q4=q4, sj=sj, sg=sg, spar=spar):
                        me.wv(v, "pl", me.cnt["pl"])
                        if q4 == 0:
                            me.wv(v, "pe", down_done.get(sg - 2, 0))
                        for j in range(4):
                            ins = v.scalar_tensor_tensor(
                                hsp(spar, j * 1024 + q4 * 256, j * 1024 + q4 * 256 + 256),
                                sj[j], 0.0, pq(j, q4), op0=ALU.is_ge, op1=ALU.mult)
                        me.inc(ins, "dv")
                    self.V(_)
                sp_arr[sg] = self.cnt["dv"]
                prev_sp_pl = self.cnt["pl"]
                prev_sp_dv = sp_arr[sg]
            load_down(15)
            emit_down(15)

    def ph_store(self, mt):
        me = self
        if True:
            self.open_block()

            def _(q):
                me.dma(q, me.out.rearrange("(k p) t -> p k t", p=128)[:, :, mt * TOK:mt * TOK + TOK],
                       me.wdst(me.A32[:, :], TOK), "do")
            self.Q(_)
            self.store_done = self.cnt["do"]

    def ph_final(self):
        me = self
        if True:
            self.open_block()

            def _(q):
                q.wait_ge(me.sem["do"], me.cnt["do"])
            self.Q(_)

    def build(self):
        self.marks = []

        def mark(name):
            self.marks.append((name, self.nc.n_instructions()))
        self.mark = mark
        mark("setup")
        self.ph_setup()
        for mt in (0, 1):
            u = self.upto
            self.mark(f"ln1.{mt}")
            self.ph_ln(mt, "x")
            self.tap(f"ln1h{mt}", [(self.LNB[:, :], 20480)])
            if u >= 1:
                self.mark(f"heads.{mt}")
                self.ph_heads(mt)
                self.tap(f"ctx{mt}", [(self.CTX[:, :], 16384)])
            if u >= 2:
                self.mark(f"u.{mt}")
                self.ph_u(mt)
                self.tap(f"u{mt}", [(self.UB[:, :], 16384)])
            if u >= 3:
                self.mark(f"olg.{mt}")
                self.ph_olocal_gate(mt)
                self.tap(f"ol{mt}", [(self.A32[:, :], 16384)])
                self.tap(f"byp{mt}", [(self.bypB[:, :], 1024)])
            if u >= 4:
                self.mark(f"sim.{mt}")
                self.ph_sim_topk(mt)
                self.tap(f"idx{mt}", [(self.idxA[:, :], 64), (self.w8A[:, :], 64)])
            if u >= 5:
                self.mark(f"mem.{mt}")
                self.ph_mem(mt)
                self.tap(f"yp{mt}", [(self.CTX[:, :], 16384)])
                self.tap(f"z{mt}", [(self.LNB[:, 0:16384], 16384)])
                self.tap(f"xp{mt}", [(self.A32[:, :], 16384)])
            if u >= 6:
                self.mark(f"ln2.{mt}")
                self.ph_ln(mt, "acc")
                self.tap(f"ln2_{mt}", [(self.LNB[:, 0:16384], 16384)])
            if u >= 7:
                self.mark(f"ffn.{mt}")
                self.ph_ffn(mt)
                self.tap(f"ff{mt}", [(self.A32[:, :], 16384)])
            self.mark(f"store.{mt}")
            self.ph_store(mt)
        self.mark("final")
        self.ph_final()
        self.close_block()
        self.nc.finalize()
        return self.nc


_CACHE = {}


def _build_masks():
    r = np.arange(128)[:, None]
    c = np.arange(384)[None, :]
    band = (c >= r + 1) & (c <= r + 256)
    mg = np.where(band, 0.0, -1.0e30).astype(np.float32)
    m0 = np.where(band & (c >= 256), 0.0, -1.0e30).astype(np.float32)
    m1 = np.where(band & (c >= 128), 0.0, -1.0e30).astype(np.float32)
    return mg, m0, m1


def _chunked(wT, ncol):
    Dr, X = wT.shape
    nch = X // ncol
    return np.ascontiguousarray(
        wT.reshape(16, 128, nch, ncol).transpose(2, 1, 0, 3).reshape(nch * 128, 16 * ncol))


def _wd_chunked(wd):
    return np.ascontiguousarray(
        wd.reshape(16, 4, 128, 2048).transpose(0, 2, 1, 3).reshape(2048, 8192))


def _ffn_perm():
    perm = np.empty(I2, dtype=np.int64)
    idx = 0
    for sg in range(16):
        for j in range(4):
            for gw in range(128):
                perm[idx] = 4 * (sg * 128 + gw) + j
                idx += 1
    return perm


def kernel(L0_attn_hidden_states, L0_attn_wq, L0_attn_wk, L0_attn_wv, L0_attn_wo,
           L0_attn_gate_w, L0_attn_gate_b, L0_attn_mem_bank,
           L0_ffn_w_gate, L0_ffn_w_up, L0_ffn_w_down):
    x = np.asarray(L0_attn_hidden_states, dtype=np.float32)
    f32c = lambda a: np.ascontiguousarray(np.asarray(a, dtype=np.float32))
    wq, wk, wv, wo = map(f32c, (L0_attn_wq, L0_attn_wk, L0_attn_wv, L0_attn_wo))
    gate_w = f32c(L0_attn_gate_w)
    gate_b = f32c(L0_attn_gate_b).reshape(1, 1)
    mbank = f32c(L0_attn_mem_bank)
    wg_, wu_, wd_ = map(f32c, (L0_ffn_w_gate, L0_ffn_w_up, L0_ffn_w_down))

    perm = _ffn_perm()
    shared = {
        "wqP": _chunked(wq.T, 256), "wkP": _chunked(wk.T, 256),
        "wvP": _chunked(wv.T, 256), "woP": _chunked(wo.T, 256),
        "wkU": _chunked(wk, 256), "mbP": _chunked(mbank.T, 256),
        "mb": mbank,
        "gwT": np.ascontiguousarray(gate_w.reshape(16, 128).T),
        "gb": gate_b,
        "wgP": _chunked(wg_[:, perm], 128),
        "wuP": _chunked(wu_[:, perm], 128),
        "wdP": _wd_chunked(wd_[perm, :]),
    }
    mg, m0, m1 = _build_masks()
    masks_h0 = np.ascontiguousarray(np.concatenate([mg, m0, m1], axis=1))
    masks_h1 = np.ascontiguousarray(np.concatenate([mg, mg, mg], axis=1))

    in_maps = []
    for c in range(8):
        b, half = c // 2, c % 2
        xT = x[b].T
        if half == 0:
            xsl = np.concatenate([np.zeros((D, HALO), np.float32), xT[:, 0:2048]], axis=1)
        else:
            xsl = xT[:, 2048 - HALO:4096]
        m = dict(shared)
        m["xs"] = np.ascontiguousarray(xsl)
        m["masks"] = masks_h0 if half == 0 else masks_h1
        in_maps.append(m)

    if "nc" not in _CACHE:
        kb = KB()
        _CACHE["nc"] = kb.build()
    nc = _CACHE["nc"]
    res = run_bass_kernel_spmd(nc, in_maps, core_ids=list(range(8)))

    out = np.empty((4, 4096, D), dtype=np.float32)
    for c in range(8):
        b, half = c // 2, c % 2
        out[b, half * 2048:(half + 1) * 2048, :] = res.results[c]["out"].T
    return out


# version 07
# speedup vs baseline: 1.0365x; 1.0195x over previous
import math
import os
import numpy as np
from contextlib import ExitStack

import concourse.bass as bass
import concourse.mybir as mybir
from concourse.bass_utils import run_bass_kernel_spmd

F32 = mybir.dt.float32
BF16 = mybir.dt.bfloat16
I32 = mybir.dt.int32
AF = mybir.ActivationFunctionType
ALU = mybir.AluOpType
AX = mybir.AxisListType

D = 2048
DK = 16
TOK = 1024
HALO = 256
WINT = TOK + HALO
MEM = 4096
I2 = 8192
U_MODE = 8
HEADS_MODE = 5
SQD = 1.0 / math.sqrt(D)
SQH = 1.0 / math.sqrt(128.0)
EPS = 1e-5


class KB:
    def __init__(self, debug_taps=(), upto=99):
        self.taps = tuple(debug_taps)
        self.upto = upto
        nc = bass.Bass("TRN2", target_bir_lowering=False, dynamic_dma_scratch_size=16384,
                       detect_race_conditions=False)
        self.nc = nc
        self.ctx = ExitStack()
        e = self.ctx.enter_context

        dt = nc.dram_tensor
        self.xs = dt("xs", [D, HALO + 2048], F32, kind="ExternalInput").ap()
        self.wqP = dt("wqP", [1024, 4096], F32, kind="ExternalInput").ap()
        self.wkP = dt("wkP", [1024, 4096], F32, kind="ExternalInput").ap()
        self.wvP = dt("wvP", [1024, 4096], F32, kind="ExternalInput").ap()
        self.woP = dt("woP", [1024, 4096], F32, kind="ExternalInput").ap()
        self.wkU = dt("wkU", [1024, 4096], F32, kind="ExternalInput").ap()
        self.mbP = dt("mbP", [2048, 4096], F32, kind="ExternalInput").ap()
        self.mb = dt("mb", [MEM, D], F32, kind="ExternalInput").ap()
        self.gwT = dt("gwT", [128, 16], F32, kind="ExternalInput").ap()
        self.gb = dt("gb", [1, 1], F32, kind="ExternalInput").ap()
        self.wgP = dt("wgP", [8192, 2048], F32, kind="ExternalInput").ap()
        self.wuP = dt("wuP", [8192, 2048], F32, kind="ExternalInput").ap()
        self.wdP = dt("wdP", [2048, 8192], F32, kind="ExternalInput").ap()
        self.masks_d = dt("masks", [128, 3 * 384], F32, kind="ExternalInput").ap()
        self.out = dt("out", [D, 2048], F32, kind="ExternalOutput").ap()
        self.dbg = {}
        for t in self.taps:
            self.dbg[t] = dt("dbg_" + t, [128, 20480], F32, kind="ExternalOutput").ap()

        self.A32 = e(nc.sbuf_tensor("A32", [128, 16384], F32))
        self.LNB = e(nc.sbuf_tensor("LNB", [128, 20480], BF16))
        self.CTX = e(nc.sbuf_tensor("CTXB", [128, 16384], BF16))
        self.UB = e(nc.sbuf_tensor("UBB", [128, 16384], BF16))
        self.SS = e(nc.sbuf_tensor("SSB", [128, 8192], BF16))

        self.masks = e(nc.sbuf_tensor("masks_s", [128, 3 * 384], F32))
        self.colidx = e(nc.sbuf_tensor("colidx", [128, 256], I32))
        self.identb = e(nc.sbuf_tensor("identb", [128, 128], BF16))
        self.identf = e(nc.sbuf_tensor("identf", [128, 128], F32))
        self.onescol = e(nc.sbuf_tensor("onescol", [128, 1], BF16))
        self.onesrow = e(nc.sbuf_tensor("onesrow", [1, 128], BF16))
        self.onesrowf = e(nc.sbuf_tensor("onesrowf", [1, 128], F32))
        ubf = self.UB[:, :].bitcast(F32)
        self.ms = [ubf[:, 2048 + 384 * i:2048 + 384 * (i + 1)] for i in range(2)]
        self.probs = [ubf[:, 2816 + 384 * i:2816 + 384 * (i + 1)] for i in range(2)]
        self.attn = [self.UB[:, 7168 + 384 * i:7168 + 384 * (i + 1)] for i in range(4)]
        self.attnT = [self.UB[:, 8704 + 384 * i:8704 + 384 * (i + 1)] for i in range(2)]
        self.nmax = [e(nc.sbuf_tensor(f"nmax{i}", [128, 1], F32)) for i in range(2)]
        self.esum = [e(nc.sbuf_tensor(f"esum{i}", [128, 1], F32)) for i in range(2)]
        self.ersum = [e(nc.sbuf_tensor(f"ersum{i}", [128, 1], F32)) for i in range(2)]
        self.packed = e(nc.sbuf_tensor("packed", [128, 256], F32))
        self.candA = e(nc.sbuf_tensor("candA", [128, 8 * 128], F32))
        self.run8 = e(nc.sbuf_tensor("run8", [128, 8], F32))
        self.vals8 = e(nc.sbuf_tensor("vals8", [128, 8], F32))
        self.ev8 = e(nc.sbuf_tensor("ev8", [128, 8], F32))
        self.idxA = e(nc.sbuf_tensor("idxA", [128, 64], I32))
        self.w8A = e(nc.sbuf_tensor("w8A", [128, 64], F32))
        self.xbs = [e(nc.sbuf_tensor(f"xbs{i}", [128, 512], BF16)) for i in range(2)]
        self.sqs = [e(nc.sbuf_tensor(f"sqs{i}", [128, 512], BF16)) for i in range(2)]
        self.nmr = ubf[0:1, 0:512]
        self.vrr = ubf[0:1, 512:1024]
        self.m2r = ubf[0:1, 1024:1536]
        self.rsr = ubf[0:1, 1536:2048]
        self.nmb = self.UB[0:1, 4096:4608]
        self.rsb = self.UB[0:1, 4608:5120]
        self.tmpf = e(nc.sbuf_tensor("tmpf", [128, 512], F32))
        ssf = self.SS[:, :].bitcast(F32)
        self.grow = ssf[0:1, 0:1024]
        self.gtmp = ssf[0:1, 1024:2048]
        self.neggb = e(nc.sbuf_tensor("neggb", [1, 1], F32))
        self.bypB = e(nc.sbuf_tensor("bypB", [128, 1024], F32))
        self.gwTs = e(nc.sbuf_tensor("gwTs", [128, 16], F32))
        self.xrest = self.candA[:, :]
        self.gbs = e(nc.sbuf_tensor("gbs", [1, 1], F32))

        self.pA = e(nc.psum_tensor("pA", [128, 512], F32))
        self.pB = e(nc.psum_tensor("pB", [128, 512], F32))
        self.pC = e(nc.psum_tensor("pC", [128, 512], F32))
        self.pD = e(nc.psum_tensor("pD", [128, 512], F32))
        self.pE = e(nc.psum_tensor("pE", [128, 512], F32))
        self.pS = e(nc.psum_tensor("pS", [1, 512], F32))
        self.pBc1 = e(nc.psum_tensor("pBc1", [128, 512], F32))
        self.pBc2 = e(nc.psum_tensor("pBc2", [128, 512], F32))

        self.sem = {}
        self.cnt = {}
        for s in ("dw", "dx", "dg", "do", "pe", "dv", "ac", "pl",
                  "dwA", "dwB", "dwC", "dwD", "dwE", "dgA", "dgB", "dxA", "dxB"):
            self.sem[s] = e(nc.semaphore(s))
            self.cnt[s] = 0
        self.bankc = {}
        self.store_done = 0
        assert nc.sbuf_bytes_remaining >= 0, nc.sbuf_bytes_remaining

    def open_block(self):
        if getattr(self, "blk", None) is not None:
            self.close_block()
        self.blk = self.nc.Block().__enter__()
        self.blk.vector(lambda e: e.nop())
        self.blk.scalar(lambda e: e.nop())
        self.blk.tensor(lambda e: e.nop())
        self.blk.gpsimd(lambda e: e.nop())
        self.blk.sync(lambda e: e.nop())

    def close_block(self):
        if self.blk is not None:
            self.blk.__exit__(None, None, None)
            self.blk = None

    def V(self, f):
        self.blk.vector(f)

    def S(self, f):
        self.blk.scalar(f)

    def T(self, f):
        self.blk.tensor(f)

    def G(self, f):
        self.blk.gpsimd(f)

    def Q(self, f):
        self.blk.sync(f)

    def inc(self, ins, s, n=1):
        ins.then_inc(self.sem[s], n)
        self.cnt[s] += n
        return self.cnt[s]

    def wv(self, eng, s, tgt):
        if tgt > 0:
            eng.wait_ge(self.sem[s], tgt)

    def bank_wait(self, eng, bank):
        c = self.bankc.get(bank.name)
        if c:
            self.wv(eng, c[0], c[1])

    def bank_set(self, bank, s):
        self.bankc[bank.name] = (s, self.cnt[s])

    def Abf(self, c0, c1):
        return self.A32[:, :].bitcast(BF16)[:, c0:c1]

    def Uf32(self, c0, c1):
        return self.UB[:, :].bitcast(F32)[:, c0:c1]

    def wsrc(self, wdram, c0, cols, k0=0, nk=DK):
        return wdram.rearrange("(k p) o -> p k o", p=128)[:, k0:k0 + nk, c0:c0 + cols]

    def wdst(self, tile_ap, cols):
        return tile_ap.rearrange("p (k f) -> p k f", f=cols)

    def dma(self, q, dst, src, s):
        return self.inc(q.dma_start(out=dst, in_=src), s, 16)

    def tap(self, name, srcs):
        if name not in self.taps:
            return
        me = self
        dst = self.dbg[name]
        if True:
            self.open_block()

            def _(g):
                col = 0
                for ap, ncols in srcs:
                    me.dma(g, dst[:, col:col + ncols], ap, "dg")
                    col += ncols
                g.wait_ge(me.sem["dg"], me.cnt["dg"])
            self.G(_)

    def ph_setup(self):
        me = self
        if True:
            self.open_block()

            def _(q):
                me.dma(q, me.masks[:, :], me.masks_d[:, :], "dxA")
                me.dma(q, me.gwTs[:, :], me.gwT[:, :], "dxA")
                me.dma(q, me.gbs[:, :], me.gb[:, :], "dxA")
            self.Q(_)

            def _(g):
                g.iota(me.colidx[:, :], [[1, 256]], base=0, channel_multiplier=0)
                g.memset(me.identb[:, :], 1.0)
                g.affine_select(me.identb[:, :], me.identb[:, :], [[1, 128]],
                                ALU.is_equal, 0.0, base=0, channel_multiplier=-1)
                g.memset(me.identf[:, :], 1.0)
                g.affine_select(me.identf[:, :], me.identf[:, :], [[1, 128]],
                                ALU.is_equal, 0.0, base=0, channel_multiplier=-1)
                g.memset(me.onescol[:, :], 1.0)
                g.memset(me.onesrow[:, :], 1.0)
                ins = g.memset(me.onesrowf[:, :], 1.0)
                me.inc(ins, "pl")
            self.G(_)

            def _(v):
                me.wv(v, "dxA", me.cnt["dxA"])
                me.wv(v, "pl", me.cnt["pl"])
                ins = v.tensor_scalar(me.neggb[:, :], me.gbs[:, :], -1.0, None, op0=ALU.mult)
                me.inc(ins, "dv")
            self.V(_)

    def ph_ln(self, mt, src):
        me = self
        base = mt * TOK
        slabs = [(0, 512), (512, 512), (1024, 256)] if src == "x" else [(0, 512), (512, 512)]
        ow = WINT if src == "x" else TOK
        if True:
            self.open_block()
            apply_done = {}
            bc_prev = 0
            for si, (s0, sw) in enumerate(slabs):
                pi = si % 2
                lsem = "dxA" if pi == 0 else "dxB"
                if src == "x":
                    def _(q, si=si, s0=s0, sw=sw, pi=pi, lsem=lsem):
                        if si >= 2:
                            me.wv(q, "dv", apply_done[si - 2])
                        if mt == 1:
                            me.wv(q, "do", me.store_done)
                        dst = me.A32[:, pi * 8192:pi * 8192 + 8192]
                        me.dma(q, me.wdst(dst, 512)[:, :, 0:sw],
                               me.xs.rearrange("(k p) t -> p k t", p=128)
                               [:, :, base + s0:base + s0 + sw], lsem)
                    self.Q(_)
                ld_tgt = self.cnt[lsem]

                def chunk(dk):
                    if src == "x":
                        return me.A32[:, pi * 8192 + dk * 512:pi * 8192 + dk * 512 + sw]
                    return me.A32[:, dk * TOK + s0:dk * TOK + s0 + sw]

                pe_at_slab_start = self.cnt["pe"]
                stats_tgt = {}
                sbk1 = me.pS if si % 2 == 0 else me.pA
                sbk2 = me.pBc1 if si % 2 == 0 else me.pB
                for dk in range(DK):
                    par = dk % 2

                    def _(sc, dk=dk, sw=sw, par=par, lsem=lsem):
                        if dk == 0 and src == "x":
                            me.wv(sc, lsem, ld_tgt)
                        me.wv(sc, "pe", stats_tgt[dk - 2] if dk > 1 else pe_at_slab_start)
                        ch = chunk(dk)
                        sc.copy(me.xbs[par][:, 0:sw], ch)
                        ins = sc.square(me.sqs[par][:, 0:sw], ch)
                        me.inc(ins, "ac")
                    self.S(_)

                    def _(t, dk=dk, sw=sw, par=par, sbk1=sbk1, sbk2=sbk2):
                        me.wv(t, "ac", me.cnt["ac"])
                        if dk == 0:
                            me.bank_wait(t, sbk1)
                            me.bank_wait(t, sbk2)
                        t.matmul(sbk1[0:1, 0:sw], me.onescol[:, :], me.xbs[par][:, 0:sw],
                                 start=(dk == 0), stop=(dk == DK - 1))
                        ins = t.matmul(sbk2[0:1, 0:sw], me.onescol[:, :], me.sqs[par][:, 0:sw],
                                       start=(dk == 0), stop=(dk == DK - 1))
                        me.inc(ins, "pe")
                        stats_tgt[dk] = me.cnt["pe"]
                    self.T(_)

                def _(v, sw=sw, sbk1=sbk1, sbk2=sbk2):
                    me.wv(v, "pe", me.cnt["pe"])
                    v.tensor_scalar(me.nmr[0:1, 0:sw], sbk1[0:1, 0:sw], -1.0 / D, None, op0=ALU.mult)
                    v.tensor_scalar(me.vrr[0:1, 0:sw], sbk2[0:1, 0:sw], 1.0 / D, None, op0=ALU.mult)
                    v.tensor_mul(me.m2r[0:1, 0:sw], me.nmr[0:1, 0:sw], me.nmr[0:1, 0:sw])
                    v.tensor_sub(me.vrr[0:1, 0:sw], me.vrr[0:1, 0:sw], me.m2r[0:1, 0:sw])
                    ins = v.tensor_scalar(me.vrr[0:1, 0:sw], me.vrr[0:1, 0:sw], EPS, None,
                                          op0=ALU.add)
                    me.inc(ins, "dv")
                self.V(_)
                self.bank_set(sbk1, "dv")

                def _(sc, sw=sw):
                    me.wv(sc, "dv", me.cnt["dv"])
                    ins = sc.activation(me.rsr[0:1, 0:sw], me.vrr[0:1, 0:sw], AF.Sqrt)
                    me.inc(ins, "ac")
                self.S(_)

                def _(v, sw=sw):
                    me.wv(v, "ac", me.cnt["ac"])
                    me.wv(v, "pe", bc_prev)
                    ins = v.reciprocal(me.vrr[0:1, 0:sw], me.rsr[0:1, 0:sw])
                    me.inc(ins, "dv")
                    v.wait_ge(me.sem["dv"], me.cnt["dv"])
                    v.tensor_copy(me.nmb[0:1, 0:sw], me.nmr[0:1, 0:sw])
                    ins = v.tensor_copy(me.rsb[0:1, 0:sw], me.vrr[0:1, 0:sw])
                    me.inc(ins, "dv")
                self.V(_)
                self.bank_set(sbk2, "dv")

                bbk1 = me.pBc2 if si % 2 == 0 else me.pC
                bbk2 = me.pD if si % 2 == 0 else me.pE

                def _(t, sw=sw, bbk1=bbk1, bbk2=bbk2):
                    me.wv(t, "dv", me.cnt["dv"])
                    me.bank_wait(t, bbk1)
                    me.bank_wait(t, bbk2)
                    t.matmul(bbk1[:, 0:sw], me.onesrow[:, :], me.nmb[0:1, 0:sw],
                             start=True, stop=True)
                    ins = t.matmul(bbk2[:, 0:sw], me.onesrow[:, :], me.rsb[0:1, 0:sw],
                                   start=True, stop=True)
                    me.inc(ins, "pe")
                self.T(_)
                bc_tgt = self.cnt["pe"]
                bc_prev = bc_tgt

                for dk in range(DK):
                    def _(v, dk=dk, s0=s0, sw=sw, bbk1=bbk1, bbk2=bbk2):
                        if dk == 0:
                            me.wv(v, "pe", bc_tgt)
                        ch = chunk(dk)
                        v.tensor_add(me.tmpf[:, 0:sw], ch, bbk1[:, 0:sw])
                        ins = v.tensor_mul(me.LNB[:, dk * ow + s0:dk * ow + s0 + sw],
                                           me.tmpf[:, 0:sw], bbk2[:, 0:sw])
                        me.inc(ins, "dv")
                    self.V(_)
                apply_done[si] = self.cnt["dv"]
                self.bank_set(bbk1, "dv")
                self.bank_set(bbk2, "dv")

    def ph_heads(self, mt):
        me = self
        if True:
            self.open_block()
            wld = {}
            war = {0: 0, 1: 0}
            proj_banks = [self.pA, self.pB, self.pE]
            tr_banks = [self.pC, self.pD]
            ctx_banks = [self.pBc1, self.pBc2]
            rot = {"p": 0}
            ctx_done = 0
            tr_done = {}
            exp_done = {}
            attn_done = {}
            ctxevac = {}

            for hp in range(8):
                pi = hp % 2
                woff = pi * 12288

                wsem = "dwA" if pi == 0 else "dwB"

                def _(g, hp=hp, pi=pi, woff=woff, wsem=wsem):
                    me.wv(g, "pe", war[pi])
                    if hp == 0:
                        me.wv(g, "do", me.store_done)
                    for w, wo in ((me.wqP, 0), (me.wkP, 4096), (me.wvP, 8192)):
                        me.dma(g, me.Abf(woff + wo, woff + wo + 4096),
                               w[hp * 128:(hp + 1) * 128, :], wsem)
                self.G(_)
                wld[hp] = (wsem, self.cnt[wsem])

                for h2 in range(2):
                    h = hp * 2 + h2
                    if HEADS_MODE < 2:
                        continue
                    for (name, wo, dst_off, spans) in (
                        ("q", 0, 24576, [(HALO, 512), (HALO + 512, 512)]),
                        ("k", 4096, 25600, [(0, 512), (512, 512), (1024, 256)]),
                        ("v", 8192, 26880, [(0, 512), (512, 512), (1024, 256)]),
                    ):
                        for (c0, cw) in spans:
                            bk = proj_banks[rot["p"] % 3]
                            rot["p"] += 1

                            def _(t, bk=bk, wo=wo, c0=c0, cw=cw, woff=woff, h2=h2, hp=hp):
                                me.wv(t, wld[hp][0], wld[hp][1])
                                me.bank_wait(t, bk)
                                for dk in range(DK):
                                    ins = t.matmul(
                                        bk[:, 0:cw],
                                        me.Abf(woff + wo + dk * 256 + h2 * 128,
                                               woff + wo + dk * 256 + h2 * 128 + 128),
                                        me.LNB[:, dk * WINT + c0:dk * WINT + c0 + cw],
                                        start=(dk == 0), stop=(dk == DK - 1))
                                me.inc(ins, "pe")
                            self.T(_)

                            def _(v, bk=bk, dst_off=dst_off, c0=c0, cw=cw, name=name,
                                  ctx_done=ctx_done):
                                me.wv(v, "pe", me.cnt["pe"])
                                me.wv(v, "pe", ctx_done)
                                b0 = c0 if name != "q" else c0 - HALO
                                ins = v.tensor_copy(me.Abf(dst_off + b0, dst_off + b0 + cw),
                                                    bk[:, 0:cw])
                                me.inc(ins, "dv")
                            self.V(_)
                            self.bank_set(bk, "dv")
                    vproj_done = self.cnt["dv"]
                    if h2 == 1:
                        war[pi] = self.cnt["pe"]

                    if HEADS_MODE < 3:
                        continue
                    for bi_, bstart in enumerate((0, 4, 8)):
                        nb = min(4, 10 - bstart)
                        bk = tr_banks[bi_ % 2]

                        def _(t, bk=bk, bstart=bstart, nb=nb, vproj_done=vproj_done):
                            me.wv(t, "dv", vproj_done)
                            me.bank_wait(t, bk)
                            bb = bk[:, :].bitcast(BF16)
                            for i in range(nb):
                                ins = t.transpose(bb[:, i * 128:(i + 1) * 128],
                                                  me.Abf(26880 + (bstart + i) * 128,
                                                         26880 + (bstart + i) * 128 + 128),
                                                  me.identb[:, :])
                            me.inc(ins, "pe")
                        self.T(_)

                        def _(v, bk=bk, bstart=bstart, nb=nb):
                            me.wv(v, "pe", me.cnt["pe"])
                            bb = bk[:, :].bitcast(BF16)
                            ins = v.tensor_copy(me.Abf(28160 + bstart * 128, 28160 + (bstart + nb) * 128),
                                                bb[:, 0:nb * 128])
                            me.inc(ins, "dv")
                        self.V(_)
                        self.bank_set(bk, "dv")

                    if HEADS_MODE < 4:
                        continue
                    attn_ready = {}
                    attn_war = {0: 0, 1: 0}

                    def stage_front(qp):
                        qt0, qt1 = 2 * qp, 2 * qp + 1
                        pp = qp % 2
                        a0, a1 = 2 * pp, 2 * pp + 1
                        sbk0 = proj_banks[rot["p"] % 3]
                        rot["p"] += 1
                        sbk1 = proj_banks[rot["p"] % 3]
                        rot["p"] += 1

                        def mi_of(qt):
                            if mt == 0 and qt == 0:
                                return 1
                            if mt == 0 and qt == 1:
                                return 2
                            return 0

                        def _(t, sbk0=sbk0, sbk1=sbk1, qt0=qt0, qt1=qt1):
                            me.bank_wait(t, sbk0)
                            ins = t.matmul(sbk0[:, 0:384],
                                           me.Abf(24576 + qt0 * 128, 24576 + qt0 * 128 + 128),
                                           me.Abf(25600 + qt0 * 128, 25600 + qt0 * 128 + 384),
                                           start=True, stop=True)
                            me.inc(ins, "pe")
                            me.bank_wait(t, sbk1)
                            ins = t.matmul(sbk1[:, 0:384],
                                           me.Abf(24576 + qt1 * 128, 24576 + qt1 * 128 + 128),
                                           me.Abf(25600 + qt1 * 128, 25600 + qt1 * 128 + 384),
                                           start=True, stop=True)
                            me.inc(ins, "pe")
                        self.T(_)

                        def _(v, sbk0=sbk0, sbk1=sbk1, qt0=qt0, qt1=qt1):
                            me.wv(v, "pe", me.cnt["pe"])
                            me.wv(v, "ac", exp_done.get(0, 0))
                            v.scalar_tensor_tensor(me.ms[0][:, :], sbk0[:, 0:384], SQH,
                                                   me.masks[:, mi_of(qt0) * 384:mi_of(qt0) * 384 + 384],
                                                   op0=ALU.mult, op1=ALU.add)
                            ins = v.scalar_tensor_tensor(me.ms[1][:, :], sbk1[:, 0:384], SQH,
                                                         me.masks[:, mi_of(qt1) * 384:mi_of(qt1) * 384 + 384],
                                                         op0=ALU.mult, op1=ALU.add)
                            me.inc(ins, "dv")
                            ins = v.reduce_max(me.nmax[0][:, :], me.ms[0][:, :],
                                               axis=AX.X, negate=True)
                            me.inc(ins, "dv")
                            ins = v.reduce_max(me.nmax[1][:, :], me.ms[1][:, :],
                                               axis=AX.X, negate=True)
                            me.inc(ins, "dv")
                        self.V(_)
                        self.bank_set(sbk0, "dv")
                        self.bank_set(sbk1, "dv")

                        def _(sc):
                            me.wv(sc, "dv", me.cnt["dv"])
                            me.wv(sc, "dv", attn_done.get(0, 0))
                            sc.activation(me.probs[0][:, :], me.ms[0][:, :], AF.Exp,
                                          bias=me.nmax[0][:, 0:1], scale=1.0,
                                          accum_out=me.esum[0][:, 0:1])
                            ins = sc.activation(me.probs[1][:, :], me.ms[1][:, :], AF.Exp,
                                                bias=me.nmax[1][:, 0:1], scale=1.0,
                                                accum_out=me.esum[1][:, 0:1])
                            me.inc(ins, "ac")
                        self.S(_)
                        exp_done[0] = self.cnt["ac"]

                        def _(v, a0=a0, a1=a1, pp=pp):
                            me.wv(v, "ac", me.cnt["ac"])
                            me.wv(v, "pe", attn_war[pp])
                            ins = v.reciprocal(me.ersum[0][:, :], me.esum[0][:, :])
                            me.inc(ins, "dv")
                            ins = v.reciprocal(me.ersum[1][:, :], me.esum[1][:, :])
                            me.inc(ins, "dv")
                            v.wait_ge(me.sem["dv"], me.cnt["dv"])
                            v.tensor_scalar(me.attn[a0][:, :], me.probs[0][:, :],
                                            me.ersum[0][:, 0:1], None, op0=ALU.mult)
                            ins = v.tensor_scalar(me.attn[a1][:, :], me.probs[1][:, :],
                                                  me.ersum[1][:, 0:1], None, op0=ALU.mult)
                            me.inc(ins, "dv")
                        self.V(_)
                        attn_done[0] = self.cnt["dv"]
                        attn_ready[qp] = self.cnt["dv"]

                    def stage_back(qp):
                        qt0, qt1 = 2 * qp, 2 * qp + 1
                        pp = qp % 2
                        a0, a1 = 2 * pp, 2 * pp + 1

                        def _(t, a0=a0, a1=a1, qp=qp):
                            me.wv(t, "dv", attn_ready[qp])
                            me.bank_wait(t, me.pC)
                            me.bank_wait(t, me.pD)
                            b0 = me.pC[:, :].bitcast(BF16)
                            b1 = me.pD[:, :].bitcast(BF16)
                            for kk in range(3):
                                t.transpose(b0[:, kk * 128:kk * 128 + 128],
                                            me.attn[a0][:, kk * 128:kk * 128 + 128],
                                            me.identb[:, :])
                            for kk in range(3):
                                ins = t.transpose(b1[:, kk * 128:kk * 128 + 128],
                                                  me.attn[a1][:, kk * 128:kk * 128 + 128],
                                                  me.identb[:, :])
                            me.inc(ins, "pe")
                        self.T(_)
                        attn_war[pp] = self.cnt["pe"]

                        def _(v):
                            me.wv(v, "pe", me.cnt["pe"])
                            me.wv(v, "pe", ctxevac.get(0, 0))
                            v.tensor_copy(me.attnT[0][:, :], me.pC[:, :].bitcast(BF16)[:, 0:384])
                            ins = v.tensor_copy(me.attnT[1][:, :], me.pD[:, :].bitcast(BF16)[:, 0:384])
                            me.inc(ins, "dv")
                        self.V(_)
                        self.bank_set(me.pC, "dv")
                        self.bank_set(me.pD, "dv")

                        def _(t, qt0=qt0, qt1=qt1):
                            me.wv(t, "dv", me.cnt["dv"])
                            me.bank_wait(t, me.pBc1)
                            me.bank_wait(t, me.pBc2)
                            for kk in range(3):
                                t.matmul(me.pBc1[:, 0:128],
                                         me.Abf(28160 + (qt0 + kk) * 128, 28160 + (qt0 + kk) * 128 + 128),
                                         me.attnT[0][:, kk * 128:kk * 128 + 128],
                                         start=(kk == 0), stop=(kk == 2))
                            for kk in range(3):
                                ins = t.matmul(me.pBc2[:, 0:128],
                                               me.Abf(28160 + (qt1 + kk) * 128, 28160 + (qt1 + kk) * 128 + 128),
                                               me.attnT[1][:, kk * 128:kk * 128 + 128],
                                               start=(kk == 0), stop=(kk == 2))
                            me.inc(ins, "pe")
                        self.T(_)
                        ctxevac[0] = self.cnt["pe"]

                        def _(sc, h=h, qt0=qt0, qt1=qt1):
                            me.wv(sc, "pe", me.cnt["pe"])
                            sc.copy(me.CTX[:, h * TOK + qt0 * 128:h * TOK + qt0 * 128 + 128],
                                    me.pBc1[:, 0:128])
                            ins = sc.copy(me.CTX[:, h * TOK + qt1 * 128:h * TOK + qt1 * 128 + 128],
                                          me.pBc2[:, 0:128])
                            me.inc(ins, "ac")
                        self.S(_)
                        self.bank_set(me.pBc1, "ac")
                        self.bank_set(me.pBc2, "ac")

                    for qp in range(5):
                        if qp < 4:
                            stage_front(qp)
                        if qp >= 1:
                            stage_back(qp - 1)
                    ctx_done = self.cnt["pe"]

    def ph_u(self, mt):
        me = self
        if True:
            self.open_block()
            banks = [self.pA, self.pB]
            war = {0: 0, 1: 0}
            for ec in range(8):
                pi = ec % 2

                wsem = "dwA" if pi == 0 else "dwB"

                def _(g, ec=ec, pi=pi, wsem=wsem):
                    me.wv(g, "pe", war[pi])
                    me.dma(g, me.Abf(pi * 8192, pi * 8192 + 4096),
                           me.wkU[ec * 128:(ec + 1) * 128, :], wsem)
                self.G(_)
                ld = (wsem, self.cnt[wsem])
                for es in range(2):
                    for tp in range(2):
                        bk = banks[(es * 2 + tp) % 2]

                        def _(t, bk=bk, es=es, tp=tp, pi=pi, ld=ld):
                            me.wv(t, ld[0], ld[1])
                            me.bank_wait(t, bk)
                            for dk in range(DK):
                                ins = t.matmul(
                                    bk[:, 0:512],
                                    me.Abf(pi * 8192 + dk * 256 + es * 128,
                                           pi * 8192 + dk * 256 + es * 128 + 128),
                                    me.LNB[:, dk * WINT + HALO + tp * 512:
                                           dk * WINT + HALO + tp * 512 + 512],
                                    start=(dk == 0), stop=(dk == DK - 1))
                            me.inc(ins, "pe")
                        self.T(_)

                        def _(v, bk=bk, ec=ec, es=es, tp=tp):
                            me.wv(v, "pe", me.cnt["pe"])
                            ek = ec * 2 + es
                            ins = v.tensor_copy(me.UB[:, ek * TOK + tp * 512:ek * TOK + tp * 512 + 512],
                                                bk[:, 0:512])
                            me.inc(ins, "dv")
                        self.V(_)
                        self.bank_set(bk, "dv")
                war[pi] = self.cnt["pe"]

    def ph_olocal_gate(self, mt):
        me = self
        if True:
            self.open_block()
            banks = [self.pA, self.pB]
            war = {0: 0, 1: 0}
            gate_banks = {0: self.pS, 1: self.pBc1}
            for oc4 in range(8):
                pi = oc4 % 2

                wsem = "dwA" if pi == 0 else "dwB"

                def _(g, oc4=oc4, pi=pi, wsem=wsem):
                    me.wv(g, "pe", war[pi])
                    me.dma(g, me.LNB[:, pi * 4096:pi * 4096 + 4096],
                           me.woP[oc4 * 128:(oc4 + 1) * 128, :], wsem)
                self.G(_)
                ld = (wsem, self.cnt[wsem])
                for os_ in range(2):
                    oc = oc4 * 2 + os_
                    for tp in range(2):
                        bk = banks[(os_ * 2 + tp) % 2]

                        def _(t, bk=bk, os_=os_, tp=tp, pi=pi, ld=ld):
                            me.wv(t, ld[0], ld[1])
                            me.bank_wait(t, bk)
                            for dk in range(DK):
                                ins = t.matmul(
                                    bk[:, 0:512],
                                    me.LNB[:, pi * 4096 + dk * 256 + os_ * 128:
                                           pi * 4096 + dk * 256 + os_ * 128 + 128],
                                    me.CTX[:, dk * TOK + tp * 512:dk * TOK + tp * 512 + 512],
                                    start=(dk == 0), stop=(dk == DK - 1))
                            me.inc(ins, "pe")
                        self.T(_)

                        def _(v, bk=bk, oc=oc, tp=tp):
                            me.wv(v, "pe", me.cnt["pe"])
                            ins = v.tensor_copy(me.A32[:, oc * TOK + tp * 512:oc * TOK + tp * 512 + 512],
                                                bk[:, 0:512])
                            me.inc(ins, "dv")
                        self.V(_)
                        self.bank_set(bk, "dv")
                war[pi] = self.cnt["pe"]
            olocal_done = self.cnt["dv"]

            for tp in range(2):
                for oc in range(DK):
                    def _(t, oc=oc, tp=tp):
                        if oc == 0:
                            me.wv(t, "dv", olocal_done)
                            me.bank_wait(t, gate_banks[tp])
                        ins = t.matmul(gate_banks[tp][0:1, 0:512], me.gwTs[:, oc:oc + 1],
                                       me.A32[:, oc * TOK + tp * 512:oc * TOK + tp * 512 + 512],
                                       start=(oc == 0), stop=(oc == DK - 1))
                        me.inc(ins, "pe")
                    self.T(_)

            def _(sc):
                me.wv(sc, "pe", me.cnt["pe"])
                sc.activation(me.gtmp[0:1, 0:512], me.pS[0:1, 0:512], AF.Exp,
                              bias=me.neggb[0:1, 0:1], scale=-1.0)
                ins = sc.activation(me.gtmp[0:1, 512:1024], me.pBc1[0:1, 0:512], AF.Exp,
                                    bias=me.neggb[0:1, 0:1], scale=-1.0)
                me.inc(ins, "ac")
            self.S(_)
            self.bank_set(self.pS, "ac")
            self.bank_set(self.pBc1, "ac")

            def _(v):
                me.wv(v, "ac", me.cnt["ac"])
                v.tensor_scalar(me.grow[0:1, :], me.gtmp[0:1, :], 1.0, None, op0=ALU.add)
                ins = v.reciprocal(me.gtmp[0:1, :], me.grow[0:1, :])
                me.inc(ins, "dv")
                v.wait_ge(me.sem["dv"], me.cnt["dv"])
                ins = v.engine_nop()
                me.inc(ins, "dv")
            self.V(_)

            for tp in range(2):
                def _(t, tp=tp):
                    me.wv(t, "dv", me.cnt["dv"])
                    me.bank_wait(t, me.pBc2)
                    ins = t.matmul(me.pBc2[:, 0:512], me.onesrowf[:, :],
                                   me.gtmp[0:1, tp * 512:tp * 512 + 512],
                                   start=True, stop=True)
                    me.inc(ins, "pe")
                self.T(_)

                def _(v, tp=tp):
                    me.wv(v, "pe", me.cnt["pe"])
                    ins = v.tensor_copy(me.bypB[:, tp * 512:tp * 512 + 512], me.pBc2[:, 0:512])
                    me.inc(ins, "dv")
                self.V(_)
                self.bank_set(self.pBc2, "dv")

    def ph_sim_topk(self, mt):
        me = self
        if True:
            self.open_block()
            banks = [self.pA, self.pB]
            war = {0: 0, 1: 0}
            for mc in range(16):
                pi = mc % 2

                wsem = "dwA" if pi == 0 else "dwB"

                def _(g, mc=mc, pi=pi, wsem=wsem):
                    me.wv(g, "pe", war[pi])
                    me.dma(g, me.SS[:, pi * 4096:pi * 4096 + 4096],
                           me.mbP[mc * 128:(mc + 1) * 128, :], wsem)
                self.G(_)
                ld = (wsem, self.cnt[wsem])
                for tt in range(8):
                    bk = banks[tt % 2]

                    def _(t, bk=bk, tt=tt, pi=pi, ld=ld):
                        me.wv(t, ld[0], ld[1])
                        me.bank_wait(t, bk)
                        for ek in range(DK):
                            ins = t.matmul(
                                bk[:, 0:256],
                                me.UB[:, ek * TOK + tt * 128:ek * TOK + tt * 128 + 128],
                                me.SS[:, pi * 4096 + ek * 256:pi * 4096 + ek * 256 + 256],
                                start=(ek == 0), stop=(ek == DK - 1))
                        me.inc(ins, "pe")
                    self.T(_)

                    def _(v, bk=bk, mc=mc, tt=tt):
                        me.wv(v, "pe", me.cnt["pe"])
                        bi = bk[:, :].bitcast(I32)
                        ins = v.tensor_scalar(me.packed[:, :].bitcast(I32), bi[:, 0:256],
                                              -4096, mc << 8, op0=ALU.bitwise_and,
                                              op1=ALU.bitwise_or)
                        me.inc(ins, "dv")
                        v.tensor_tensor(me.packed[:, :].bitcast(I32),
                                        me.packed[:, :].bitcast(I32),
                                        me.colidx[:, :], op=ALU.bitwise_or)
                        ins = v.max(out=me.candA[:, tt * 128 + mc * 8:tt * 128 + mc * 8 + 8],
                                    in_=me.packed[:, :])
                        me.inc(ins, "dv")
                    self.V(_)
                    self.bank_set(bk, "dv")
                war[pi] = self.cnt["pe"]

            for tt in range(8):
                def _(v, tt=tt):
                    me.wv(v, "dv", me.cnt["dv"])
                    ins = v.max(out=me.run8[:, :], in_=me.candA[:, tt * 128:tt * 128 + 128])
                    me.inc(ins, "dv")
                    v.wait_ge(me.sem["dv"], me.cnt["dv"])
                    v.tensor_scalar(me.idxA[:, tt * 8:tt * 8 + 8], me.run8[:, :].bitcast(I32),
                                    4095, None, op0=ALU.bitwise_and)
                    ins = v.tensor_scalar(me.vals8[:, :].bitcast(I32), me.run8[:, :].bitcast(I32),
                                          -4096, None, op0=ALU.bitwise_and)
                    me.inc(ins, "dv")
                    v.wait_ge(me.sem["dv"], me.cnt["dv"])
                    ins = v.tensor_scalar(me.vals8[:, :], me.vals8[:, :], SQD, None, op0=ALU.mult)
                    me.inc(ins, "dv")
                    v.wait_ge(me.sem["dv"], me.cnt["dv"])
                    ins = v.reduce_max(me.nmax[0][:, :], me.vals8[:, :], axis=AX.X, negate=True)
                    me.inc(ins, "dv")
                self.V(_)

                def _(sc):
                    me.wv(sc, "dv", me.cnt["dv"])
                    ins = sc.activation(me.ev8[:, :], me.vals8[:, :], AF.Exp,
                                        bias=me.nmax[0][:, 0:1], scale=1.0,
                                        accum_out=me.esum[0][:, 0:1])
                    me.inc(ins, "ac")
                self.S(_)

                def _(v, tt=tt):
                    me.wv(v, "ac", me.cnt["ac"])
                    ins = v.reciprocal(me.ersum[0][:, :], me.esum[0][:, :])
                    me.inc(ins, "dv")
                    v.wait_ge(me.sem["dv"], me.cnt["dv"])
                    ins = v.tensor_scalar(me.w8A[:, tt * 8:tt * 8 + 8], me.ev8[:, :],
                                          me.ersum[0][:, 0:1], None, op0=ALU.mult)
                    me.inc(ins, "dv")
                self.V(_)

    def ph_mem(self, mt):
        me = self
        base = mt * TOK
        if True:
            self.open_block()
            tr_banks = [self.pC, self.pD]
            banks = [self.pA, self.pB]
            mac_done = {}
            ytm_war = {0: 0, 1: 0}
            for half in range(2):
                for j in range(4):
                    tt = half * 4 + j
                    yo = 2048 + (j % 2) * 2048
                    for k in range(8):
                        seq = tt * 8 + k

                        gsem = "dgA" if k % 2 == 0 else "dgB"

                        def _(g, tt=tt, k=k, seq=seq, gsem=gsem):
                            me.wv(g, "dv", mac_done.get(seq - 2, 0))
                            ins = g.indirect_dma_start(
                                out=me.UB[:, (k % 2) * 2048:(k % 2) * 2048 + 2048],
                                out_offset=None, in_=me.mb[:, :],
                                in_offset=bass.IndirectOffsetOnAxis(
                                    me.idxA[:, tt * 8 + k:tt * 8 + k + 1], axis=0))
                            me.inc(ins, gsem, 16)
                        self.G(_)

                        def _(v, tt=tt, k=k, yo=yo, j=j, gsem=gsem):
                            me.wv(v, gsem, me.cnt[gsem])
                            if k == 0:
                                me.wv(v, "ac", ytm_war[j % 2])
                            gbuf = me.UB[:, (k % 2) * 2048:(k % 2) * 2048 + 2048]
                            ytm = me.Uf32(yo, yo + 2048)
                            if k == 0:
                                ins = v.tensor_scalar(ytm, gbuf, me.w8A[:, tt * 8:tt * 8 + 1],
                                                      None, op0=ALU.mult)
                            else:
                                ins = v.scalar_tensor_tensor(
                                    ytm, gbuf, me.w8A[:, tt * 8 + k:tt * 8 + k + 1],
                                    ytm, op0=ALU.mult, op1=ALU.add)
                            me.inc(ins, "dv")
                        self.V(_)
                        mac_done[seq] = self.cnt["dv"]

                    for bq in range(4):
                        bk = tr_banks[bq % 2]

                        def _(t, bk=bk, bq=bq, yo=yo):
                            me.wv(t, "dv", me.cnt["dv"])
                            me.bank_wait(t, bk)
                            for i in range(4):
                                ins = t.transpose(
                                    bk[:, i * 128:i * 128 + 128],
                                    me.Uf32(yo + (bq * 4 + i) * 128, yo + (bq * 4 + i) * 128 + 128),
                                    me.identf[:, :])
                            me.inc(ins, "pe")
                        self.T(_)

                        def _(sc, bk=bk, bq=bq, half=half, j=j):
                            me.wv(sc, "pe", me.cnt["pe"])
                            for i in range(4):
                                dk = bq * 4 + i
                                ins = sc.copy(
                                    me.CTX[:, dk * TOK + half * 512 + j * 128:
                                           dk * TOK + half * 512 + j * 128 + 128],
                                    bk[:, i * 128:i * 128 + 128])
                            me.inc(ins, "ac")
                        self.S(_)
                        self.bank_set(bk, "ac")
                    ytm_war[j % 2] = self.cnt["ac"]
                ypre_done = self.cnt["ac"]

                sswar = self.cnt["pe"]
                zwar = {0: sswar, 1: sswar}
                for zc in range(8):
                    pi = zc % 2

                    wsem = "dwA" if pi == 0 else "dwB"

                    def _(g, zc=zc, pi=pi, wsem=wsem):
                        me.wv(g, "pe", zwar[pi])
                        me.dma(g, me.SS[:, pi * 4096:pi * 4096 + 4096],
                               me.wvP[zc * 128:(zc + 1) * 128, :], wsem)
                    self.G(_)
                    ld = (wsem, self.cnt[wsem])
                    for ic2 in range(2):
                        ic = zc * 2 + ic2
                        bk = banks[ic % 2]

                        def _(t, bk=bk, ic2=ic2, pi=pi, ld=ld, half=half):
                            me.wv(t, ld[0], ld[1])
                            me.wv(t, "ac", ypre_done)
                            me.bank_wait(t, bk)
                            for dk in range(DK):
                                ins = t.matmul(
                                    bk[:, 0:512],
                                    me.SS[:, pi * 4096 + dk * 256 + ic2 * 128:
                                           pi * 4096 + dk * 256 + ic2 * 128 + 128],
                                    me.CTX[:, dk * TOK + half * 512:dk * TOK + half * 512 + 512],
                                    start=(dk == 0), stop=(dk == DK - 1))
                            me.inc(ins, "pe")
                        self.T(_)

                        def _(v, bk=bk, ic=ic, half=half):
                            me.wv(v, "pe", me.cnt["pe"])
                            ins = v.tensor_copy(
                                me.LNB[:, ic * TOK + half * 512:ic * TOK + half * 512 + 512],
                                bk[:, 0:512])
                            me.inc(ins, "dv")
                        self.V(_)
                        self.bank_set(bk, "dv")
                    zwar[pi] = self.cnt["pe"]

                z_done = self.cnt["dv"]
                sswar2 = self.cnt["pe"]
                owar = {0: sswar2, 1: sswar2}
                xw0 = self.cnt["dv"]
                xres_war = {0: xw0, 1: xw0}
                for oc2 in range(8):
                    pi = oc2 % 2

                    wsem = "dwA" if pi == 0 else "dwB"

                    def _(g, oc2=oc2, pi=pi, wsem=wsem):
                        me.wv(g, "pe", owar[pi])
                        me.dma(g, me.SS[:, pi * 4096:pi * 4096 + 4096],
                               me.woP[oc2 * 128:(oc2 + 1) * 128, :], wsem)
                    self.G(_)
                    ld = (wsem, self.cnt[wsem])
                    for o2 in range(2):
                        oc = oc2 * 2 + o2
                        bk = banks[oc % 2]
                        xb = (oc % 2) * 512

                        xsem = "dxA" if oc % 2 == 0 else "dxB"

                        def _(q, oc=oc, xb=xb, half=half, xsem=xsem):
                            me.wv(q, "dv", xres_war[oc % 2])
                            src = me.xs.rearrange("(k p) t -> p k t", p=128)[
                                :, oc:oc + 1, HALO + base + half * 512:HALO + base + half * 512 + 512]
                            me.dma(q, me.xrest[:, xb:xb + 512].rearrange("p (a t) -> p a t", a=1),
                                   src, xsem)
                        self.Q(_)
                        xld = (xsem, self.cnt[xsem])

                        def _(t, bk=bk, o2=o2, pi=pi, ld=ld, half=half):
                            me.wv(t, ld[0], ld[1])
                            me.wv(t, "dv", z_done)
                            me.bank_wait(t, bk)
                            for ik in range(DK):
                                ins = t.matmul(
                                    bk[:, 0:512],
                                    me.SS[:, pi * 4096 + ik * 256 + o2 * 128:
                                           pi * 4096 + ik * 256 + o2 * 128 + 128],
                                    me.LNB[:, ik * TOK + half * 512:ik * TOK + half * 512 + 512],
                                    start=(ik == 0), stop=(ik == DK - 1))
                            me.inc(ins, "pe")
                        self.T(_)

                        def _(v, bk=bk, oc=oc, xb=xb, half=half, xld=xld):
                            me.wv(v, "pe", me.cnt["pe"])
                            me.wv(v, xld[0], xld[1])
                            a = me.A32[:, oc * TOK + half * 512:oc * TOK + half * 512 + 512]
                            v.tensor_mul(me.tmpf[:, :], bk[:, 0:512],
                                         me.bypB[:, half * 512:half * 512 + 512])
                            v.tensor_add(me.tmpf[:, :], me.tmpf[:, :], a)
                            ins = v.tensor_add(a, me.tmpf[:, :], me.xrest[:, xb:xb + 512])
                            me.inc(ins, "dv")
                        self.V(_)
                        self.bank_set(bk, "dv")
                        xres_war[oc % 2] = self.cnt["dv"]
                    owar[pi] = self.cnt["pe"]
            self.close_block()

    def ph_ffn(self, mt):
        me = self
        if True:
            self.open_block()
            prev_sp_pl = 0
            prev_sp_dv = 0
            dCD = [self.pC, self.pD]
            QS = 6144
            gupld = {}
            ju_done = {}
            dload = {}
            sp_arr = {}
            down_done = {}

            JSEM = ("dwA", "dwB", "dwC", "dwD")

            def load_gu(sg, j):
                js = JSEM[j]

                def _(g, sg=sg, j=j, js=js):
                    me.wv(g, "pe", ju_done.get((sg - 1, j), 0))
                    r0 = (sg * 4 + j) * 128
                    me.dma(g, me.CTX[:, j * 2048:j * 2048 + 2048],
                           me.wgP[r0:r0 + 128, :], js)
                    me.dma(g, me.CTX[:, 8192 + j * 2048:8192 + j * 2048 + 2048],
                           me.wuP[r0:r0 + 128, :], js)
                self.G(_)
                gupld[(sg, j)] = (js, self.cnt[js])

            def load_down(sg):
                def _(g, sg=sg):
                    me.wv(g, "pe", down_done.get(sg - 1, 0))
                    me.dma(g, me.SS[:, 0:8192], me.wdP[sg * 128:(sg + 1) * 128, :], "dwE")
                self.G(_)
                dload[sg] = self.cnt["dwE"]

            def hsp(par, c0, c1):
                if par == 0:
                    return me.UB[:, c0:c1]
                return me.LNB[:, 16384 + c0:16384 + c1]

            def emit_down(sg):
                par = sg % 2
                for oc in range(16):
                    for tp in range(2):
                        bk = dCD[(oc * 2 + tp) % 2]

                        def _(t, bk=bk, oc=oc, tp=tp, sg=sg, par=par):
                            me.wv(t, "dwE", dload[sg])
                            me.wv(t, "dv", sp_arr[sg])
                            me.bank_wait(t, bk)
                            for ic in range(4):
                                ins = t.matmul(
                                    bk[:, 0:512],
                                    me.SS[:, ic * 2048 + oc * 128:ic * 2048 + oc * 128 + 128],
                                    hsp(par, ic * 1024 + tp * 512, ic * 1024 + tp * 512 + 512),
                                    start=(ic == 0), stop=(ic == 3))
                            me.inc(ins, "pe")
                        self.T(_)

                        def _(v, bk=bk, oc=oc, tp=tp):
                            me.wv(v, "pe", me.cnt["pe"])
                            a = me.A32[:, oc * TOK + tp * 512:oc * TOK + tp * 512 + 512]
                            ins = v.tensor_add(a, a, bk[:, 0:512])
                            me.inc(ins, "dv")
                        self.V(_)
                        self.bank_set(bk, "dv")
                down_done[sg] = self.cnt["pe"]

            for j in range(4):
                load_gu(0, j)
            load_down(0)

            for sg in range(16):
                gbanks = [self.pA, self.pE]
                ubanks = [self.pB, self.pBc1]
                stmp = [QS, QS + 1024]
                stmp_war = {0: prev_sp_dv, 1: prev_sp_dv}
                for j in range(4):
                    for tp in range(2):
                        par = (j * 2 + tp) % 2
                        gbk = gbanks[par]
                        ubk = ubanks[par]

                        def _(t, j=j, tp=tp, gbk=gbk, sg=sg):
                            me.wv(t, gupld[(sg, j)][0], gupld[(sg, j)][1])
                            me.bank_wait(t, gbk)
                            for dk in range(DK):
                                ins = t.matmul(
                                    gbk[:, 0:512],
                                    me.CTX[:, j * 2048 + dk * 128:j * 2048 + dk * 128 + 128],
                                    me.LNB[:, dk * TOK + tp * 512:dk * TOK + tp * 512 + 512],
                                    start=(dk == 0), stop=(dk == DK - 1))
                            me.inc(ins, "pe")
                        self.T(_)
                        gate_pe = self.cnt["pe"]

                        def _(t, j=j, tp=tp, ubk=ubk, sg=sg):
                            me.bank_wait(t, ubk)
                            for dk in range(DK):
                                ins = t.matmul(
                                    ubk[:, 0:512],
                                    me.CTX[:, 8192 + j * 2048 + dk * 128:8192 + j * 2048 + dk * 128 + 128],
                                    me.LNB[:, dk * TOK + tp * 512:dk * TOK + tp * 512 + 512],
                                    start=(dk == 0), stop=(dk == DK - 1))
                            me.inc(ins, "pe")
                        self.T(_)
                        if tp == 1:
                            ju_done[(sg, j)] = self.cnt["pe"]
                            if sg < 15:
                                load_gu(sg + 1, j)

                        def _(sc, j=j, tp=tp, gbk=gbk, par=par, gate_pe=gate_pe):
                            me.wv(sc, "pe", gate_pe)
                            me.wv(sc, "dv", stmp_war[par])
                            if j == 0 and tp == 0:
                                me.wv(sc, "pl", prev_sp_pl)
                            ins = sc.activation(
                                me.Uf32(stmp[par], stmp[par] + 1024)[:, tp * 512:tp * 512 + 512],
                                gbk[:, 0:512], AF.Silu)
                            me.inc(ins, "ac")
                        self.S(_)
                        self.bank_set(gbk, "ac")

                        def _(v, j=j, tp=tp, ubk=ubk, par=par):
                            me.wv(v, "pe", me.cnt["pe"])
                            me.wv(v, "ac", me.cnt["ac"])
                            if j == 0 and tp == 0:
                                me.wv(v, "pl", prev_sp_pl)
                            ins = v.tensor_mul(
                                me.Uf32(2048 + j * 1024, 2048 + j * 1024 + 1024)[:, tp * 512:tp * 512 + 512],
                                me.Uf32(stmp[par], stmp[par] + 1024)[:, tp * 512:tp * 512 + 512],
                                ubk[:, 0:512])
                            me.inc(ins, "dv")
                        self.V(_)
                        self.bank_set(ubk, "dv")
                        stmp_war[par] = self.cnt["dv"]
                p_done = self.cnt["dv"]

                if sg >= 1:
                    emit_down(sg - 1)
                    if sg < 15:
                        load_down(sg)

                def pq(j, q4):
                    return me.Uf32(2048 + j * 1024 + q4 * 256, 2048 + j * 1024 + q4 * 256 + 256)

                spar = sg % 2
                for q4 in range(4):
                    sj = [me.Uf32(QS + 256 * j, QS + 256 * (j + 1)) for j in range(4)]
                    sA = me.Uf32(QS + 1024, QS + 1280)
                    sB = me.Uf32(QS + 1280, QS + 1536)
                    sC = me.Uf32(QS + 1536, QS + 1792)
                    sD = me.Uf32(QS + 1792, QS + 2048)

                    def _(g, q4=q4, sj=sj):
                        me.wv(g, "dv", p_done if q4 == 0 else me.cnt["dv"])
                        for j in range(4):
                            ins = g.tensor_mul(sj[j], pq(j, q4), pq(j, q4))
                        me.inc(ins, "pl")
                    self.G(_)

                    def _(v, sj=sj, sA=sA, sB=sB, sC=sC, sD=sD):
                        me.wv(v, "pl", me.cnt["pl"])
                        v.tensor_tensor(sA, sj[0], sj[1], op=ALU.max)
                        v.tensor_tensor(sB, sj[0], sj[1], op=ALU.min)
                        v.tensor_tensor(sC, sj[2], sj[3], op=ALU.max)
                        v.tensor_tensor(sD, sj[2], sj[3], op=ALU.min)
                        v.tensor_tensor(sA, sA, sC, op=ALU.min)
                        v.tensor_tensor(sB, sB, sD, op=ALU.max)
                        ins = v.tensor_tensor(sA, sA, sB, op=ALU.max)
                        me.inc(ins, "dv")
                    self.V(_)

                    def _(g, sj=sj, sA=sA):
                        me.wv(g, "dv", me.cnt["dv"])
                        for j in range(4):
                            ins = g.tensor_sub(sj[j], sj[j], sA)
                        me.inc(ins, "pl")
                    self.G(_)

                    def _(v, q4=q4, sj=sj, sg=sg, spar=spar):
                        me.wv(v, "pl", me.cnt["pl"])
                        if q4 == 0:
                            me.wv(v, "pe", down_done.get(sg - 2, 0))
                        for j in range(4):
                            ins = v.scalar_tensor_tensor(
                                hsp(spar, j * 1024 + q4 * 256, j * 1024 + q4 * 256 + 256),
                                sj[j], 0.0, pq(j, q4), op0=ALU.is_ge, op1=ALU.mult)
                        me.inc(ins, "dv")
                    self.V(_)
                sp_arr[sg] = self.cnt["dv"]
                prev_sp_pl = self.cnt["pl"]
                prev_sp_dv = sp_arr[sg]
            load_down(15)
            emit_down(15)

    def ph_store(self, mt):
        me = self
        if True:
            self.open_block()

            def _(q):
                me.dma(q, me.out.rearrange("(k p) t -> p k t", p=128)[:, :, mt * TOK:mt * TOK + TOK],
                       me.wdst(me.A32[:, :], TOK), "do")
            self.Q(_)
            self.store_done = self.cnt["do"]

    def ph_final(self):
        me = self
        if True:
            self.open_block()

            def _(q):
                q.wait_ge(me.sem["do"], me.cnt["do"])
            self.Q(_)

    def build(self):
        self.marks = []

        def mark(name):
            self.marks.append((name, self.nc.n_instructions()))
        self.mark = mark
        mark("setup")
        self.ph_setup()
        for mt in (0, 1):
            u = self.upto
            self.mark(f"ln1.{mt}")
            self.ph_ln(mt, "x")
            self.tap(f"ln1h{mt}", [(self.LNB[:, :], 20480)])
            if u >= 1:
                self.mark(f"heads.{mt}")
                self.ph_heads(mt)
                self.tap(f"ctx{mt}", [(self.CTX[:, :], 16384)])
            if u >= 2:
                self.mark(f"u.{mt}")
                self.ph_u(mt)
                self.tap(f"u{mt}", [(self.UB[:, :], 16384)])
            if u >= 3:
                self.mark(f"olg.{mt}")
                self.ph_olocal_gate(mt)
                self.tap(f"ol{mt}", [(self.A32[:, :], 16384)])
                self.tap(f"byp{mt}", [(self.bypB[:, :], 1024)])
            if u >= 4:
                self.mark(f"sim.{mt}")
                self.ph_sim_topk(mt)
                self.tap(f"idx{mt}", [(self.idxA[:, :], 64), (self.w8A[:, :], 64)])
            if u >= 5:
                self.mark(f"mem.{mt}")
                self.ph_mem(mt)
                self.tap(f"yp{mt}", [(self.CTX[:, :], 16384)])
                self.tap(f"z{mt}", [(self.LNB[:, 0:16384], 16384)])
                self.tap(f"xp{mt}", [(self.A32[:, :], 16384)])
            if u >= 6:
                self.mark(f"ln2.{mt}")
                self.ph_ln(mt, "acc")
                self.tap(f"ln2_{mt}", [(self.LNB[:, 0:16384], 16384)])
            if u >= 7:
                self.mark(f"ffn.{mt}")
                self.ph_ffn(mt)
                self.tap(f"ff{mt}", [(self.A32[:, :], 16384)])
            self.mark(f"store.{mt}")
            self.ph_store(mt)
        self.mark("final")
        self.ph_final()
        self.close_block()
        self.nc.finalize()
        return self.nc


_CACHE = {}


def _build_masks():
    r = np.arange(128)[:, None]
    c = np.arange(384)[None, :]
    band = (c >= r + 1) & (c <= r + 256)
    mg = np.where(band, 0.0, -1.0e30).astype(np.float32)
    m0 = np.where(band & (c >= 256), 0.0, -1.0e30).astype(np.float32)
    m1 = np.where(band & (c >= 128), 0.0, -1.0e30).astype(np.float32)
    return mg, m0, m1


def _chunked(wT, ncol):
    Dr, X = wT.shape
    nch = X // ncol
    return np.ascontiguousarray(
        wT.reshape(16, 128, nch, ncol).transpose(2, 1, 0, 3).reshape(nch * 128, 16 * ncol))


def _wd_chunked(wd):
    return np.ascontiguousarray(
        wd.reshape(16, 4, 128, 2048).transpose(0, 2, 1, 3).reshape(2048, 8192))


def _ffn_perm():
    perm = np.empty(I2, dtype=np.int64)
    idx = 0
    for sg in range(16):
        for j in range(4):
            for gw in range(128):
                perm[idx] = 4 * (sg * 128 + gw) + j
                idx += 1
    return perm


def kernel(L0_attn_hidden_states, L0_attn_wq, L0_attn_wk, L0_attn_wv, L0_attn_wo,
           L0_attn_gate_w, L0_attn_gate_b, L0_attn_mem_bank,
           L0_ffn_w_gate, L0_ffn_w_up, L0_ffn_w_down):
    x = np.asarray(L0_attn_hidden_states, dtype=np.float32)
    f32c = lambda a: np.ascontiguousarray(np.asarray(a, dtype=np.float32))
    wq, wk, wv, wo = map(f32c, (L0_attn_wq, L0_attn_wk, L0_attn_wv, L0_attn_wo))
    gate_w = f32c(L0_attn_gate_w)
    gate_b = f32c(L0_attn_gate_b).reshape(1, 1)
    mbank = f32c(L0_attn_mem_bank)
    wg_, wu_, wd_ = map(f32c, (L0_ffn_w_gate, L0_ffn_w_up, L0_ffn_w_down))

    perm = _ffn_perm()
    shared = {
        "wqP": _chunked(wq.T, 256), "wkP": _chunked(wk.T, 256),
        "wvP": _chunked(wv.T, 256), "woP": _chunked(wo.T, 256),
        "wkU": _chunked(wk, 256), "mbP": _chunked(mbank.T, 256),
        "mb": mbank,
        "gwT": np.ascontiguousarray(gate_w.reshape(16, 128).T),
        "gb": gate_b,
        "wgP": _chunked(wg_[:, perm], 128),
        "wuP": _chunked(wu_[:, perm], 128),
        "wdP": _wd_chunked(wd_[perm, :]),
    }
    mg, m0, m1 = _build_masks()
    masks_h0 = np.ascontiguousarray(np.concatenate([mg, m0, m1], axis=1))
    masks_h1 = np.ascontiguousarray(np.concatenate([mg, mg, mg], axis=1))

    in_maps = []
    for c in range(8):
        b, half = c // 2, c % 2
        xT = x[b].T
        if half == 0:
            xsl = np.concatenate([np.zeros((D, HALO), np.float32), xT[:, 0:2048]], axis=1)
        else:
            xsl = xT[:, 2048 - HALO:4096]
        m = dict(shared)
        m["xs"] = np.ascontiguousarray(xsl)
        m["masks"] = masks_h0 if half == 0 else masks_h1
        in_maps.append(m)

    if "nc" not in _CACHE:
        kb = KB()
        _CACHE["nc"] = kb.build()
    nc = _CACHE["nc"]
    res = run_bass_kernel_spmd(nc, in_maps, core_ids=list(range(8)))

    out = np.empty((4, 4096, D), dtype=np.float32)
    for c in range(8):
        b, half = c // 2, c % 2
        out[b, half * 2048:(half + 1) * 2048, :] = res.results[c]["out"].T
    return out


# version 08
# speedup vs baseline: 1.0507x; 1.0137x over previous
import math
import os
import numpy as np
from contextlib import ExitStack

import concourse.bass as bass
import concourse.mybir as mybir
from concourse.bass_utils import run_bass_kernel_spmd

F32 = mybir.dt.float32
BF16 = mybir.dt.bfloat16
I32 = mybir.dt.int32
AF = mybir.ActivationFunctionType
ALU = mybir.AluOpType
AX = mybir.AxisListType

D = 2048
DK = 16
TOK = 1024
HALO = 256
WINT = TOK + HALO
MEM = 4096
I2 = 8192
U_MODE = 8
HEADS_MODE = 5
SQD = 1.0 / math.sqrt(D)
SQH = 1.0 / math.sqrt(128.0)
EPS = 1e-5


class KB:
    def __init__(self, debug_taps=(), upto=99):
        self.taps = tuple(debug_taps)
        self.upto = upto
        nc = bass.Bass("TRN2", target_bir_lowering=False, dynamic_dma_scratch_size=16384,
                       detect_race_conditions=False)
        self.nc = nc
        self.ctx = ExitStack()
        e = self.ctx.enter_context

        dt = nc.dram_tensor
        self.xs = dt("xs", [D, HALO + 2048], F32, kind="ExternalInput").ap()
        self.wqP = dt("wqP", [1024, 4096], F32, kind="ExternalInput").ap()
        self.wkP = dt("wkP", [1024, 4096], F32, kind="ExternalInput").ap()
        self.wvP = dt("wvP", [1024, 4096], F32, kind="ExternalInput").ap()
        self.woP = dt("woP", [1024, 4096], F32, kind="ExternalInput").ap()
        self.wkU = dt("wkU", [1024, 4096], F32, kind="ExternalInput").ap()
        self.mbP = dt("mbP", [2048, 4096], F32, kind="ExternalInput").ap()
        self.mb = dt("mb", [MEM, D], F32, kind="ExternalInput").ap()
        self.gwT = dt("gwT", [128, 16], F32, kind="ExternalInput").ap()
        self.gb = dt("gb", [1, 1], F32, kind="ExternalInput").ap()
        self.wgP = dt("wgP", [8192, 2048], F32, kind="ExternalInput").ap()
        self.wuP = dt("wuP", [8192, 2048], F32, kind="ExternalInput").ap()
        self.wdP = dt("wdP", [2048, 8192], F32, kind="ExternalInput").ap()
        self.masks_d = dt("masks", [128, 3 * 384], F32, kind="ExternalInput").ap()
        self.out = dt("out", [D, 2048], F32, kind="ExternalOutput").ap()
        self.dbg = {}
        for t in self.taps:
            self.dbg[t] = dt("dbg_" + t, [128, 20480], F32, kind="ExternalOutput").ap()

        self.A32 = e(nc.sbuf_tensor("A32", [128, 16384], F32))
        self.LNB = e(nc.sbuf_tensor("LNB", [128, 20480], BF16))
        self.CTX = e(nc.sbuf_tensor("CTXB", [128, 16384], BF16))
        self.UB = e(nc.sbuf_tensor("UBB", [128, 16384], BF16))
        self.SS = e(nc.sbuf_tensor("SSB", [128, 8192], BF16))

        self.masks = e(nc.sbuf_tensor("masks_s", [128, 3 * 384], F32))
        self.colidx = e(nc.sbuf_tensor("colidx", [128, 256], I32))
        self.identb = e(nc.sbuf_tensor("identb", [128, 128], BF16))
        self.identf = e(nc.sbuf_tensor("identf", [128, 128], F32))
        self.onescol = e(nc.sbuf_tensor("onescol", [128, 1], BF16))
        self.onesrow = e(nc.sbuf_tensor("onesrow", [1, 128], BF16))
        self.onesrowf = e(nc.sbuf_tensor("onesrowf", [1, 128], F32))
        ubf = self.UB[:, :].bitcast(F32)
        self.ms = [ubf[:, 2048 + 384 * i:2048 + 384 * (i + 1)] for i in range(2)]
        self.probs = [ubf[:, 2816 + 384 * i:2816 + 384 * (i + 1)] for i in range(2)]
        self.attn = [self.UB[:, 7168 + 384 * i:7168 + 384 * (i + 1)] for i in range(4)]
        self.attnT = [self.UB[:, 8704 + 384 * i:8704 + 384 * (i + 1)] for i in range(2)]
        self.nmax = [e(nc.sbuf_tensor(f"nmax{i}", [128, 1], F32)) for i in range(2)]
        self.esum = [e(nc.sbuf_tensor(f"esum{i}", [128, 1], F32)) for i in range(2)]
        self.ersum = [e(nc.sbuf_tensor(f"ersum{i}", [128, 1], F32)) for i in range(2)]
        self.packed = e(nc.sbuf_tensor("packed", [128, 256], F32))
        self.candA = e(nc.sbuf_tensor("candA", [128, 8 * 128], F32))
        self.run8 = e(nc.sbuf_tensor("run8", [128, 8], F32))
        self.vals8 = e(nc.sbuf_tensor("vals8", [128, 8], F32))
        self.ev8 = e(nc.sbuf_tensor("ev8", [128, 8], F32))
        self.idxA = e(nc.sbuf_tensor("idxA", [128, 64], I32))
        self.w8A = e(nc.sbuf_tensor("w8A", [128, 64], F32))
        self.xbs = [e(nc.sbuf_tensor(f"xbs{i}", [128, 512], BF16)) for i in range(2)]
        self.sqs = [e(nc.sbuf_tensor(f"sqs{i}", [128, 512], BF16)) for i in range(2)]
        self.nmr = ubf[0:1, 0:512]
        self.vrr = ubf[0:1, 512:1024]
        self.m2r = ubf[0:1, 1024:1536]
        self.rsr = ubf[0:1, 1536:2048]
        self.nmb = self.UB[0:1, 4096:4608]
        self.rsb = self.UB[0:1, 4608:5120]
        self.tmpf = e(nc.sbuf_tensor("tmpf", [128, 512], F32))
        ssf = self.SS[:, :].bitcast(F32)
        self.grow = ssf[0:1, 0:1024]
        self.gtmp = ssf[0:1, 1024:2048]
        self.neggb = e(nc.sbuf_tensor("neggb", [1, 1], F32))
        self.bypB = e(nc.sbuf_tensor("bypB", [128, 1024], F32))
        self.gwTs = e(nc.sbuf_tensor("gwTs", [128, 16], F32))
        self.xrest = self.candA[:, :]
        self.gbs = e(nc.sbuf_tensor("gbs", [1, 1], F32))

        self.pA = e(nc.psum_tensor("pA", [128, 512], F32))
        self.pB = e(nc.psum_tensor("pB", [128, 512], F32))
        self.pC = e(nc.psum_tensor("pC", [128, 512], F32))
        self.pD = e(nc.psum_tensor("pD", [128, 512], F32))
        self.pE = e(nc.psum_tensor("pE", [128, 512], F32))
        self.pS = e(nc.psum_tensor("pS", [1, 512], F32))
        self.pBc1 = e(nc.psum_tensor("pBc1", [128, 512], F32))
        self.pBc2 = e(nc.psum_tensor("pBc2", [128, 512], F32))

        self.sem = {}
        self.cnt = {}
        for s in ("dw", "dx", "dg", "do", "pe", "dv", "ac", "pl",
                  "dwA", "dwB", "dwC", "dwD", "dwE", "dgA", "dgB", "dxA", "dxB"):
            self.sem[s] = e(nc.semaphore(s))
            self.cnt[s] = 0
        self.bankc = {}
        self.store_done = 0
        assert nc.sbuf_bytes_remaining >= 0, nc.sbuf_bytes_remaining

    def open_block(self):
        if getattr(self, "blk", None) is not None:
            self.close_block()
        self.blk = self.nc.Block().__enter__()
        self.blk.vector(lambda e: e.nop())
        self.blk.scalar(lambda e: e.nop())
        self.blk.tensor(lambda e: e.nop())
        self.blk.gpsimd(lambda e: e.nop())
        self.blk.sync(lambda e: e.nop())

    def close_block(self):
        if self.blk is not None:
            self.blk.__exit__(None, None, None)
            self.blk = None

    def V(self, f):
        self.blk.vector(f)

    def S(self, f):
        self.blk.scalar(f)

    def T(self, f):
        self.blk.tensor(f)

    def G(self, f):
        self.blk.gpsimd(f)

    def Q(self, f):
        self.blk.sync(f)

    def inc(self, ins, s, n=1):
        ins.then_inc(self.sem[s], n)
        self.cnt[s] += n
        return self.cnt[s]

    def wv(self, eng, s, tgt):
        if tgt > 0:
            eng.wait_ge(self.sem[s], tgt)

    def bank_wait(self, eng, bank):
        c = self.bankc.get(bank.name)
        if c:
            self.wv(eng, c[0], c[1])

    def bank_set(self, bank, s):
        self.bankc[bank.name] = (s, self.cnt[s])

    def Abf(self, c0, c1):
        return self.A32[:, :].bitcast(BF16)[:, c0:c1]

    def Uf32(self, c0, c1):
        return self.UB[:, :].bitcast(F32)[:, c0:c1]

    def wsrc(self, wdram, c0, cols, k0=0, nk=DK):
        return wdram.rearrange("(k p) o -> p k o", p=128)[:, k0:k0 + nk, c0:c0 + cols]

    def wdst(self, tile_ap, cols):
        return tile_ap.rearrange("p (k f) -> p k f", f=cols)

    def dma(self, q, dst, src, s):
        return self.inc(q.dma_start(out=dst, in_=src), s, 16)

    def tap(self, name, srcs):
        if name not in self.taps:
            return
        me = self
        dst = self.dbg[name]
        if True:
            self.open_block()

            def _(g):
                col = 0
                for ap, ncols in srcs:
                    me.dma(g, dst[:, col:col + ncols], ap, "dg")
                    col += ncols
                g.wait_ge(me.sem["dg"], me.cnt["dg"])
            self.G(_)

    def ph_setup(self):
        me = self
        if True:
            self.open_block()

            def _(q):
                me.dma(q, me.masks[:, :], me.masks_d[:, :], "dxA")
                me.dma(q, me.gwTs[:, :], me.gwT[:, :], "dxA")
                me.dma(q, me.gbs[:, :], me.gb[:, :], "dxA")
            self.Q(_)

            def _(g):
                g.iota(me.colidx[:, :], [[1, 256]], base=0, channel_multiplier=0)
                g.memset(me.identb[:, :], 1.0)
                g.affine_select(me.identb[:, :], me.identb[:, :], [[1, 128]],
                                ALU.is_equal, 0.0, base=0, channel_multiplier=-1)
                g.memset(me.identf[:, :], 1.0)
                g.affine_select(me.identf[:, :], me.identf[:, :], [[1, 128]],
                                ALU.is_equal, 0.0, base=0, channel_multiplier=-1)
                g.memset(me.onescol[:, :], 1.0)
                g.memset(me.onesrow[:, :], 1.0)
                ins = g.memset(me.onesrowf[:, :], 1.0)
                me.inc(ins, "pl")
            self.G(_)

            def _(v):
                me.wv(v, "dxA", me.cnt["dxA"])
                me.wv(v, "pl", me.cnt["pl"])
                ins = v.tensor_scalar(me.neggb[:, :], me.gbs[:, :], -1.0, None, op0=ALU.mult)
                me.inc(ins, "dv")
            self.V(_)

    def ph_ln(self, mt, src):
        me = self
        base = mt * TOK
        slabs = [(0, 512), (512, 512), (1024, 256)] if src == "x" else [(0, 512), (512, 512)]
        ow = WINT if src == "x" else TOK
        if True:
            self.open_block()
            apply_done = {}
            bc_prev = 0
            for si, (s0, sw) in enumerate(slabs):
                pi = si % 2
                lsem = "dxA" if pi == 0 else "dxB"
                if src == "x":
                    def _(q, si=si, s0=s0, sw=sw, pi=pi, lsem=lsem):
                        if si >= 2:
                            me.wv(q, "dv", apply_done[si - 2])
                        if mt == 1:
                            me.wv(q, "do", me.store_done)
                        dst = me.A32[:, pi * 8192:pi * 8192 + 8192]
                        me.dma(q, me.wdst(dst, 512)[:, :, 0:sw],
                               me.xs.rearrange("(k p) t -> p k t", p=128)
                               [:, :, base + s0:base + s0 + sw], lsem)
                    self.Q(_)
                ld_tgt = self.cnt[lsem]

                def chunk(dk):
                    if src == "x":
                        return me.A32[:, pi * 8192 + dk * 512:pi * 8192 + dk * 512 + sw]
                    return me.A32[:, dk * TOK + s0:dk * TOK + s0 + sw]

                pe_at_slab_start = self.cnt["pe"]
                stats_tgt = {}
                sbk1 = me.pS if si % 2 == 0 else me.pA
                sbk2 = me.pBc1 if si % 2 == 0 else me.pB
                for dk in range(DK):
                    par = dk % 2

                    def _(sc, dk=dk, sw=sw, par=par, lsem=lsem):
                        if dk == 0 and src == "x":
                            me.wv(sc, lsem, ld_tgt)
                        me.wv(sc, "pe", stats_tgt[dk - 2] if dk > 1 else pe_at_slab_start)
                        ch = chunk(dk)
                        sc.copy(me.xbs[par][:, 0:sw], ch)
                        ins = sc.square(me.sqs[par][:, 0:sw], ch)
                        me.inc(ins, "ac")
                    self.S(_)

                    def _(t, dk=dk, sw=sw, par=par, sbk1=sbk1, sbk2=sbk2):
                        me.wv(t, "ac", me.cnt["ac"])
                        if dk == 0:
                            me.bank_wait(t, sbk1)
                            me.bank_wait(t, sbk2)
                        t.matmul(sbk1[0:1, 0:sw], me.onescol[:, :], me.xbs[par][:, 0:sw],
                                 start=(dk == 0), stop=(dk == DK - 1))
                        ins = t.matmul(sbk2[0:1, 0:sw], me.onescol[:, :], me.sqs[par][:, 0:sw],
                                       start=(dk == 0), stop=(dk == DK - 1))
                        me.inc(ins, "pe")
                        stats_tgt[dk] = me.cnt["pe"]
                    self.T(_)

                def _(v, sw=sw, sbk1=sbk1, sbk2=sbk2):
                    me.wv(v, "pe", me.cnt["pe"])
                    v.tensor_scalar(me.nmr[0:1, 0:sw], sbk1[0:1, 0:sw], -1.0 / D, None, op0=ALU.mult)
                    v.tensor_scalar(me.vrr[0:1, 0:sw], sbk2[0:1, 0:sw], 1.0 / D, None, op0=ALU.mult)
                    v.tensor_mul(me.m2r[0:1, 0:sw], me.nmr[0:1, 0:sw], me.nmr[0:1, 0:sw])
                    v.tensor_sub(me.vrr[0:1, 0:sw], me.vrr[0:1, 0:sw], me.m2r[0:1, 0:sw])
                    ins = v.tensor_scalar(me.vrr[0:1, 0:sw], me.vrr[0:1, 0:sw], EPS, None,
                                          op0=ALU.add)
                    me.inc(ins, "dv")
                self.V(_)
                self.bank_set(sbk1, "dv")

                def _(sc, sw=sw):
                    me.wv(sc, "dv", me.cnt["dv"])
                    ins = sc.activation(me.rsr[0:1, 0:sw], me.vrr[0:1, 0:sw], AF.Sqrt)
                    me.inc(ins, "ac")
                self.S(_)

                def _(v, sw=sw):
                    me.wv(v, "ac", me.cnt["ac"])
                    me.wv(v, "pe", bc_prev)
                    ins = v.reciprocal(me.vrr[0:1, 0:sw], me.rsr[0:1, 0:sw])
                    me.inc(ins, "dv")
                    v.wait_ge(me.sem["dv"], me.cnt["dv"])
                    v.tensor_copy(me.nmb[0:1, 0:sw], me.nmr[0:1, 0:sw])
                    ins = v.tensor_copy(me.rsb[0:1, 0:sw], me.vrr[0:1, 0:sw])
                    me.inc(ins, "dv")
                self.V(_)
                self.bank_set(sbk2, "dv")

                bbk1 = me.pBc2 if si % 2 == 0 else me.pC
                bbk2 = me.pD if si % 2 == 0 else me.pE

                def _(t, sw=sw, bbk1=bbk1, bbk2=bbk2):
                    me.wv(t, "dv", me.cnt["dv"])
                    me.bank_wait(t, bbk1)
                    me.bank_wait(t, bbk2)
                    t.matmul(bbk1[:, 0:sw], me.onesrow[:, :], me.nmb[0:1, 0:sw],
                             start=True, stop=True)
                    ins = t.matmul(bbk2[:, 0:sw], me.onesrow[:, :], me.rsb[0:1, 0:sw],
                                   start=True, stop=True)
                    me.inc(ins, "pe")
                self.T(_)
                bc_tgt = self.cnt["pe"]
                bc_prev = bc_tgt

                for dk in range(DK):
                    def _(v, dk=dk, s0=s0, sw=sw, bbk1=bbk1, bbk2=bbk2):
                        if dk == 0:
                            me.wv(v, "pe", bc_tgt)
                        ch = chunk(dk)
                        v.tensor_add(me.tmpf[:, 0:sw], ch, bbk1[:, 0:sw])
                        ins = v.tensor_mul(me.LNB[:, dk * ow + s0:dk * ow + s0 + sw],
                                           me.tmpf[:, 0:sw], bbk2[:, 0:sw])
                        me.inc(ins, "dv")
                    self.V(_)
                apply_done[si] = self.cnt["dv"]
                self.bank_set(bbk1, "dv")
                self.bank_set(bbk2, "dv")

    def ph_heads(self, mt):
        me = self
        if True:
            self.open_block()
            wld = {}
            war = {0: 0, 1: 0}
            proj_banks = [self.pA, self.pB, self.pE]
            tr_banks = [self.pC, self.pD]
            ctx_banks = [self.pBc1, self.pBc2]
            rot = {"p": 0}
            ctx_done = 0
            tr_done = {}
            exp_done = {}
            attn_done = {}
            ctxevac = {}

            for hp in range(8):
                pi = hp % 2
                woff = pi * 12288

                wsem = "dwA" if pi == 0 else "dwB"

                def _(g, hp=hp, pi=pi, woff=woff, wsem=wsem):
                    me.wv(g, "pe", war[pi])
                    if hp == 0:
                        me.wv(g, "do", me.store_done)
                    for w, wo in ((me.wqP, 0), (me.wkP, 4096), (me.wvP, 8192)):
                        me.dma(g, me.Abf(woff + wo, woff + wo + 4096),
                               w[hp * 128:(hp + 1) * 128, :], wsem)
                self.G(_)
                wld[hp] = (wsem, self.cnt[wsem])

                for h2 in range(2):
                    h = hp * 2 + h2
                    if HEADS_MODE < 2:
                        continue
                    for (name, wo, dst_off, spans) in (
                        ("q", 0, 24576, [(HALO, 512), (HALO + 512, 512)]),
                        ("k", 4096, 25600, [(0, 512), (512, 512), (1024, 256)]),
                        ("v", 8192, 26880, [(0, 512), (512, 512), (1024, 256)]),
                    ):
                        for (c0, cw) in spans:
                            bk = proj_banks[rot["p"] % 3]
                            rot["p"] += 1

                            def _(t, bk=bk, wo=wo, c0=c0, cw=cw, woff=woff, h2=h2, hp=hp):
                                me.wv(t, wld[hp][0], wld[hp][1])
                                me.bank_wait(t, bk)
                                for dk in range(DK):
                                    ins = t.matmul(
                                        bk[:, 0:cw],
                                        me.Abf(woff + wo + dk * 256 + h2 * 128,
                                               woff + wo + dk * 256 + h2 * 128 + 128),
                                        me.LNB[:, dk * WINT + c0:dk * WINT + c0 + cw],
                                        start=(dk == 0), stop=(dk == DK - 1))
                                me.inc(ins, "pe")
                            self.T(_)

                            def _(v, bk=bk, dst_off=dst_off, c0=c0, cw=cw, name=name,
                                  ctx_done=ctx_done):
                                me.wv(v, "pe", me.cnt["pe"])
                                me.wv(v, "pe", ctx_done)
                                b0 = c0 if name != "q" else c0 - HALO
                                ins = v.tensor_copy(me.Abf(dst_off + b0, dst_off + b0 + cw),
                                                    bk[:, 0:cw])
                                me.inc(ins, "dv")
                            self.V(_)
                            self.bank_set(bk, "dv")
                    vproj_done = self.cnt["dv"]
                    if h2 == 1:
                        war[pi] = self.cnt["pe"]

                    if HEADS_MODE < 3:
                        continue
                    for bi_, bstart in enumerate((0, 4, 8)):
                        nb = min(4, 10 - bstart)
                        bk = tr_banks[bi_ % 2]

                        def _(t, bk=bk, bstart=bstart, nb=nb, vproj_done=vproj_done):
                            me.wv(t, "dv", vproj_done)
                            me.bank_wait(t, bk)
                            bb = bk[:, :].bitcast(BF16)
                            for i in range(nb):
                                ins = t.transpose(bb[:, i * 128:(i + 1) * 128],
                                                  me.Abf(26880 + (bstart + i) * 128,
                                                         26880 + (bstart + i) * 128 + 128),
                                                  me.identb[:, :])
                            me.inc(ins, "pe")
                        self.T(_)

                        def _(v, bk=bk, bstart=bstart, nb=nb):
                            me.wv(v, "pe", me.cnt["pe"])
                            bb = bk[:, :].bitcast(BF16)
                            ins = v.tensor_copy(me.Abf(28160 + bstart * 128, 28160 + (bstart + nb) * 128),
                                                bb[:, 0:nb * 128])
                            me.inc(ins, "dv")
                        self.V(_)
                        self.bank_set(bk, "dv")

                    if HEADS_MODE < 4:
                        continue
                    attn_ready = {}
                    attn_war = {0: 0, 1: 0}

                    def stage_front(qp):
                        qt0, qt1 = 2 * qp, 2 * qp + 1
                        pp = qp % 2
                        a0, a1 = 2 * pp, 2 * pp + 1
                        sbk0 = proj_banks[rot["p"] % 3]
                        rot["p"] += 1
                        sbk1 = proj_banks[rot["p"] % 3]
                        rot["p"] += 1

                        def mi_of(qt):
                            if mt == 0 and qt == 0:
                                return 1
                            if mt == 0 and qt == 1:
                                return 2
                            return 0

                        def _(t, sbk0=sbk0, sbk1=sbk1, qt0=qt0, qt1=qt1):
                            me.bank_wait(t, sbk0)
                            ins = t.matmul(sbk0[:, 0:384],
                                           me.Abf(24576 + qt0 * 128, 24576 + qt0 * 128 + 128),
                                           me.Abf(25600 + qt0 * 128, 25600 + qt0 * 128 + 384),
                                           start=True, stop=True)
                            me.inc(ins, "pe")
                            me.bank_wait(t, sbk1)
                            ins = t.matmul(sbk1[:, 0:384],
                                           me.Abf(24576 + qt1 * 128, 24576 + qt1 * 128 + 128),
                                           me.Abf(25600 + qt1 * 128, 25600 + qt1 * 128 + 384),
                                           start=True, stop=True)
                            me.inc(ins, "pe")
                        self.T(_)

                        def _(v, sbk0=sbk0, sbk1=sbk1, qt0=qt0, qt1=qt1):
                            me.wv(v, "pe", me.cnt["pe"])
                            me.wv(v, "ac", exp_done.get(0, 0))
                            v.scalar_tensor_tensor(me.ms[0][:, :], sbk0[:, 0:384], SQH,
                                                   me.masks[:, mi_of(qt0) * 384:mi_of(qt0) * 384 + 384],
                                                   op0=ALU.mult, op1=ALU.add)
                            ins = v.scalar_tensor_tensor(me.ms[1][:, :], sbk1[:, 0:384], SQH,
                                                         me.masks[:, mi_of(qt1) * 384:mi_of(qt1) * 384 + 384],
                                                         op0=ALU.mult, op1=ALU.add)
                            me.inc(ins, "dv")
                            ins = v.reduce_max(me.nmax[0][:, :], me.ms[0][:, :],
                                               axis=AX.X, negate=True)
                            me.inc(ins, "dv")
                            ins = v.reduce_max(me.nmax[1][:, :], me.ms[1][:, :],
                                               axis=AX.X, negate=True)
                            me.inc(ins, "dv")
                        self.V(_)
                        self.bank_set(sbk0, "dv")
                        self.bank_set(sbk1, "dv")

                        def _(sc):
                            me.wv(sc, "dv", me.cnt["dv"])
                            me.wv(sc, "dv", attn_done.get(0, 0))
                            sc.activation(me.probs[0][:, :], me.ms[0][:, :], AF.Exp,
                                          bias=me.nmax[0][:, 0:1], scale=1.0,
                                          accum_out=me.esum[0][:, 0:1])
                            ins = sc.activation(me.probs[1][:, :], me.ms[1][:, :], AF.Exp,
                                                bias=me.nmax[1][:, 0:1], scale=1.0,
                                                accum_out=me.esum[1][:, 0:1])
                            me.inc(ins, "ac")
                        self.S(_)
                        exp_done[0] = self.cnt["ac"]

                        def _(v, a0=a0, a1=a1, pp=pp):
                            me.wv(v, "ac", me.cnt["ac"])
                            me.wv(v, "pe", attn_war[pp])
                            ins = v.reciprocal(me.ersum[0][:, :], me.esum[0][:, :])
                            me.inc(ins, "dv")
                            ins = v.reciprocal(me.ersum[1][:, :], me.esum[1][:, :])
                            me.inc(ins, "dv")
                            v.wait_ge(me.sem["dv"], me.cnt["dv"])
                            v.tensor_scalar(me.attn[a0][:, :], me.probs[0][:, :],
                                            me.ersum[0][:, 0:1], None, op0=ALU.mult)
                            ins = v.tensor_scalar(me.attn[a1][:, :], me.probs[1][:, :],
                                                  me.ersum[1][:, 0:1], None, op0=ALU.mult)
                            me.inc(ins, "dv")
                        self.V(_)
                        attn_done[0] = self.cnt["dv"]
                        attn_ready[qp] = self.cnt["dv"]

                    def stage_back(qp):
                        qt0, qt1 = 2 * qp, 2 * qp + 1
                        pp = qp % 2
                        a0, a1 = 2 * pp, 2 * pp + 1

                        def _(t, a0=a0, a1=a1, qp=qp):
                            me.wv(t, "dv", attn_ready[qp])
                            me.bank_wait(t, me.pC)
                            me.bank_wait(t, me.pD)
                            b0 = me.pC[:, :].bitcast(BF16)
                            b1 = me.pD[:, :].bitcast(BF16)
                            for kk in range(3):
                                t.transpose(b0[:, kk * 128:kk * 128 + 128],
                                            me.attn[a0][:, kk * 128:kk * 128 + 128],
                                            me.identb[:, :])
                            for kk in range(3):
                                ins = t.transpose(b1[:, kk * 128:kk * 128 + 128],
                                                  me.attn[a1][:, kk * 128:kk * 128 + 128],
                                                  me.identb[:, :])
                            me.inc(ins, "pe")
                        self.T(_)
                        attn_war[pp] = self.cnt["pe"]

                        tr_pe = self.cnt["pe"]

                        def _(v):
                            me.wv(v, "pe", tr_pe)
                            me.wv(v, "pe", ctxevac.get(0, 0))
                            ins = v.tensor_copy(me.attnT[0][:, :], me.pC[:, :].bitcast(BF16)[:, 0:384])
                            me.inc(ins, "dv")
                        self.V(_)
                        self.bank_set(me.pC, "dv")

                        def _(sc):
                            me.wv(sc, "pe", tr_pe)
                            me.wv(sc, "pe", ctxevac.get(0, 0))
                            ins = sc.copy(me.attnT[1][:, :], me.pD[:, :].bitcast(BF16)[:, 0:384])
                            me.inc(ins, "ac")
                        self.S(_)
                        self.bank_set(me.pD, "ac")

                        def _(t, qt0=qt0, qt1=qt1):
                            me.wv(t, "dv", me.cnt["dv"])
                            me.wv(t, "ac", me.cnt["ac"])
                            me.bank_wait(t, me.pBc1)
                            me.bank_wait(t, me.pBc2)
                            for kk in range(3):
                                t.matmul(me.pBc1[:, 0:128],
                                         me.Abf(28160 + (qt0 + kk) * 128, 28160 + (qt0 + kk) * 128 + 128),
                                         me.attnT[0][:, kk * 128:kk * 128 + 128],
                                         start=(kk == 0), stop=(kk == 2))
                            for kk in range(3):
                                ins = t.matmul(me.pBc2[:, 0:128],
                                               me.Abf(28160 + (qt1 + kk) * 128, 28160 + (qt1 + kk) * 128 + 128),
                                               me.attnT[1][:, kk * 128:kk * 128 + 128],
                                               start=(kk == 0), stop=(kk == 2))
                            me.inc(ins, "pe")
                        self.T(_)
                        ctxevac[0] = self.cnt["pe"]

                        def _(sc, h=h, qt0=qt0, qt1=qt1):
                            me.wv(sc, "pe", me.cnt["pe"])
                            sc.copy(me.CTX[:, h * TOK + qt0 * 128:h * TOK + qt0 * 128 + 128],
                                    me.pBc1[:, 0:128])
                            ins = sc.copy(me.CTX[:, h * TOK + qt1 * 128:h * TOK + qt1 * 128 + 128],
                                          me.pBc2[:, 0:128])
                            me.inc(ins, "ac")
                        self.S(_)
                        self.bank_set(me.pBc1, "ac")
                        self.bank_set(me.pBc2, "ac")

                    for qp in range(5):
                        if qp < 4:
                            stage_front(qp)
                        if qp >= 1:
                            stage_back(qp - 1)
                    ctx_done = self.cnt["pe"]

    def ph_u(self, mt):
        me = self
        if True:
            self.open_block()
            banks = [self.pA, self.pB]
            war = {0: 0, 1: 0}
            for ec in range(8):
                pi = ec % 2

                wsem = "dwA" if pi == 0 else "dwB"

                def _(g, ec=ec, pi=pi, wsem=wsem):
                    me.wv(g, "pe", war[pi])
                    me.dma(g, me.Abf(pi * 8192, pi * 8192 + 4096),
                           me.wkU[ec * 128:(ec + 1) * 128, :], wsem)
                self.G(_)
                ld = (wsem, self.cnt[wsem])
                for es in range(2):
                    for tp in range(2):
                        bk = banks[(es * 2 + tp) % 2]

                        def _(t, bk=bk, es=es, tp=tp, pi=pi, ld=ld):
                            me.wv(t, ld[0], ld[1])
                            me.bank_wait(t, bk)
                            for dk in range(DK):
                                ins = t.matmul(
                                    bk[:, 0:512],
                                    me.Abf(pi * 8192 + dk * 256 + es * 128,
                                           pi * 8192 + dk * 256 + es * 128 + 128),
                                    me.LNB[:, dk * WINT + HALO + tp * 512:
                                           dk * WINT + HALO + tp * 512 + 512],
                                    start=(dk == 0), stop=(dk == DK - 1))
                            me.inc(ins, "pe")
                        self.T(_)

                        def _(v, bk=bk, ec=ec, es=es, tp=tp):
                            me.wv(v, "pe", me.cnt["pe"])
                            ek = ec * 2 + es
                            ins = v.tensor_copy(me.UB[:, ek * TOK + tp * 512:ek * TOK + tp * 512 + 512],
                                                bk[:, 0:512])
                            me.inc(ins, "dv")
                        self.V(_)
                        self.bank_set(bk, "dv")
                war[pi] = self.cnt["pe"]

    def ph_olocal_gate(self, mt):
        me = self
        if True:
            self.open_block()
            banks = [self.pA, self.pB]
            war = {0: 0, 1: 0}
            gate_banks = {0: self.pS, 1: self.pBc1}
            for oc4 in range(8):
                pi = oc4 % 2

                wsem = "dwA" if pi == 0 else "dwB"

                def _(g, oc4=oc4, pi=pi, wsem=wsem):
                    me.wv(g, "pe", war[pi])
                    me.dma(g, me.LNB[:, pi * 4096:pi * 4096 + 4096],
                           me.woP[oc4 * 128:(oc4 + 1) * 128, :], wsem)
                self.G(_)
                ld = (wsem, self.cnt[wsem])
                for os_ in range(2):
                    oc = oc4 * 2 + os_
                    for tp in range(2):
                        bk = banks[(os_ * 2 + tp) % 2]

                        def _(t, bk=bk, os_=os_, tp=tp, pi=pi, ld=ld):
                            me.wv(t, ld[0], ld[1])
                            me.bank_wait(t, bk)
                            for dk in range(DK):
                                ins = t.matmul(
                                    bk[:, 0:512],
                                    me.LNB[:, pi * 4096 + dk * 256 + os_ * 128:
                                           pi * 4096 + dk * 256 + os_ * 128 + 128],
                                    me.CTX[:, dk * TOK + tp * 512:dk * TOK + tp * 512 + 512],
                                    start=(dk == 0), stop=(dk == DK - 1))
                            me.inc(ins, "pe")
                        self.T(_)

                        def _(v, bk=bk, oc=oc, tp=tp):
                            me.wv(v, "pe", me.cnt["pe"])
                            ins = v.tensor_copy(me.A32[:, oc * TOK + tp * 512:oc * TOK + tp * 512 + 512],
                                                bk[:, 0:512])
                            me.inc(ins, "dv")
                        self.V(_)
                        self.bank_set(bk, "dv")
                war[pi] = self.cnt["pe"]
            olocal_done = self.cnt["dv"]

            for tp in range(2):
                for oc in range(DK):
                    def _(t, oc=oc, tp=tp):
                        if oc == 0:
                            me.wv(t, "dv", olocal_done)
                            me.bank_wait(t, gate_banks[tp])
                        ins = t.matmul(gate_banks[tp][0:1, 0:512], me.gwTs[:, oc:oc + 1],
                                       me.A32[:, oc * TOK + tp * 512:oc * TOK + tp * 512 + 512],
                                       start=(oc == 0), stop=(oc == DK - 1))
                        me.inc(ins, "pe")
                    self.T(_)

            def _(sc):
                me.wv(sc, "pe", me.cnt["pe"])
                sc.activation(me.gtmp[0:1, 0:512], me.pS[0:1, 0:512], AF.Exp,
                              bias=me.neggb[0:1, 0:1], scale=-1.0)
                ins = sc.activation(me.gtmp[0:1, 512:1024], me.pBc1[0:1, 0:512], AF.Exp,
                                    bias=me.neggb[0:1, 0:1], scale=-1.0)
                me.inc(ins, "ac")
            self.S(_)
            self.bank_set(self.pS, "ac")
            self.bank_set(self.pBc1, "ac")

            def _(v):
                me.wv(v, "ac", me.cnt["ac"])
                v.tensor_scalar(me.grow[0:1, :], me.gtmp[0:1, :], 1.0, None, op0=ALU.add)
                ins = v.reciprocal(me.gtmp[0:1, :], me.grow[0:1, :])
                me.inc(ins, "dv")
                v.wait_ge(me.sem["dv"], me.cnt["dv"])
                ins = v.engine_nop()
                me.inc(ins, "dv")
            self.V(_)

            for tp in range(2):
                def _(t, tp=tp):
                    me.wv(t, "dv", me.cnt["dv"])
                    me.bank_wait(t, me.pBc2)
                    ins = t.matmul(me.pBc2[:, 0:512], me.onesrowf[:, :],
                                   me.gtmp[0:1, tp * 512:tp * 512 + 512],
                                   start=True, stop=True)
                    me.inc(ins, "pe")
                self.T(_)

                def _(v, tp=tp):
                    me.wv(v, "pe", me.cnt["pe"])
                    ins = v.tensor_copy(me.bypB[:, tp * 512:tp * 512 + 512], me.pBc2[:, 0:512])
                    me.inc(ins, "dv")
                self.V(_)
                self.bank_set(self.pBc2, "dv")

    def ph_sim_topk(self, mt):
        me = self
        if True:
            self.open_block()
            banks = [self.pA, self.pB]
            war = {0: 0, 1: 0}
            for mc in range(16):
                pi = mc % 2

                wsem = "dwA" if pi == 0 else "dwB"

                def _(g, mc=mc, pi=pi, wsem=wsem):
                    me.wv(g, "pe", war[pi])
                    me.dma(g, me.SS[:, pi * 4096:pi * 4096 + 4096],
                           me.mbP[mc * 128:(mc + 1) * 128, :], wsem)
                self.G(_)
                ld = (wsem, self.cnt[wsem])
                for tt in range(8):
                    bk = banks[tt % 2]

                    def _(t, bk=bk, tt=tt, pi=pi, ld=ld):
                        me.wv(t, ld[0], ld[1])
                        me.bank_wait(t, bk)
                        for ek in range(DK):
                            ins = t.matmul(
                                bk[:, 0:256],
                                me.UB[:, ek * TOK + tt * 128:ek * TOK + tt * 128 + 128],
                                me.SS[:, pi * 4096 + ek * 256:pi * 4096 + ek * 256 + 256],
                                start=(ek == 0), stop=(ek == DK - 1))
                        me.inc(ins, "pe")
                    self.T(_)

                    def _(v, bk=bk, mc=mc, tt=tt):
                        me.wv(v, "pe", me.cnt["pe"])
                        bi = bk[:, :].bitcast(I32)
                        ins = v.tensor_scalar(me.packed[:, :].bitcast(I32), bi[:, 0:256],
                                              -4096, mc << 8, op0=ALU.bitwise_and,
                                              op1=ALU.bitwise_or)
                        me.inc(ins, "dv")
                        v.tensor_tensor(me.packed[:, :].bitcast(I32),
                                        me.packed[:, :].bitcast(I32),
                                        me.colidx[:, :], op=ALU.bitwise_or)
                        ins = v.max(out=me.candA[:, tt * 128 + mc * 8:tt * 128 + mc * 8 + 8],
                                    in_=me.packed[:, :])
                        me.inc(ins, "dv")
                    self.V(_)
                    self.bank_set(bk, "dv")
                war[pi] = self.cnt["pe"]

            for tt in range(8):
                def _(v, tt=tt):
                    me.wv(v, "dv", me.cnt["dv"])
                    ins = v.max(out=me.run8[:, :], in_=me.candA[:, tt * 128:tt * 128 + 128])
                    me.inc(ins, "dv")
                    v.wait_ge(me.sem["dv"], me.cnt["dv"])
                    v.tensor_scalar(me.idxA[:, tt * 8:tt * 8 + 8], me.run8[:, :].bitcast(I32),
                                    4095, None, op0=ALU.bitwise_and)
                    ins = v.tensor_scalar(me.vals8[:, :].bitcast(I32), me.run8[:, :].bitcast(I32),
                                          -4096, None, op0=ALU.bitwise_and)
                    me.inc(ins, "dv")
                    v.wait_ge(me.sem["dv"], me.cnt["dv"])
                    ins = v.tensor_scalar(me.vals8[:, :], me.vals8[:, :], SQD, None, op0=ALU.mult)
                    me.inc(ins, "dv")
                    v.wait_ge(me.sem["dv"], me.cnt["dv"])
                    ins = v.reduce_max(me.nmax[0][:, :], me.vals8[:, :], axis=AX.X, negate=True)
                    me.inc(ins, "dv")
                self.V(_)

                def _(sc):
                    me.wv(sc, "dv", me.cnt["dv"])
                    ins = sc.activation(me.ev8[:, :], me.vals8[:, :], AF.Exp,
                                        bias=me.nmax[0][:, 0:1], scale=1.0,
                                        accum_out=me.esum[0][:, 0:1])
                    me.inc(ins, "ac")
                self.S(_)

                def _(v, tt=tt):
                    me.wv(v, "ac", me.cnt["ac"])
                    ins = v.reciprocal(me.ersum[0][:, :], me.esum[0][:, :])
                    me.inc(ins, "dv")
                    v.wait_ge(me.sem["dv"], me.cnt["dv"])
                    ins = v.tensor_scalar(me.w8A[:, tt * 8:tt * 8 + 8], me.ev8[:, :],
                                          me.ersum[0][:, 0:1], None, op0=ALU.mult)
                    me.inc(ins, "dv")
                self.V(_)

    def ph_mem(self, mt):
        me = self
        base = mt * TOK
        if True:
            self.open_block()
            tr_banks = [self.pC, self.pD]
            banks = [self.pA, self.pB]
            mac_done = {}
            ytm_war = {0: 0, 1: 0}
            for half in range(2):
                for j in range(4):
                    tt = half * 4 + j
                    yo = 2048 + (j % 2) * 2048
                    for k in range(8):
                        seq = tt * 8 + k

                        gsem = "dgA" if k % 2 == 0 else "dgB"

                        def _(g, tt=tt, k=k, seq=seq, gsem=gsem):
                            me.wv(g, "dv", mac_done.get(seq - 2, 0))
                            ins = g.indirect_dma_start(
                                out=me.UB[:, (k % 2) * 2048:(k % 2) * 2048 + 2048],
                                out_offset=None, in_=me.mb[:, :],
                                in_offset=bass.IndirectOffsetOnAxis(
                                    me.idxA[:, tt * 8 + k:tt * 8 + k + 1], axis=0))
                            me.inc(ins, gsem, 16)
                        self.G(_)

                        def _(v, tt=tt, k=k, yo=yo, j=j, gsem=gsem):
                            me.wv(v, gsem, me.cnt[gsem])
                            if k == 0:
                                me.wv(v, "ac", ytm_war[j % 2])
                            gbuf = me.UB[:, (k % 2) * 2048:(k % 2) * 2048 + 2048]
                            ytm = me.Uf32(yo, yo + 2048)
                            if k == 0:
                                ins = v.tensor_scalar(ytm, gbuf, me.w8A[:, tt * 8:tt * 8 + 1],
                                                      None, op0=ALU.mult)
                            else:
                                ins = v.scalar_tensor_tensor(
                                    ytm, gbuf, me.w8A[:, tt * 8 + k:tt * 8 + k + 1],
                                    ytm, op0=ALU.mult, op1=ALU.add)
                            me.inc(ins, "dv")
                        self.V(_)
                        mac_done[seq] = self.cnt["dv"]

                    for bq in range(4):
                        bk = tr_banks[bq % 2]

                        def _(t, bk=bk, bq=bq, yo=yo):
                            me.wv(t, "dv", me.cnt["dv"])
                            me.bank_wait(t, bk)
                            for i in range(4):
                                ins = t.transpose(
                                    bk[:, i * 128:i * 128 + 128],
                                    me.Uf32(yo + (bq * 4 + i) * 128, yo + (bq * 4 + i) * 128 + 128),
                                    me.identf[:, :])
                            me.inc(ins, "pe")
                        self.T(_)

                        def _(sc, bk=bk, bq=bq, half=half, j=j):
                            me.wv(sc, "pe", me.cnt["pe"])
                            for i in range(4):
                                dk = bq * 4 + i
                                ins = sc.copy(
                                    me.CTX[:, dk * TOK + half * 512 + j * 128:
                                           dk * TOK + half * 512 + j * 128 + 128],
                                    bk[:, i * 128:i * 128 + 128])
                            me.inc(ins, "ac")
                        self.S(_)
                        self.bank_set(bk, "ac")
                    ytm_war[j % 2] = self.cnt["ac"]
                ypre_done = self.cnt["ac"]

                sswar = self.cnt["pe"]
                zwar = {0: sswar, 1: sswar}
                for zc in range(8):
                    pi = zc % 2

                    wsem = "dwA" if pi == 0 else "dwB"

                    def _(g, zc=zc, pi=pi, wsem=wsem):
                        me.wv(g, "pe", zwar[pi])
                        me.dma(g, me.SS[:, pi * 4096:pi * 4096 + 4096],
                               me.wvP[zc * 128:(zc + 1) * 128, :], wsem)
                    self.G(_)
                    ld = (wsem, self.cnt[wsem])
                    for ic2 in range(2):
                        ic = zc * 2 + ic2
                        bk = banks[ic % 2]

                        def _(t, bk=bk, ic2=ic2, pi=pi, ld=ld, half=half):
                            me.wv(t, ld[0], ld[1])
                            me.wv(t, "ac", ypre_done)
                            me.bank_wait(t, bk)
                            for dk in range(DK):
                                ins = t.matmul(
                                    bk[:, 0:512],
                                    me.SS[:, pi * 4096 + dk * 256 + ic2 * 128:
                                           pi * 4096 + dk * 256 + ic2 * 128 + 128],
                                    me.CTX[:, dk * TOK + half * 512:dk * TOK + half * 512 + 512],
                                    start=(dk == 0), stop=(dk == DK - 1))
                            me.inc(ins, "pe")
                        self.T(_)

                        def _(v, bk=bk, ic=ic, half=half):
                            me.wv(v, "pe", me.cnt["pe"])
                            ins = v.tensor_copy(
                                me.LNB[:, ic * TOK + half * 512:ic * TOK + half * 512 + 512],
                                bk[:, 0:512])
                            me.inc(ins, "dv")
                        self.V(_)
                        self.bank_set(bk, "dv")
                    zwar[pi] = self.cnt["pe"]

                z_done = self.cnt["dv"]
                sswar2 = self.cnt["pe"]
                owar = {0: sswar2, 1: sswar2}
                xw0 = self.cnt["dv"]
                xres_war = {0: xw0, 1: xw0}
                for oc2 in range(8):
                    pi = oc2 % 2

                    wsem = "dwA" if pi == 0 else "dwB"

                    def _(g, oc2=oc2, pi=pi, wsem=wsem):
                        me.wv(g, "pe", owar[pi])
                        me.dma(g, me.SS[:, pi * 4096:pi * 4096 + 4096],
                               me.woP[oc2 * 128:(oc2 + 1) * 128, :], wsem)
                    self.G(_)
                    ld = (wsem, self.cnt[wsem])
                    for o2 in range(2):
                        oc = oc2 * 2 + o2
                        bk = banks[oc % 2]
                        xb = (oc % 2) * 512

                        xsem = "dxA" if oc % 2 == 0 else "dxB"

                        def _(q, oc=oc, xb=xb, half=half, xsem=xsem):
                            me.wv(q, "dv", xres_war[oc % 2])
                            src = me.xs.rearrange("(k p) t -> p k t", p=128)[
                                :, oc:oc + 1, HALO + base + half * 512:HALO + base + half * 512 + 512]
                            me.dma(q, me.xrest[:, xb:xb + 512].rearrange("p (a t) -> p a t", a=1),
                                   src, xsem)
                        self.Q(_)
                        xld = (xsem, self.cnt[xsem])

                        def _(t, bk=bk, o2=o2, pi=pi, ld=ld, half=half):
                            me.wv(t, ld[0], ld[1])
                            me.wv(t, "dv", z_done)
                            me.bank_wait(t, bk)
                            for ik in range(DK):
                                ins = t.matmul(
                                    bk[:, 0:512],
                                    me.SS[:, pi * 4096 + ik * 256 + o2 * 128:
                                           pi * 4096 + ik * 256 + o2 * 128 + 128],
                                    me.LNB[:, ik * TOK + half * 512:ik * TOK + half * 512 + 512],
                                    start=(ik == 0), stop=(ik == DK - 1))
                            me.inc(ins, "pe")
                        self.T(_)

                        def _(v, bk=bk, oc=oc, xb=xb, half=half, xld=xld):
                            me.wv(v, "pe", me.cnt["pe"])
                            me.wv(v, xld[0], xld[1])
                            a = me.A32[:, oc * TOK + half * 512:oc * TOK + half * 512 + 512]
                            v.tensor_mul(me.tmpf[:, :], bk[:, 0:512],
                                         me.bypB[:, half * 512:half * 512 + 512])
                            v.tensor_add(me.tmpf[:, :], me.tmpf[:, :], a)
                            ins = v.tensor_add(a, me.tmpf[:, :], me.xrest[:, xb:xb + 512])
                            me.inc(ins, "dv")
                        self.V(_)
                        self.bank_set(bk, "dv")
                        xres_war[oc % 2] = self.cnt["dv"]
                    owar[pi] = self.cnt["pe"]
            self.close_block()

    def ph_ffn(self, mt):
        me = self
        if True:
            self.open_block()
            prev_sp_pl = 0
            prev_sp_dv = 0
            dCD = [self.pC, self.pD]
            QS = 6144
            gupld = {}
            ju_done = {}
            dload = {}
            sp_arr = {}
            down_done = {}

            JSEM = ("dwA", "dwB", "dwC", "dwD")

            def load_gu(sg, j):
                js = JSEM[j]

                def _(g, sg=sg, j=j, js=js):
                    me.wv(g, "pe", ju_done.get((sg - 1, j), 0))
                    r0 = (sg * 4 + j) * 128
                    me.dma(g, me.CTX[:, j * 2048:j * 2048 + 2048],
                           me.wgP[r0:r0 + 128, :], js)
                    me.dma(g, me.CTX[:, 8192 + j * 2048:8192 + j * 2048 + 2048],
                           me.wuP[r0:r0 + 128, :], js)
                self.G(_)
                gupld[(sg, j)] = (js, self.cnt[js])

            def load_down(sg):
                def _(g, sg=sg):
                    me.wv(g, "pe", down_done.get(sg - 1, 0))
                    me.dma(g, me.SS[:, 0:8192], me.wdP[sg * 128:(sg + 1) * 128, :], "dwE")
                self.G(_)
                dload[sg] = self.cnt["dwE"]

            def hsp(par, c0, c1):
                if par == 0:
                    return me.UB[:, c0:c1]
                return me.LNB[:, 16384 + c0:16384 + c1]

            def emit_down(sg):
                par = sg % 2
                for oc in range(16):
                    for tp in range(2):
                        bk = dCD[(oc * 2 + tp) % 2]

                        def _(t, bk=bk, oc=oc, tp=tp, sg=sg, par=par):
                            me.wv(t, "dwE", dload[sg])
                            me.wv(t, "dv", sp_arr[sg])
                            me.bank_wait(t, bk)
                            for ic in range(4):
                                ins = t.matmul(
                                    bk[:, 0:512],
                                    me.SS[:, ic * 2048 + oc * 128:ic * 2048 + oc * 128 + 128],
                                    hsp(par, ic * 1024 + tp * 512, ic * 1024 + tp * 512 + 512),
                                    start=(ic == 0), stop=(ic == 3))
                            me.inc(ins, "pe")
                        self.T(_)

                        def _(v, bk=bk, oc=oc, tp=tp):
                            me.wv(v, "pe", me.cnt["pe"])
                            a = me.A32[:, oc * TOK + tp * 512:oc * TOK + tp * 512 + 512]
                            ins = v.tensor_add(a, a, bk[:, 0:512])
                            me.inc(ins, "dv")
                        self.V(_)
                        self.bank_set(bk, "dv")
                down_done[sg] = self.cnt["pe"]

            for j in range(4):
                load_gu(0, j)
            load_down(0)

            for sg in range(16):
                gbanks = [self.pA, self.pE]
                ubanks = [self.pB, self.pBc1]
                stmp = [QS, QS + 1024]
                stmp_war = {0: prev_sp_dv, 1: prev_sp_dv}
                for j in range(4):
                    for tp in range(2):
                        par = (j * 2 + tp) % 2
                        gbk = gbanks[par]
                        ubk = ubanks[par]

                        def _(t, j=j, tp=tp, gbk=gbk, sg=sg):
                            me.wv(t, gupld[(sg, j)][0], gupld[(sg, j)][1])
                            me.bank_wait(t, gbk)
                            for dk in range(DK):
                                ins = t.matmul(
                                    gbk[:, 0:512],
                                    me.CTX[:, j * 2048 + dk * 128:j * 2048 + dk * 128 + 128],
                                    me.LNB[:, dk * TOK + tp * 512:dk * TOK + tp * 512 + 512],
                                    start=(dk == 0), stop=(dk == DK - 1))
                            me.inc(ins, "pe")
                        self.T(_)
                        gate_pe = self.cnt["pe"]

                        def _(t, j=j, tp=tp, ubk=ubk, sg=sg):
                            me.bank_wait(t, ubk)
                            for dk in range(DK):
                                ins = t.matmul(
                                    ubk[:, 0:512],
                                    me.CTX[:, 8192 + j * 2048 + dk * 128:8192 + j * 2048 + dk * 128 + 128],
                                    me.LNB[:, dk * TOK + tp * 512:dk * TOK + tp * 512 + 512],
                                    start=(dk == 0), stop=(dk == DK - 1))
                            me.inc(ins, "pe")
                        self.T(_)
                        if tp == 1:
                            ju_done[(sg, j)] = self.cnt["pe"]
                            if sg < 15:
                                load_gu(sg + 1, j)

                        def _(sc, j=j, tp=tp, gbk=gbk, par=par, gate_pe=gate_pe):
                            me.wv(sc, "pe", gate_pe)
                            me.wv(sc, "dv", stmp_war[par])
                            if j == 0 and tp == 0:
                                me.wv(sc, "pl", prev_sp_pl)
                            ins = sc.activation(
                                me.Uf32(stmp[par], stmp[par] + 1024)[:, tp * 512:tp * 512 + 512],
                                gbk[:, 0:512], AF.Silu)
                            me.inc(ins, "ac")
                        self.S(_)
                        self.bank_set(gbk, "ac")

                        def _(v, j=j, tp=tp, ubk=ubk, par=par):
                            me.wv(v, "pe", me.cnt["pe"])
                            me.wv(v, "ac", me.cnt["ac"])
                            if j == 0 and tp == 0:
                                me.wv(v, "pl", prev_sp_pl)
                            ins = v.tensor_mul(
                                me.Uf32(2048 + j * 1024, 2048 + j * 1024 + 1024)[:, tp * 512:tp * 512 + 512],
                                me.Uf32(stmp[par], stmp[par] + 1024)[:, tp * 512:tp * 512 + 512],
                                ubk[:, 0:512])
                            me.inc(ins, "dv")
                        self.V(_)
                        self.bank_set(ubk, "dv")
                        stmp_war[par] = self.cnt["dv"]
                p_done = self.cnt["dv"]

                if sg >= 1:
                    emit_down(sg - 1)
                    if sg < 15:
                        load_down(sg)

                def pq(j, q4):
                    return me.Uf32(2048 + j * 1024 + q4 * 256, 2048 + j * 1024 + q4 * 256 + 256)

                spar = sg % 2
                for q4 in range(4):
                    sj = [me.Uf32(QS + 256 * j, QS + 256 * (j + 1)) for j in range(4)]
                    sA = me.Uf32(QS + 1024, QS + 1280)
                    sB = me.Uf32(QS + 1280, QS + 1536)
                    sC = me.Uf32(QS + 1536, QS + 1792)
                    sD = me.Uf32(QS + 1792, QS + 2048)

                    def _(g, q4=q4, sj=sj):
                        me.wv(g, "dv", p_done if q4 == 0 else me.cnt["dv"])
                        for j in range(4):
                            ins = g.tensor_mul(sj[j], pq(j, q4), pq(j, q4))
                        me.inc(ins, "pl")
                    self.G(_)

                    def _(v, sj=sj, sA=sA, sB=sB, sC=sC, sD=sD):
                        me.wv(v, "pl", me.cnt["pl"])
                        v.tensor_tensor(sA, sj[0], sj[1], op=ALU.max)
                        v.tensor_tensor(sB, sj[0], sj[1], op=ALU.min)
                        v.tensor_tensor(sC, sj[2], sj[3], op=ALU.max)
                        v.tensor_tensor(sD, sj[2], sj[3], op=ALU.min)
                        v.tensor_tensor(sA, sA, sC, op=ALU.min)
                        v.tensor_tensor(sB, sB, sD, op=ALU.max)
                        ins = v.tensor_tensor(sA, sA, sB, op=ALU.max)
                        me.inc(ins, "dv")
                    self.V(_)

                    def _(g, sj=sj, sA=sA):
                        me.wv(g, "dv", me.cnt["dv"])
                        for j in range(4):
                            ins = g.tensor_sub(sj[j], sj[j], sA)
                        me.inc(ins, "pl")
                    self.G(_)

                    def _(v, q4=q4, sj=sj, sg=sg, spar=spar):
                        me.wv(v, "pl", me.cnt["pl"])
                        if q4 == 0:
                            me.wv(v, "pe", down_done.get(sg - 2, 0))
                        for j in range(4):
                            ins = v.scalar_tensor_tensor(
                                hsp(spar, j * 1024 + q4 * 256, j * 1024 + q4 * 256 + 256),
                                sj[j], 0.0, pq(j, q4), op0=ALU.is_ge, op1=ALU.mult)
                        me.inc(ins, "dv")
                    self.V(_)
                sp_arr[sg] = self.cnt["dv"]
                prev_sp_pl = self.cnt["pl"]
                prev_sp_dv = sp_arr[sg]
            load_down(15)
            emit_down(15)

    def ph_store(self, mt):
        me = self
        if True:
            self.open_block()

            def _(q):
                me.dma(q, me.out.rearrange("(k p) t -> p k t", p=128)[:, :, mt * TOK:mt * TOK + TOK],
                       me.wdst(me.A32[:, :], TOK), "do")
            self.Q(_)
            self.store_done = self.cnt["do"]

    def ph_final(self):
        me = self
        if True:
            self.open_block()

            def _(q):
                q.wait_ge(me.sem["do"], me.cnt["do"])
            self.Q(_)

    def build(self):
        self.marks = []

        def mark(name):
            self.marks.append((name, self.nc.n_instructions()))
        self.mark = mark
        mark("setup")
        self.ph_setup()
        for mt in (0, 1):
            u = self.upto
            self.mark(f"ln1.{mt}")
            self.ph_ln(mt, "x")
            self.tap(f"ln1h{mt}", [(self.LNB[:, :], 20480)])
            if u >= 1:
                self.mark(f"heads.{mt}")
                self.ph_heads(mt)
                self.tap(f"ctx{mt}", [(self.CTX[:, :], 16384)])
            if u >= 2:
                self.mark(f"u.{mt}")
                self.ph_u(mt)
                self.tap(f"u{mt}", [(self.UB[:, :], 16384)])
            if u >= 3:
                self.mark(f"olg.{mt}")
                self.ph_olocal_gate(mt)
                self.tap(f"ol{mt}", [(self.A32[:, :], 16384)])
                self.tap(f"byp{mt}", [(self.bypB[:, :], 1024)])
            if u >= 4:
                self.mark(f"sim.{mt}")
                self.ph_sim_topk(mt)
                self.tap(f"idx{mt}", [(self.idxA[:, :], 64), (self.w8A[:, :], 64)])
            if u >= 5:
                self.mark(f"mem.{mt}")
                self.ph_mem(mt)
                self.tap(f"yp{mt}", [(self.CTX[:, :], 16384)])
                self.tap(f"z{mt}", [(self.LNB[:, 0:16384], 16384)])
                self.tap(f"xp{mt}", [(self.A32[:, :], 16384)])
            if u >= 6:
                self.mark(f"ln2.{mt}")
                self.ph_ln(mt, "acc")
                self.tap(f"ln2_{mt}", [(self.LNB[:, 0:16384], 16384)])
            if u >= 7:
                self.mark(f"ffn.{mt}")
                self.ph_ffn(mt)
                self.tap(f"ff{mt}", [(self.A32[:, :], 16384)])
            self.mark(f"store.{mt}")
            self.ph_store(mt)
        self.mark("final")
        self.ph_final()
        self.close_block()
        self.nc.finalize()
        return self.nc


_CACHE = {}


def _build_masks():
    r = np.arange(128)[:, None]
    c = np.arange(384)[None, :]
    band = (c >= r + 1) & (c <= r + 256)
    mg = np.where(band, 0.0, -1.0e30).astype(np.float32)
    m0 = np.where(band & (c >= 256), 0.0, -1.0e30).astype(np.float32)
    m1 = np.where(band & (c >= 128), 0.0, -1.0e30).astype(np.float32)
    return mg, m0, m1


def _chunked(wT, ncol):
    Dr, X = wT.shape
    nch = X // ncol
    return np.ascontiguousarray(
        wT.reshape(16, 128, nch, ncol).transpose(2, 1, 0, 3).reshape(nch * 128, 16 * ncol))


def _wd_chunked(wd):
    return np.ascontiguousarray(
        wd.reshape(16, 4, 128, 2048).transpose(0, 2, 1, 3).reshape(2048, 8192))


def _ffn_perm():
    perm = np.empty(I2, dtype=np.int64)
    idx = 0
    for sg in range(16):
        for j in range(4):
            for gw in range(128):
                perm[idx] = 4 * (sg * 128 + gw) + j
                idx += 1
    return perm


def kernel(L0_attn_hidden_states, L0_attn_wq, L0_attn_wk, L0_attn_wv, L0_attn_wo,
           L0_attn_gate_w, L0_attn_gate_b, L0_attn_mem_bank,
           L0_ffn_w_gate, L0_ffn_w_up, L0_ffn_w_down):
    x = np.asarray(L0_attn_hidden_states, dtype=np.float32)
    f32c = lambda a: np.ascontiguousarray(np.asarray(a, dtype=np.float32))
    wq, wk, wv, wo = map(f32c, (L0_attn_wq, L0_attn_wk, L0_attn_wv, L0_attn_wo))
    gate_w = f32c(L0_attn_gate_w)
    gate_b = f32c(L0_attn_gate_b).reshape(1, 1)
    mbank = f32c(L0_attn_mem_bank)
    wg_, wu_, wd_ = map(f32c, (L0_ffn_w_gate, L0_ffn_w_up, L0_ffn_w_down))

    perm = _ffn_perm()
    shared = {
        "wqP": _chunked(wq.T, 256), "wkP": _chunked(wk.T, 256),
        "wvP": _chunked(wv.T, 256), "woP": _chunked(wo.T, 256),
        "wkU": _chunked(wk, 256), "mbP": _chunked(mbank.T, 256),
        "mb": mbank,
        "gwT": np.ascontiguousarray(gate_w.reshape(16, 128).T),
        "gb": gate_b,
        "wgP": _chunked(wg_[:, perm], 128),
        "wuP": _chunked(wu_[:, perm], 128),
        "wdP": _wd_chunked(wd_[perm, :]),
    }
    mg, m0, m1 = _build_masks()
    masks_h0 = np.ascontiguousarray(np.concatenate([mg, m0, m1], axis=1))
    masks_h1 = np.ascontiguousarray(np.concatenate([mg, mg, mg], axis=1))

    in_maps = []
    for c in range(8):
        b, half = c // 2, c % 2
        xT = x[b].T
        if half == 0:
            xsl = np.concatenate([np.zeros((D, HALO), np.float32), xT[:, 0:2048]], axis=1)
        else:
            xsl = xT[:, 2048 - HALO:4096]
        m = dict(shared)
        m["xs"] = np.ascontiguousarray(xsl)
        m["masks"] = masks_h0 if half == 0 else masks_h1
        in_maps.append(m)

    if "nc" not in _CACHE:
        kb = KB()
        _CACHE["nc"] = kb.build()
    nc = _CACHE["nc"]
    res = run_bass_kernel_spmd(nc, in_maps, core_ids=list(range(8)))

    out = np.empty((4, 4096, D), dtype=np.float32)
    for c in range(8):
        b, half = c // 2, c % 2
        out[b, half * 2048:(half + 1) * 2048, :] = res.results[c]["out"].T
    return out


# version 09 (final)
# speedup vs baseline: 1.0701x; 1.0185x over previous
import math
import os
import numpy as np
from contextlib import ExitStack

import concourse.bass as bass
import concourse.mybir as mybir
from concourse.bass_utils import run_bass_kernel_spmd

F32 = mybir.dt.float32
BF16 = mybir.dt.bfloat16
I32 = mybir.dt.int32
AF = mybir.ActivationFunctionType
ALU = mybir.AluOpType
AX = mybir.AxisListType

D = 2048
DK = 16
TOK = 1024
HALO = 256
WINT = TOK + HALO
MEM = 4096
I2 = 8192
U_MODE = 8
HEADS_MODE = 5
SQD = 1.0 / math.sqrt(D)
SQH = 1.0 / math.sqrt(128.0)
EPS = 1e-5


class KB:
    def __init__(self, debug_taps=(), upto=99):
        self.taps = tuple(debug_taps)
        self.upto = upto
        nc = bass.Bass("TRN2", target_bir_lowering=False, dynamic_dma_scratch_size=16384,
                       detect_race_conditions=False)
        self.nc = nc
        self.ctx = ExitStack()
        e = self.ctx.enter_context

        dt = nc.dram_tensor
        self.xs = dt("xs", [D, HALO + 2048], F32, kind="ExternalInput").ap()
        self.wqP = dt("wqP", [1024, 4096], F32, kind="ExternalInput").ap()
        self.wkP = dt("wkP", [1024, 4096], F32, kind="ExternalInput").ap()
        self.wvP = dt("wvP", [1024, 4096], F32, kind="ExternalInput").ap()
        self.woP = dt("woP", [1024, 4096], F32, kind="ExternalInput").ap()
        self.wkU = dt("wkU", [1024, 4096], F32, kind="ExternalInput").ap()
        self.mbP = dt("mbP", [2048, 4096], F32, kind="ExternalInput").ap()
        self.mb = dt("mb", [MEM, D], F32, kind="ExternalInput").ap()
        self.gwT = dt("gwT", [128, 16], F32, kind="ExternalInput").ap()
        self.gb = dt("gb", [1, 1], F32, kind="ExternalInput").ap()
        self.wgP = dt("wgP", [8192, 2048], F32, kind="ExternalInput").ap()
        self.wuP = dt("wuP", [8192, 2048], F32, kind="ExternalInput").ap()
        self.wdP = dt("wdP", [2048, 8192], F32, kind="ExternalInput").ap()
        self.masks_d = dt("masks", [128, 3 * 384], F32, kind="ExternalInput").ap()
        self.out = dt("out", [D, 2048], F32, kind="ExternalOutput").ap()
        self.dbg = {}
        for t in self.taps:
            self.dbg[t] = dt("dbg_" + t, [128, 20480], F32, kind="ExternalOutput").ap()

        self.A32 = e(nc.sbuf_tensor("A32", [128, 16384], F32))
        self.LNB = e(nc.sbuf_tensor("LNB", [128, 20480], BF16))
        self.CTX = e(nc.sbuf_tensor("CTXB", [128, 16384], BF16))
        self.UB = e(nc.sbuf_tensor("UBB", [128, 16384], BF16))
        self.SS = e(nc.sbuf_tensor("SSB", [128, 8192], BF16))

        self.masks = e(nc.sbuf_tensor("masks_s", [128, 3 * 384], F32))
        self.colidx = e(nc.sbuf_tensor("colidx", [128, 256], I32))
        self.identb = e(nc.sbuf_tensor("identb", [128, 128], BF16))
        self.identf = e(nc.sbuf_tensor("identf", [128, 128], F32))
        self.onescol = e(nc.sbuf_tensor("onescol", [128, 1], BF16))
        self.onesrow = e(nc.sbuf_tensor("onesrow", [1, 128], BF16))
        self.onesrowf = e(nc.sbuf_tensor("onesrowf", [1, 128], F32))
        ubf = self.UB[:, :].bitcast(F32)
        self.ms = [ubf[:, 2048 + 384 * i:2048 + 384 * (i + 1)] for i in range(2)]
        self.probs = [ubf[:, 2816 + 384 * i:2816 + 384 * (i + 1)] for i in range(2)]
        self.attn = [self.UB[:, 7168 + 384 * i:7168 + 384 * (i + 1)] for i in range(4)]
        self.attnT = [self.UB[:, 8704 + 384 * i:8704 + 384 * (i + 1)] for i in range(2)]
        self.nmax = [e(nc.sbuf_tensor(f"nmax{i}", [128, 1], F32)) for i in range(2)]
        self.esum = [e(nc.sbuf_tensor(f"esum{i}", [128, 1], F32)) for i in range(2)]
        self.ersum = [e(nc.sbuf_tensor(f"ersum{i}", [128, 1], F32)) for i in range(2)]
        self.packed = e(nc.sbuf_tensor("packed", [128, 256], F32))
        self.candA = e(nc.sbuf_tensor("candA", [128, 8 * 128], F32))
        self.run8 = e(nc.sbuf_tensor("run8", [128, 8], F32))
        self.vals8 = e(nc.sbuf_tensor("vals8", [128, 8], F32))
        self.ev8 = e(nc.sbuf_tensor("ev8", [128, 8], F32))
        self.idxA = e(nc.sbuf_tensor("idxA", [128, 64], I32))
        self.w8A = e(nc.sbuf_tensor("w8A", [128, 64], F32))
        self.xbs = [e(nc.sbuf_tensor(f"xbs{i}", [128, 512], BF16)) for i in range(2)]
        self.sqs = [e(nc.sbuf_tensor(f"sqs{i}", [128, 512], BF16)) for i in range(2)]
        self.nmr = ubf[0:1, 0:512]
        self.vrr = ubf[0:1, 512:1024]
        self.m2r = ubf[0:1, 1024:1536]
        self.rsr = ubf[0:1, 1536:2048]
        self.nmb = self.UB[0:1, 4096:4608]
        self.rsb = self.UB[0:1, 4608:5120]
        self.tmpf = e(nc.sbuf_tensor("tmpf", [128, 512], F32))
        ssf = self.SS[:, :].bitcast(F32)
        self.grow = ssf[0:1, 0:1024]
        self.gtmp = ssf[0:1, 1024:2048]
        self.neggb = e(nc.sbuf_tensor("neggb", [1, 1], F32))
        self.bypB = e(nc.sbuf_tensor("bypB", [128, 1024], F32))
        self.gwTs = e(nc.sbuf_tensor("gwTs", [128, 16], F32))
        self.xrest = self.candA[:, :]
        self.gbs = e(nc.sbuf_tensor("gbs", [1, 1], F32))

        self.pA = e(nc.psum_tensor("pA", [128, 512], F32))
        self.pB = e(nc.psum_tensor("pB", [128, 512], F32))
        self.pC = e(nc.psum_tensor("pC", [128, 512], F32))
        self.pD = e(nc.psum_tensor("pD", [128, 512], F32))
        self.pE = e(nc.psum_tensor("pE", [128, 512], F32))
        self.pS = e(nc.psum_tensor("pS", [1, 512], F32))
        self.pBc1 = e(nc.psum_tensor("pBc1", [128, 512], F32))
        self.pBc2 = e(nc.psum_tensor("pBc2", [128, 512], F32))

        self.sem = {}
        self.cnt = {}
        for s in ("dw", "dx", "dg", "do", "pe", "dv", "ac", "pl",
                  "dwA", "dwB", "dwC", "dwD", "dwE", "dgA", "dgB", "dxA", "dxB"):
            self.sem[s] = e(nc.semaphore(s))
            self.cnt[s] = 0
        self.bankc = {}
        self.store_done = 0
        assert nc.sbuf_bytes_remaining >= 0, nc.sbuf_bytes_remaining

    def open_block(self):
        if getattr(self, "blk", None) is not None:
            self.close_block()
        self.blk = self.nc.Block().__enter__()
        self.blk.vector(lambda e: e.nop())
        self.blk.scalar(lambda e: e.nop())
        self.blk.tensor(lambda e: e.nop())
        self.blk.gpsimd(lambda e: e.nop())
        self.blk.sync(lambda e: e.nop())

    def close_block(self):
        if self.blk is not None:
            self.blk.__exit__(None, None, None)
            self.blk = None

    def V(self, f):
        self.blk.vector(f)

    def S(self, f):
        self.blk.scalar(f)

    def T(self, f):
        self.blk.tensor(f)

    def G(self, f):
        self.blk.gpsimd(f)

    def Q(self, f):
        self.blk.sync(f)

    def inc(self, ins, s, n=1):
        ins.then_inc(self.sem[s], n)
        self.cnt[s] += n
        return self.cnt[s]

    def wv(self, eng, s, tgt):
        if tgt > 0:
            eng.wait_ge(self.sem[s], tgt)

    def bank_wait(self, eng, bank):
        c = self.bankc.get(bank.name)
        if c:
            self.wv(eng, c[0], c[1])

    def bank_set(self, bank, s):
        self.bankc[bank.name] = (s, self.cnt[s])

    def Abf(self, c0, c1):
        return self.A32[:, :].bitcast(BF16)[:, c0:c1]

    def Uf32(self, c0, c1):
        return self.UB[:, :].bitcast(F32)[:, c0:c1]

    def wsrc(self, wdram, c0, cols, k0=0, nk=DK):
        return wdram.rearrange("(k p) o -> p k o", p=128)[:, k0:k0 + nk, c0:c0 + cols]

    def wdst(self, tile_ap, cols):
        return tile_ap.rearrange("p (k f) -> p k f", f=cols)

    def dma(self, q, dst, src, s):
        return self.inc(q.dma_start(out=dst, in_=src), s, 16)

    def tap(self, name, srcs):
        if name not in self.taps:
            return
        me = self
        dst = self.dbg[name]
        if True:
            self.open_block()

            def _(g):
                col = 0
                for ap, ncols in srcs:
                    me.dma(g, dst[:, col:col + ncols], ap, "dg")
                    col += ncols
                g.wait_ge(me.sem["dg"], me.cnt["dg"])
            self.G(_)

    def ph_setup(self):
        me = self
        if True:
            self.open_block()

            def _(q):
                me.dma(q, me.masks[:, :], me.masks_d[:, :], "dxA")
                me.dma(q, me.gwTs[:, :], me.gwT[:, :], "dxA")
                me.dma(q, me.gbs[:, :], me.gb[:, :], "dxA")
            self.Q(_)

            def _(g):
                g.iota(me.colidx[:, :], [[1, 256]], base=0, channel_multiplier=0)
                g.memset(me.identb[:, :], 1.0)
                g.affine_select(me.identb[:, :], me.identb[:, :], [[1, 128]],
                                ALU.is_equal, 0.0, base=0, channel_multiplier=-1)
                g.memset(me.identf[:, :], 1.0)
                g.affine_select(me.identf[:, :], me.identf[:, :], [[1, 128]],
                                ALU.is_equal, 0.0, base=0, channel_multiplier=-1)
                g.memset(me.onescol[:, :], 1.0)
                g.memset(me.onesrow[:, :], 1.0)
                ins = g.memset(me.onesrowf[:, :], 1.0)
                me.inc(ins, "pl")
            self.G(_)

            def _(v):
                me.wv(v, "dxA", me.cnt["dxA"])
                me.wv(v, "pl", me.cnt["pl"])
                ins = v.tensor_scalar(me.neggb[:, :], me.gbs[:, :], -1.0, None, op0=ALU.mult)
                me.inc(ins, "dv")
            self.V(_)

    def ph_ln(self, mt, src):
        me = self
        base = mt * TOK
        slabs = [(0, 512), (512, 512), (1024, 256)] if src == "x" else [(0, 512), (512, 512)]
        ow = WINT if src == "x" else TOK
        if True:
            self.open_block()
            apply_done = {}
            bc_prev = 0
            for si, (s0, sw) in enumerate(slabs):
                pi = si % 2
                lsem = "dxA" if pi == 0 else "dxB"
                if src == "x":
                    def _(q, si=si, s0=s0, sw=sw, pi=pi, lsem=lsem):
                        if si >= 2:
                            me.wv(q, "dv", apply_done[si - 2])
                        if mt == 1:
                            me.wv(q, "do", me.store_done)
                        dst = me.A32[:, pi * 8192:pi * 8192 + 8192]
                        me.dma(q, me.wdst(dst, 512)[:, :, 0:sw],
                               me.xs.rearrange("(k p) t -> p k t", p=128)
                               [:, :, base + s0:base + s0 + sw], lsem)
                    self.Q(_)
                ld_tgt = self.cnt[lsem]

                def chunk(dk):
                    if src == "x":
                        return me.A32[:, pi * 8192 + dk * 512:pi * 8192 + dk * 512 + sw]
                    return me.A32[:, dk * TOK + s0:dk * TOK + s0 + sw]

                pe_at_slab_start = self.cnt["pe"]
                stats_tgt = {}
                sbk1 = me.pS if si % 2 == 0 else me.pA
                sbk2 = me.pBc1 if si % 2 == 0 else me.pB
                for dk in range(DK):
                    par = dk % 2

                    def _(sc, dk=dk, sw=sw, par=par, lsem=lsem):
                        if dk == 0 and src == "x":
                            me.wv(sc, lsem, ld_tgt)
                        me.wv(sc, "pe", stats_tgt[dk - 2] if dk > 1 else pe_at_slab_start)
                        ch = chunk(dk)
                        sc.copy(me.xbs[par][:, 0:sw], ch)
                        ins = sc.square(me.sqs[par][:, 0:sw], ch)
                        me.inc(ins, "ac")
                    self.S(_)

                    def _(t, dk=dk, sw=sw, par=par, sbk1=sbk1, sbk2=sbk2):
                        me.wv(t, "ac", me.cnt["ac"])
                        if dk == 0:
                            me.bank_wait(t, sbk1)
                            me.bank_wait(t, sbk2)
                        t.matmul(sbk1[0:1, 0:sw], me.onescol[:, :], me.xbs[par][:, 0:sw],
                                 start=(dk == 0), stop=(dk == DK - 1))
                        ins = t.matmul(sbk2[0:1, 0:sw], me.onescol[:, :], me.sqs[par][:, 0:sw],
                                       start=(dk == 0), stop=(dk == DK - 1))
                        me.inc(ins, "pe")
                        stats_tgt[dk] = me.cnt["pe"]
                    self.T(_)

                def _(v, sw=sw, sbk1=sbk1, sbk2=sbk2):
                    me.wv(v, "pe", me.cnt["pe"])
                    v.tensor_scalar(me.nmr[0:1, 0:sw], sbk1[0:1, 0:sw], -1.0 / D, None, op0=ALU.mult)
                    v.tensor_scalar(me.vrr[0:1, 0:sw], sbk2[0:1, 0:sw], 1.0 / D, None, op0=ALU.mult)
                    v.tensor_mul(me.m2r[0:1, 0:sw], me.nmr[0:1, 0:sw], me.nmr[0:1, 0:sw])
                    v.tensor_sub(me.vrr[0:1, 0:sw], me.vrr[0:1, 0:sw], me.m2r[0:1, 0:sw])
                    ins = v.tensor_scalar(me.vrr[0:1, 0:sw], me.vrr[0:1, 0:sw], EPS, None,
                                          op0=ALU.add)
                    me.inc(ins, "dv")
                self.V(_)
                self.bank_set(sbk1, "dv")

                def _(sc, sw=sw):
                    me.wv(sc, "dv", me.cnt["dv"])
                    ins = sc.activation(me.rsr[0:1, 0:sw], me.vrr[0:1, 0:sw], AF.Sqrt)
                    me.inc(ins, "ac")
                self.S(_)

                def _(v, sw=sw):
                    me.wv(v, "ac", me.cnt["ac"])
                    me.wv(v, "pe", bc_prev)
                    ins = v.reciprocal(me.vrr[0:1, 0:sw], me.rsr[0:1, 0:sw])
                    me.inc(ins, "dv")
                    v.wait_ge(me.sem["dv"], me.cnt["dv"])
                    v.tensor_copy(me.nmb[0:1, 0:sw], me.nmr[0:1, 0:sw])
                    ins = v.tensor_copy(me.rsb[0:1, 0:sw], me.vrr[0:1, 0:sw])
                    me.inc(ins, "dv")
                self.V(_)
                self.bank_set(sbk2, "dv")

                bbk1 = me.pBc2 if si % 2 == 0 else me.pC
                bbk2 = me.pD if si % 2 == 0 else me.pE

                def _(t, sw=sw, bbk1=bbk1, bbk2=bbk2):
                    me.wv(t, "dv", me.cnt["dv"])
                    me.bank_wait(t, bbk1)
                    me.bank_wait(t, bbk2)
                    t.matmul(bbk1[:, 0:sw], me.onesrow[:, :], me.nmb[0:1, 0:sw],
                             start=True, stop=True)
                    ins = t.matmul(bbk2[:, 0:sw], me.onesrow[:, :], me.rsb[0:1, 0:sw],
                                   start=True, stop=True)
                    me.inc(ins, "pe")
                self.T(_)
                bc_tgt = self.cnt["pe"]
                bc_prev = bc_tgt

                for dk in range(DK):
                    def _(v, dk=dk, s0=s0, sw=sw, bbk1=bbk1, bbk2=bbk2):
                        if dk == 0:
                            me.wv(v, "pe", bc_tgt)
                        ch = chunk(dk)
                        v.tensor_add(me.tmpf[:, 0:sw], ch, bbk1[:, 0:sw])
                        ins = v.tensor_mul(me.LNB[:, dk * ow + s0:dk * ow + s0 + sw],
                                           me.tmpf[:, 0:sw], bbk2[:, 0:sw])
                        me.inc(ins, "dv")
                    self.V(_)
                apply_done[si] = self.cnt["dv"]
                self.bank_set(bbk1, "dv")
                self.bank_set(bbk2, "dv")

    def ph_heads(self, mt):
        me = self
        if True:
            self.open_block()
            wld = {}
            war = {0: 0, 1: 0}
            proj_banks = [self.pA, self.pB, self.pE]
            tr_banks = [self.pC, self.pD]
            ctx_banks = [self.pBc1, self.pBc2]
            rot = {"p": 0}
            ctx_done = 0
            tr_done = {}
            exp_done = {}
            attn_done = {}
            ctxevac = {}

            for hp in range(8):
                pi = hp % 2
                woff = pi * 12288

                wsem = "dwA" if pi == 0 else "dwB"

                def _(g, hp=hp, pi=pi, woff=woff, wsem=wsem):
                    me.wv(g, "pe", war[pi])
                    if hp == 0:
                        me.wv(g, "do", me.store_done)
                    for w, wo in ((me.wqP, 0), (me.wkP, 4096), (me.wvP, 8192)):
                        me.dma(g, me.Abf(woff + wo, woff + wo + 4096),
                               w[hp * 128:(hp + 1) * 128, :], wsem)
                self.G(_)
                wld[hp] = (wsem, self.cnt[wsem])

                for h2 in range(2):
                    h = hp * 2 + h2
                    if HEADS_MODE < 2:
                        continue
                    for (name, wo, dst_off, spans) in (
                        ("q", 0, 24576, [(HALO, 512), (HALO + 512, 512)]),
                        ("k", 4096, 25600, [(0, 512), (512, 512), (1024, 256)]),
                        ("v", 8192, 26880, [(0, 512), (512, 512), (1024, 256)]),
                    ):
                        for (c0, cw) in spans:
                            bk = proj_banks[rot["p"] % 3]
                            rot["p"] += 1

                            def _(t, bk=bk, wo=wo, c0=c0, cw=cw, woff=woff, h2=h2, hp=hp):
                                me.wv(t, wld[hp][0], wld[hp][1])
                                me.bank_wait(t, bk)
                                for dk in range(DK):
                                    ins = t.matmul(
                                        bk[:, 0:cw],
                                        me.Abf(woff + wo + dk * 256 + h2 * 128,
                                               woff + wo + dk * 256 + h2 * 128 + 128),
                                        me.LNB[:, dk * WINT + c0:dk * WINT + c0 + cw],
                                        start=(dk == 0), stop=(dk == DK - 1))
                                me.inc(ins, "pe")
                            self.T(_)

                            def _(v, bk=bk, dst_off=dst_off, c0=c0, cw=cw, name=name,
                                  ctx_done=ctx_done):
                                me.wv(v, "pe", me.cnt["pe"])
                                me.wv(v, "pe", ctx_done)
                                b0 = c0 if name != "q" else c0 - HALO
                                ins = v.tensor_copy(me.Abf(dst_off + b0, dst_off + b0 + cw),
                                                    bk[:, 0:cw])
                                me.inc(ins, "dv")
                            self.V(_)
                            self.bank_set(bk, "dv")
                    vproj_done = self.cnt["dv"]
                    if h2 == 1:
                        war[pi] = self.cnt["pe"]

                    if HEADS_MODE < 3:
                        continue
                    for bi_, bstart in enumerate((0, 4, 8)):
                        nb = min(4, 10 - bstart)
                        bk = tr_banks[bi_ % 2]

                        def _(t, bk=bk, bstart=bstart, nb=nb, vproj_done=vproj_done):
                            me.wv(t, "dv", vproj_done)
                            me.bank_wait(t, bk)
                            bb = bk[:, :].bitcast(BF16)
                            for i in range(nb):
                                ins = t.transpose(bb[:, i * 128:(i + 1) * 128],
                                                  me.Abf(26880 + (bstart + i) * 128,
                                                         26880 + (bstart + i) * 128 + 128),
                                                  me.identb[:, :])
                            me.inc(ins, "pe")
                        self.T(_)

                        def _(v, bk=bk, bstart=bstart, nb=nb):
                            me.wv(v, "pe", me.cnt["pe"])
                            bb = bk[:, :].bitcast(BF16)
                            ins = v.tensor_copy(me.Abf(28160 + bstart * 128, 28160 + (bstart + nb) * 128),
                                                bb[:, 0:nb * 128])
                            me.inc(ins, "dv")
                        self.V(_)
                        self.bank_set(bk, "dv")

                    if HEADS_MODE < 4:
                        continue
                    attn_ready = {}
                    attn_war = {0: 0, 1: 0}

                    def stage_front(qp):
                        qt0, qt1 = 2 * qp, 2 * qp + 1
                        pp = qp % 2
                        a0, a1 = 2 * pp, 2 * pp + 1
                        sbk0 = proj_banks[rot["p"] % 3]
                        rot["p"] += 1
                        sbk1 = proj_banks[rot["p"] % 3]
                        rot["p"] += 1

                        def mi_of(qt):
                            if mt == 0 and qt == 0:
                                return 1
                            if mt == 0 and qt == 1:
                                return 2
                            return 0

                        def _(t, sbk0=sbk0, sbk1=sbk1, qt0=qt0, qt1=qt1):
                            me.bank_wait(t, sbk0)
                            ins = t.matmul(sbk0[:, 0:384],
                                           me.Abf(24576 + qt0 * 128, 24576 + qt0 * 128 + 128),
                                           me.Abf(25600 + qt0 * 128, 25600 + qt0 * 128 + 384),
                                           start=True, stop=True)
                            me.inc(ins, "pe")
                            me.bank_wait(t, sbk1)
                            ins = t.matmul(sbk1[:, 0:384],
                                           me.Abf(24576 + qt1 * 128, 24576 + qt1 * 128 + 128),
                                           me.Abf(25600 + qt1 * 128, 25600 + qt1 * 128 + 384),
                                           start=True, stop=True)
                            me.inc(ins, "pe")
                        self.T(_)

                        def _(v, sbk0=sbk0, sbk1=sbk1, qt0=qt0, qt1=qt1):
                            me.wv(v, "pe", me.cnt["pe"])
                            me.wv(v, "ac", exp_done.get(0, 0))
                            v.scalar_tensor_tensor(me.ms[0][:, :], sbk0[:, 0:384], SQH,
                                                   me.masks[:, mi_of(qt0) * 384:mi_of(qt0) * 384 + 384],
                                                   op0=ALU.mult, op1=ALU.add)
                            ins = v.scalar_tensor_tensor(me.ms[1][:, :], sbk1[:, 0:384], SQH,
                                                         me.masks[:, mi_of(qt1) * 384:mi_of(qt1) * 384 + 384],
                                                         op0=ALU.mult, op1=ALU.add)
                            me.inc(ins, "dv")
                            ins = v.reduce_max(me.nmax[0][:, :], me.ms[0][:, :],
                                               axis=AX.X, negate=True)
                            me.inc(ins, "dv")
                            ins = v.reduce_max(me.nmax[1][:, :], me.ms[1][:, :],
                                               axis=AX.X, negate=True)
                            me.inc(ins, "dv")
                        self.V(_)
                        self.bank_set(sbk0, "dv")
                        self.bank_set(sbk1, "dv")

                        def _(sc):
                            me.wv(sc, "dv", me.cnt["dv"])
                            me.wv(sc, "dv", attn_done.get(0, 0))
                            sc.activation(me.probs[0][:, :], me.ms[0][:, :], AF.Exp,
                                          bias=me.nmax[0][:, 0:1], scale=1.0,
                                          accum_out=me.esum[0][:, 0:1])
                            ins = sc.activation(me.probs[1][:, :], me.ms[1][:, :], AF.Exp,
                                                bias=me.nmax[1][:, 0:1], scale=1.0,
                                                accum_out=me.esum[1][:, 0:1])
                            me.inc(ins, "ac")
                        self.S(_)
                        exp_done[0] = self.cnt["ac"]

                        def _(v, a0=a0, a1=a1, pp=pp):
                            me.wv(v, "ac", me.cnt["ac"])
                            me.wv(v, "pe", attn_war[pp])
                            ins = v.reciprocal(me.ersum[0][:, :], me.esum[0][:, :])
                            me.inc(ins, "dv")
                            ins = v.reciprocal(me.ersum[1][:, :], me.esum[1][:, :])
                            me.inc(ins, "dv")
                            v.wait_ge(me.sem["dv"], me.cnt["dv"])
                            v.tensor_scalar(me.attn[a0][:, :], me.probs[0][:, :],
                                            me.ersum[0][:, 0:1], None, op0=ALU.mult)
                            ins = v.tensor_scalar(me.attn[a1][:, :], me.probs[1][:, :],
                                                  me.ersum[1][:, 0:1], None, op0=ALU.mult)
                            me.inc(ins, "dv")
                        self.V(_)
                        attn_done[0] = self.cnt["dv"]
                        attn_ready[qp] = self.cnt["dv"]

                    def stage_back(qp):
                        qt0, qt1 = 2 * qp, 2 * qp + 1
                        pp = qp % 2
                        a0, a1 = 2 * pp, 2 * pp + 1

                        def _(t, a0=a0, a1=a1, qp=qp):
                            me.wv(t, "dv", attn_ready[qp])
                            me.bank_wait(t, me.pC)
                            me.bank_wait(t, me.pD)
                            b0 = me.pC[:, :].bitcast(BF16)
                            b1 = me.pD[:, :].bitcast(BF16)
                            for kk in range(3):
                                t.transpose(b0[:, kk * 128:kk * 128 + 128],
                                            me.attn[a0][:, kk * 128:kk * 128 + 128],
                                            me.identb[:, :])
                            for kk in range(3):
                                ins = t.transpose(b1[:, kk * 128:kk * 128 + 128],
                                                  me.attn[a1][:, kk * 128:kk * 128 + 128],
                                                  me.identb[:, :])
                            me.inc(ins, "pe")
                        self.T(_)
                        attn_war[pp] = self.cnt["pe"]

                        tr_pe = self.cnt["pe"]

                        def _(v):
                            me.wv(v, "pe", tr_pe)
                            me.wv(v, "pe", ctxevac.get(0, 0))
                            ins = v.tensor_copy(me.attnT[0][:, :], me.pC[:, :].bitcast(BF16)[:, 0:384])
                            me.inc(ins, "dv")
                        self.V(_)
                        self.bank_set(me.pC, "dv")

                        def _(sc):
                            me.wv(sc, "pe", tr_pe)
                            me.wv(sc, "pe", ctxevac.get(0, 0))
                            ins = sc.copy(me.attnT[1][:, :], me.pD[:, :].bitcast(BF16)[:, 0:384])
                            me.inc(ins, "ac")
                        self.S(_)
                        self.bank_set(me.pD, "ac")

                        def _(t, qt0=qt0, qt1=qt1):
                            me.wv(t, "dv", me.cnt["dv"])
                            me.wv(t, "ac", me.cnt["ac"])
                            me.bank_wait(t, me.pBc1)
                            me.bank_wait(t, me.pBc2)
                            for kk in range(3):
                                t.matmul(me.pBc1[:, 0:128],
                                         me.Abf(28160 + (qt0 + kk) * 128, 28160 + (qt0 + kk) * 128 + 128),
                                         me.attnT[0][:, kk * 128:kk * 128 + 128],
                                         start=(kk == 0), stop=(kk == 2))
                            for kk in range(3):
                                ins = t.matmul(me.pBc2[:, 0:128],
                                               me.Abf(28160 + (qt1 + kk) * 128, 28160 + (qt1 + kk) * 128 + 128),
                                               me.attnT[1][:, kk * 128:kk * 128 + 128],
                                               start=(kk == 0), stop=(kk == 2))
                            me.inc(ins, "pe")
                        self.T(_)
                        ctxevac[0] = self.cnt["pe"]

                        def _(sc, h=h, qt0=qt0, qt1=qt1):
                            me.wv(sc, "pe", me.cnt["pe"])
                            sc.copy(me.CTX[:, h * TOK + qt0 * 128:h * TOK + qt0 * 128 + 128],
                                    me.pBc1[:, 0:128])
                            ins = sc.copy(me.CTX[:, h * TOK + qt1 * 128:h * TOK + qt1 * 128 + 128],
                                          me.pBc2[:, 0:128])
                            me.inc(ins, "ac")
                        self.S(_)
                        self.bank_set(me.pBc1, "ac")
                        self.bank_set(me.pBc2, "ac")

                    for qp in range(5):
                        if qp < 4:
                            stage_front(qp)
                        if qp >= 1:
                            stage_back(qp - 1)
                    ctx_done = self.cnt["pe"]

    def ph_u(self, mt):
        me = self
        if True:
            self.open_block()
            banks = [self.pA, self.pB]
            war = {0: 0, 1: 0}
            for ec in range(8):
                pi = ec % 2

                wsem = "dwA" if pi == 0 else "dwB"

                def _(g, ec=ec, pi=pi, wsem=wsem):
                    me.wv(g, "pe", war[pi])
                    me.dma(g, me.Abf(pi * 8192, pi * 8192 + 4096),
                           me.wkU[ec * 128:(ec + 1) * 128, :], wsem)
                self.G(_)
                ld = (wsem, self.cnt[wsem])
                for es in range(2):
                    for tp in range(2):
                        bk = banks[(es * 2 + tp) % 2]

                        def _(t, bk=bk, es=es, tp=tp, pi=pi, ld=ld):
                            me.wv(t, ld[0], ld[1])
                            me.bank_wait(t, bk)
                            for dk in range(DK):
                                ins = t.matmul(
                                    bk[:, 0:512],
                                    me.Abf(pi * 8192 + dk * 256 + es * 128,
                                           pi * 8192 + dk * 256 + es * 128 + 128),
                                    me.LNB[:, dk * WINT + HALO + tp * 512:
                                           dk * WINT + HALO + tp * 512 + 512],
                                    start=(dk == 0), stop=(dk == DK - 1))
                            me.inc(ins, "pe")
                        self.T(_)

                        def _(v, bk=bk, ec=ec, es=es, tp=tp):
                            me.wv(v, "pe", me.cnt["pe"])
                            ek = ec * 2 + es
                            ins = v.tensor_copy(me.UB[:, ek * TOK + tp * 512:ek * TOK + tp * 512 + 512],
                                                bk[:, 0:512])
                            me.inc(ins, "dv")
                        self.V(_)
                        self.bank_set(bk, "dv")
                war[pi] = self.cnt["pe"]

    def ph_olocal_gate(self, mt):
        me = self
        if True:
            self.open_block()
            banks = [self.pA, self.pB]
            war = {0: 0, 1: 0}
            gate_banks = {0: self.pS, 1: self.pBc1}
            for oc4 in range(8):
                pi = oc4 % 2

                wsem = "dwA" if pi == 0 else "dwB"

                def _(g, oc4=oc4, pi=pi, wsem=wsem):
                    me.wv(g, "pe", war[pi])
                    me.dma(g, me.LNB[:, pi * 4096:pi * 4096 + 4096],
                           me.woP[oc4 * 128:(oc4 + 1) * 128, :], wsem)
                self.G(_)
                ld = (wsem, self.cnt[wsem])
                for os_ in range(2):
                    oc = oc4 * 2 + os_
                    for tp in range(2):
                        bk = banks[(os_ * 2 + tp) % 2]

                        def _(t, bk=bk, os_=os_, tp=tp, pi=pi, ld=ld):
                            me.wv(t, ld[0], ld[1])
                            me.bank_wait(t, bk)
                            for dk in range(DK):
                                ins = t.matmul(
                                    bk[:, 0:512],
                                    me.LNB[:, pi * 4096 + dk * 256 + os_ * 128:
                                           pi * 4096 + dk * 256 + os_ * 128 + 128],
                                    me.CTX[:, dk * TOK + tp * 512:dk * TOK + tp * 512 + 512],
                                    start=(dk == 0), stop=(dk == DK - 1))
                            me.inc(ins, "pe")
                        self.T(_)

                        def _(v, bk=bk, oc=oc, tp=tp):
                            me.wv(v, "pe", me.cnt["pe"])
                            ins = v.tensor_copy(me.A32[:, oc * TOK + tp * 512:oc * TOK + tp * 512 + 512],
                                                bk[:, 0:512])
                            me.inc(ins, "dv")
                        self.V(_)
                        self.bank_set(bk, "dv")
                war[pi] = self.cnt["pe"]
            olocal_done = self.cnt["dv"]

            for tp in range(2):
                for oc in range(DK):
                    def _(t, oc=oc, tp=tp):
                        if oc == 0:
                            me.wv(t, "dv", olocal_done)
                            me.bank_wait(t, gate_banks[tp])
                        ins = t.matmul(gate_banks[tp][0:1, 0:512], me.gwTs[:, oc:oc + 1],
                                       me.A32[:, oc * TOK + tp * 512:oc * TOK + tp * 512 + 512],
                                       start=(oc == 0), stop=(oc == DK - 1))
                        me.inc(ins, "pe")
                    self.T(_)

            def _(sc):
                me.wv(sc, "pe", me.cnt["pe"])
                sc.activation(me.gtmp[0:1, 0:512], me.pS[0:1, 0:512], AF.Exp,
                              bias=me.neggb[0:1, 0:1], scale=-1.0)
                ins = sc.activation(me.gtmp[0:1, 512:1024], me.pBc1[0:1, 0:512], AF.Exp,
                                    bias=me.neggb[0:1, 0:1], scale=-1.0)
                me.inc(ins, "ac")
            self.S(_)
            self.bank_set(self.pS, "ac")
            self.bank_set(self.pBc1, "ac")

            def _(v):
                me.wv(v, "ac", me.cnt["ac"])
                v.tensor_scalar(me.grow[0:1, :], me.gtmp[0:1, :], 1.0, None, op0=ALU.add)
                ins = v.reciprocal(me.gtmp[0:1, :], me.grow[0:1, :])
                me.inc(ins, "dv")
                v.wait_ge(me.sem["dv"], me.cnt["dv"])
                ins = v.engine_nop()
                me.inc(ins, "dv")
            self.V(_)

            for tp in range(2):
                def _(t, tp=tp):
                    me.wv(t, "dv", me.cnt["dv"])
                    me.bank_wait(t, me.pBc2)
                    ins = t.matmul(me.pBc2[:, 0:512], me.onesrowf[:, :],
                                   me.gtmp[0:1, tp * 512:tp * 512 + 512],
                                   start=True, stop=True)
                    me.inc(ins, "pe")
                self.T(_)

                def _(v, tp=tp):
                    me.wv(v, "pe", me.cnt["pe"])
                    ins = v.tensor_copy(me.bypB[:, tp * 512:tp * 512 + 512], me.pBc2[:, 0:512])
                    me.inc(ins, "dv")
                self.V(_)
                self.bank_set(self.pBc2, "dv")

    def ph_sim_topk(self, mt):
        me = self
        if True:
            self.open_block()
            banks = [self.pA, self.pB]
            war = {0: 0, 1: 0}
            for mc in range(16):
                pi = mc % 2

                wsem = "dwA" if pi == 0 else "dwB"

                def _(g, mc=mc, pi=pi, wsem=wsem):
                    me.wv(g, "pe", war[pi])
                    me.dma(g, me.SS[:, pi * 4096:pi * 4096 + 4096],
                           me.mbP[mc * 128:(mc + 1) * 128, :], wsem)
                self.G(_)
                ld = (wsem, self.cnt[wsem])
                for tt in range(8):
                    bk = banks[tt % 2]

                    def _(t, bk=bk, tt=tt, pi=pi, ld=ld):
                        me.wv(t, ld[0], ld[1])
                        me.bank_wait(t, bk)
                        for ek in range(DK):
                            ins = t.matmul(
                                bk[:, 0:256],
                                me.UB[:, ek * TOK + tt * 128:ek * TOK + tt * 128 + 128],
                                me.SS[:, pi * 4096 + ek * 256:pi * 4096 + ek * 256 + 256],
                                start=(ek == 0), stop=(ek == DK - 1))
                        me.inc(ins, "pe")
                    self.T(_)

                    def _(v, bk=bk, mc=mc, tt=tt):
                        me.wv(v, "pe", me.cnt["pe"])
                        bi = bk[:, :].bitcast(I32)
                        ins = v.tensor_scalar(me.packed[:, :].bitcast(I32), bi[:, 0:256],
                                              -4096, mc << 8, op0=ALU.bitwise_and,
                                              op1=ALU.bitwise_or)
                        me.inc(ins, "dv")
                        v.tensor_tensor(me.packed[:, :].bitcast(I32),
                                        me.packed[:, :].bitcast(I32),
                                        me.colidx[:, :], op=ALU.bitwise_or)
                        ins = v.max(out=me.candA[:, tt * 128 + mc * 8:tt * 128 + mc * 8 + 8],
                                    in_=me.packed[:, :])
                        me.inc(ins, "dv")
                    self.V(_)
                    self.bank_set(bk, "dv")
                war[pi] = self.cnt["pe"]

            for tt in range(8):
                def _(v, tt=tt):
                    me.wv(v, "dv", me.cnt["dv"])
                    ins = v.max(out=me.run8[:, :], in_=me.candA[:, tt * 128:tt * 128 + 128])
                    me.inc(ins, "dv")
                    v.wait_ge(me.sem["dv"], me.cnt["dv"])
                    v.tensor_scalar(me.idxA[:, tt * 8:tt * 8 + 8], me.run8[:, :].bitcast(I32),
                                    4095, None, op0=ALU.bitwise_and)
                    ins = v.tensor_scalar(me.vals8[:, :].bitcast(I32), me.run8[:, :].bitcast(I32),
                                          -4096, None, op0=ALU.bitwise_and)
                    me.inc(ins, "dv")
                    v.wait_ge(me.sem["dv"], me.cnt["dv"])
                    ins = v.tensor_scalar(me.vals8[:, :], me.vals8[:, :], SQD, None, op0=ALU.mult)
                    me.inc(ins, "dv")
                    v.wait_ge(me.sem["dv"], me.cnt["dv"])
                    ins = v.reduce_max(me.nmax[0][:, :], me.vals8[:, :], axis=AX.X, negate=True)
                    me.inc(ins, "dv")
                self.V(_)

                def _(sc):
                    me.wv(sc, "dv", me.cnt["dv"])
                    ins = sc.activation(me.ev8[:, :], me.vals8[:, :], AF.Exp,
                                        bias=me.nmax[0][:, 0:1], scale=1.0,
                                        accum_out=me.esum[0][:, 0:1])
                    me.inc(ins, "ac")
                self.S(_)

                def _(v, tt=tt):
                    me.wv(v, "ac", me.cnt["ac"])
                    ins = v.reciprocal(me.ersum[0][:, :], me.esum[0][:, :])
                    me.inc(ins, "dv")
                    v.wait_ge(me.sem["dv"], me.cnt["dv"])
                    ins = v.tensor_scalar(me.w8A[:, tt * 8:tt * 8 + 8], me.ev8[:, :],
                                          me.ersum[0][:, 0:1], None, op0=ALU.mult)
                    me.inc(ins, "dv")
                self.V(_)

    def ph_mem(self, mt):
        me = self
        base = mt * TOK
        if True:
            self.open_block()
            tr_banks = [self.pC, self.pD]
            banks = [self.pA, self.pB]
            mac_done = {}
            ytm_war = {0: 0, 1: 0}
            for half in range(2):
                for j in range(4):
                    tt = half * 4 + j
                    yo = 2048 + (j % 2) * 2048
                    for k in range(8):
                        seq = tt * 8 + k

                        gsem = "dgA" if k % 2 == 0 else "dgB"

                        def _(g, tt=tt, k=k, seq=seq, gsem=gsem):
                            me.wv(g, "dv", mac_done.get(seq - 2, 0))
                            ins = g.indirect_dma_start(
                                out=me.UB[:, (k % 2) * 2048:(k % 2) * 2048 + 2048],
                                out_offset=None, in_=me.mb[:, :],
                                in_offset=bass.IndirectOffsetOnAxis(
                                    me.idxA[:, tt * 8 + k:tt * 8 + k + 1], axis=0))
                            me.inc(ins, gsem, 16)
                        self.G(_)

                        def _(v, tt=tt, k=k, yo=yo, j=j, gsem=gsem):
                            me.wv(v, gsem, me.cnt[gsem])
                            if k == 0:
                                me.wv(v, "ac", ytm_war[j % 2])
                            gbuf = me.UB[:, (k % 2) * 2048:(k % 2) * 2048 + 2048]
                            ytm = me.Uf32(yo, yo + 2048)
                            if k == 0:
                                ins = v.tensor_scalar(ytm, gbuf, me.w8A[:, tt * 8:tt * 8 + 1],
                                                      None, op0=ALU.mult)
                            else:
                                ins = v.scalar_tensor_tensor(
                                    ytm, gbuf, me.w8A[:, tt * 8 + k:tt * 8 + k + 1],
                                    ytm, op0=ALU.mult, op1=ALU.add)
                            me.inc(ins, "dv")
                        self.V(_)
                        mac_done[seq] = self.cnt["dv"]

                    for bq in range(4):
                        bk = tr_banks[bq % 2]

                        def _(t, bk=bk, bq=bq, yo=yo):
                            me.wv(t, "dv", me.cnt["dv"])
                            me.bank_wait(t, bk)
                            for i in range(4):
                                ins = t.transpose(
                                    bk[:, i * 128:i * 128 + 128],
                                    me.Uf32(yo + (bq * 4 + i) * 128, yo + (bq * 4 + i) * 128 + 128),
                                    me.identf[:, :])
                            me.inc(ins, "pe")
                        self.T(_)

                        def _(sc, bk=bk, bq=bq, half=half, j=j):
                            me.wv(sc, "pe", me.cnt["pe"])
                            for i in range(4):
                                dk = bq * 4 + i
                                ins = sc.copy(
                                    me.CTX[:, dk * TOK + half * 512 + j * 128:
                                           dk * TOK + half * 512 + j * 128 + 128],
                                    bk[:, i * 128:i * 128 + 128])
                            me.inc(ins, "ac")
                        self.S(_)
                        self.bank_set(bk, "ac")
                    ytm_war[j % 2] = self.cnt["ac"]
                ypre_done = self.cnt["ac"]

                sswar = self.cnt["pe"]
                zwar = {0: sswar, 1: sswar}
                for zc in range(8):
                    pi = zc % 2

                    wsem = "dwA" if pi == 0 else "dwB"

                    def _(g, zc=zc, pi=pi, wsem=wsem):
                        me.wv(g, "pe", zwar[pi])
                        me.dma(g, me.SS[:, pi * 4096:pi * 4096 + 4096],
                               me.wvP[zc * 128:(zc + 1) * 128, :], wsem)
                    self.G(_)
                    ld = (wsem, self.cnt[wsem])
                    for ic2 in range(2):
                        ic = zc * 2 + ic2
                        bk = banks[ic % 2]

                        def _(t, bk=bk, ic2=ic2, pi=pi, ld=ld, half=half):
                            me.wv(t, ld[0], ld[1])
                            me.wv(t, "ac", ypre_done)
                            me.bank_wait(t, bk)
                            for dk in range(DK):
                                ins = t.matmul(
                                    bk[:, 0:512],
                                    me.SS[:, pi * 4096 + dk * 256 + ic2 * 128:
                                           pi * 4096 + dk * 256 + ic2 * 128 + 128],
                                    me.CTX[:, dk * TOK + half * 512:dk * TOK + half * 512 + 512],
                                    start=(dk == 0), stop=(dk == DK - 1))
                            me.inc(ins, "pe")
                        self.T(_)

                        def _(v, bk=bk, ic=ic, half=half):
                            me.wv(v, "pe", me.cnt["pe"])
                            ins = v.tensor_copy(
                                me.LNB[:, ic * TOK + half * 512:ic * TOK + half * 512 + 512],
                                bk[:, 0:512])
                            me.inc(ins, "dv")
                        self.V(_)
                        self.bank_set(bk, "dv")
                    zwar[pi] = self.cnt["pe"]

                z_done = self.cnt["dv"]
                sswar2 = self.cnt["pe"]
                owar = {0: sswar2, 1: sswar2}
                xw0 = self.cnt["dv"]
                xres_war = {0: xw0, 1: xw0}
                for oc2 in range(8):
                    pi = oc2 % 2

                    wsem = "dwA" if pi == 0 else "dwB"

                    def _(g, oc2=oc2, pi=pi, wsem=wsem):
                        me.wv(g, "pe", owar[pi])
                        me.dma(g, me.SS[:, pi * 4096:pi * 4096 + 4096],
                               me.woP[oc2 * 128:(oc2 + 1) * 128, :], wsem)
                    self.G(_)
                    ld = (wsem, self.cnt[wsem])
                    for o2 in range(2):
                        oc = oc2 * 2 + o2
                        bk = banks[oc % 2]
                        xb = (oc % 2) * 512

                        xsem = "dxA" if oc % 2 == 0 else "dxB"

                        def _(q, oc=oc, xb=xb, half=half, xsem=xsem):
                            me.wv(q, "dv", xres_war[oc % 2])
                            src = me.xs.rearrange("(k p) t -> p k t", p=128)[
                                :, oc:oc + 1, HALO + base + half * 512:HALO + base + half * 512 + 512]
                            me.dma(q, me.xrest[:, xb:xb + 512].rearrange("p (a t) -> p a t", a=1),
                                   src, xsem)
                        self.Q(_)
                        xld = (xsem, self.cnt[xsem])

                        def _(t, bk=bk, o2=o2, pi=pi, ld=ld, half=half):
                            me.wv(t, ld[0], ld[1])
                            me.wv(t, "dv", z_done)
                            me.bank_wait(t, bk)
                            for ik in range(DK):
                                ins = t.matmul(
                                    bk[:, 0:512],
                                    me.SS[:, pi * 4096 + ik * 256 + o2 * 128:
                                           pi * 4096 + ik * 256 + o2 * 128 + 128],
                                    me.LNB[:, ik * TOK + half * 512:ik * TOK + half * 512 + 512],
                                    start=(ik == 0), stop=(ik == DK - 1))
                            me.inc(ins, "pe")
                        self.T(_)

                        def _(v, bk=bk, oc=oc, xb=xb, half=half, xld=xld):
                            me.wv(v, "pe", me.cnt["pe"])
                            me.wv(v, xld[0], xld[1])
                            a = me.A32[:, oc * TOK + half * 512:oc * TOK + half * 512 + 512]
                            v.tensor_mul(me.tmpf[:, :], bk[:, 0:512],
                                         me.bypB[:, half * 512:half * 512 + 512])
                            v.tensor_add(me.tmpf[:, :], me.tmpf[:, :], a)
                            ins = v.tensor_add(a, me.tmpf[:, :], me.xrest[:, xb:xb + 512])
                            me.inc(ins, "dv")
                        self.V(_)
                        self.bank_set(bk, "dv")
                        xres_war[oc % 2] = self.cnt["dv"]
                    owar[pi] = self.cnt["pe"]
            self.close_block()

    def ph_ffn(self, mt):
        me = self
        if True:
            self.open_block()
            prev_sp_pl = 0
            prev_sp_dv = 0
            dCD = [self.pC, self.pD]
            QS = 6144
            gupld = {}
            ju_done = {}
            dload = {}
            sp_arr = {}
            down_done = {}

            JSEM = ("dwA", "dwB", "dwC", "dwD")

            def load_gu(sg, j):
                js = JSEM[j]

                def _(g, sg=sg, j=j, js=js):
                    me.wv(g, "pe", ju_done.get((sg - 1, j), 0))
                    r0 = (sg * 4 + j) * 128
                    me.dma(g, me.CTX[:, j * 2048:j * 2048 + 2048],
                           me.wgP[r0:r0 + 128, :], js)
                    me.dma(g, me.CTX[:, 8192 + j * 2048:8192 + j * 2048 + 2048],
                           me.wuP[r0:r0 + 128, :], js)
                self.G(_)
                gupld[(sg, j)] = (js, self.cnt[js])

            def load_down(sg):
                def _(g, sg=sg):
                    me.wv(g, "pe", down_done.get(sg - 1, 0))
                    me.dma(g, me.SS[:, 0:8192], me.wdP[sg * 128:(sg + 1) * 128, :], "dwE")
                self.G(_)
                dload[sg] = self.cnt["dwE"]

            def hsp(par, c0, c1):
                if par == 0:
                    return me.UB[:, c0:c1]
                return me.LNB[:, 16384 + c0:16384 + c1]

            def emit_down(sg):
                par = sg % 2
                for oc in range(16):
                    for tp in range(2):
                        bk = dCD[(oc * 2 + tp) % 2]

                        def _(t, bk=bk, oc=oc, tp=tp, sg=sg, par=par):
                            me.wv(t, "dwE", dload[sg])
                            me.wv(t, "dv", sp_arr[sg])
                            me.bank_wait(t, bk)
                            for ic in range(4):
                                ins = t.matmul(
                                    bk[:, 0:512],
                                    me.SS[:, ic * 2048 + oc * 128:ic * 2048 + oc * 128 + 128],
                                    hsp(par, ic * 1024 + tp * 512, ic * 1024 + tp * 512 + 512),
                                    start=(ic == 0), stop=(ic == 3))
                            me.inc(ins, "pe")
                        self.T(_)

                        def _(v, bk=bk, oc=oc, tp=tp):
                            me.wv(v, "pe", me.cnt["pe"])
                            a = me.A32[:, oc * TOK + tp * 512:oc * TOK + tp * 512 + 512]
                            ins = v.tensor_add(a, a, bk[:, 0:512])
                            me.inc(ins, "dv")
                        self.V(_)
                        self.bank_set(bk, "dv")
                down_done[sg] = self.cnt["pe"]

            for j in range(4):
                load_gu(0, j)
            load_down(0)

            for sg in range(16):
                gbanks = [self.pA, self.pE]
                ubanks = [self.pB, self.pBc1]
                stmp = [QS, QS + 1024]
                stmp_war = {0: prev_sp_dv, 1: prev_sp_dv}
                for j in range(4):
                    for tp in range(2):
                        par = (j * 2 + tp) % 2
                        gbk = gbanks[par]
                        ubk = ubanks[par]

                        def _(t, j=j, tp=tp, gbk=gbk, sg=sg):
                            me.wv(t, gupld[(sg, j)][0], gupld[(sg, j)][1])
                            me.bank_wait(t, gbk)
                            for dk in range(DK):
                                ins = t.matmul(
                                    gbk[:, 0:512],
                                    me.CTX[:, j * 2048 + dk * 128:j * 2048 + dk * 128 + 128],
                                    me.LNB[:, dk * TOK + tp * 512:dk * TOK + tp * 512 + 512],
                                    start=(dk == 0), stop=(dk == DK - 1))
                            me.inc(ins, "pe")
                        self.T(_)
                        gate_pe = self.cnt["pe"]

                        def _(t, j=j, tp=tp, ubk=ubk, sg=sg):
                            me.bank_wait(t, ubk)
                            for dk in range(DK):
                                ins = t.matmul(
                                    ubk[:, 0:512],
                                    me.CTX[:, 8192 + j * 2048 + dk * 128:8192 + j * 2048 + dk * 128 + 128],
                                    me.LNB[:, dk * TOK + tp * 512:dk * TOK + tp * 512 + 512],
                                    start=(dk == 0), stop=(dk == DK - 1))
                            me.inc(ins, "pe")
                        self.T(_)
                        if tp == 1:
                            ju_done[(sg, j)] = self.cnt["pe"]
                            if sg < 15:
                                load_gu(sg + 1, j)

                        def _(sc, j=j, tp=tp, gbk=gbk, par=par, gate_pe=gate_pe):
                            me.wv(sc, "pe", gate_pe)
                            me.wv(sc, "dv", stmp_war[par])
                            if j == 0 and tp == 0:
                                me.wv(sc, "pl", prev_sp_pl)
                            ins = sc.activation(
                                me.Uf32(stmp[par], stmp[par] + 1024)[:, tp * 512:tp * 512 + 512],
                                gbk[:, 0:512], AF.Silu)
                            me.inc(ins, "ac")
                        self.S(_)
                        self.bank_set(gbk, "ac")

                        def _(v, j=j, tp=tp, ubk=ubk, par=par):
                            me.wv(v, "pe", me.cnt["pe"])
                            me.wv(v, "ac", me.cnt["ac"])
                            if j == 0 and tp == 0:
                                me.wv(v, "pl", prev_sp_pl)
                            ins = v.tensor_mul(
                                me.Uf32(2048 + j * 1024, 2048 + j * 1024 + 1024)[:, tp * 512:tp * 512 + 512],
                                me.Uf32(stmp[par], stmp[par] + 1024)[:, tp * 512:tp * 512 + 512],
                                ubk[:, 0:512])
                            me.inc(ins, "dv")
                        self.V(_)
                        self.bank_set(ubk, "dv")
                        stmp_war[par] = self.cnt["dv"]
                p_done = self.cnt["dv"]

                if sg >= 1:
                    emit_down(sg - 1)
                    if sg < 15:
                        load_down(sg)

                def pq(j, q4):
                    return me.Uf32(2048 + j * 1024 + q4 * 256, 2048 + j * 1024 + q4 * 256 + 256)

                spar = sg % 2
                for q4 in range(4):
                    sj = [me.Uf32(QS + 256 * j, QS + 256 * (j + 1)) for j in range(4)]
                    sA = me.Uf32(QS + 1024, QS + 1280)
                    sB = me.Uf32(QS + 1280, QS + 1536)
                    sC = me.Uf32(QS + 1536, QS + 1792)
                    sD = me.Uf32(QS + 1792, QS + 2048)

                    def _(g, q4=q4, sj=sj):
                        me.wv(g, "dv", p_done if q4 == 0 else me.cnt["dv"])
                        for j in range(4):
                            ins = g.tensor_mul(sj[j], pq(j, q4), pq(j, q4))
                        me.inc(ins, "pl")
                    self.G(_)

                    def _(v, sj=sj, sA=sA, sB=sB, sC=sC, sD=sD):
                        me.wv(v, "pl", me.cnt["pl"])
                        v.tensor_tensor(sA, sj[0], sj[1], op=ALU.max)
                        v.tensor_tensor(sB, sj[0], sj[1], op=ALU.min)
                        v.tensor_tensor(sC, sj[2], sj[3], op=ALU.max)
                        v.tensor_tensor(sD, sj[2], sj[3], op=ALU.min)
                        v.tensor_tensor(sA, sA, sC, op=ALU.min)
                        v.tensor_tensor(sB, sB, sD, op=ALU.max)
                        v.tensor_tensor(sA, sA, sB, op=ALU.max)
                        for j in range(4):
                            ins = v.tensor_tensor(sj[j], sj[j], sA, op=ALU.is_ge)
                        me.inc(ins, "dv")
                    self.V(_)

                    def _(g, q4=q4, sj=sj, sg=sg, spar=spar):
                        me.wv(g, "dv", me.cnt["dv"])
                        if q4 == 0:
                            me.wv(g, "pe", down_done.get(sg - 2, 0))
                        for j in range(4):
                            ins = g.tensor_mul(
                                hsp(spar, j * 1024 + q4 * 256, j * 1024 + q4 * 256 + 256),
                                sj[j], pq(j, q4))
                        me.inc(ins, "dv")
                    self.G(_)
                sp_arr[sg] = self.cnt["dv"]
                prev_sp_pl = self.cnt["pl"]
                prev_sp_dv = sp_arr[sg]
            load_down(15)
            emit_down(15)

    def ph_store(self, mt):
        me = self
        if True:
            self.open_block()

            def _(q):
                me.dma(q, me.out.rearrange("(k p) t -> p k t", p=128)[:, :, mt * TOK:mt * TOK + TOK],
                       me.wdst(me.A32[:, :], TOK), "do")
            self.Q(_)
            self.store_done = self.cnt["do"]

    def ph_final(self):
        me = self
        if True:
            self.open_block()

            def _(q):
                q.wait_ge(me.sem["do"], me.cnt["do"])
            self.Q(_)

    def build(self):
        self.marks = []

        def mark(name):
            self.marks.append((name, self.nc.n_instructions()))
        self.mark = mark
        mark("setup")
        self.ph_setup()
        for mt in (0, 1):
            u = self.upto
            self.mark(f"ln1.{mt}")
            self.ph_ln(mt, "x")
            self.tap(f"ln1h{mt}", [(self.LNB[:, :], 20480)])
            if u >= 1:
                self.mark(f"heads.{mt}")
                self.ph_heads(mt)
                self.tap(f"ctx{mt}", [(self.CTX[:, :], 16384)])
            if u >= 2:
                self.mark(f"u.{mt}")
                self.ph_u(mt)
                self.tap(f"u{mt}", [(self.UB[:, :], 16384)])
            if u >= 3:
                self.mark(f"olg.{mt}")
                self.ph_olocal_gate(mt)
                self.tap(f"ol{mt}", [(self.A32[:, :], 16384)])
                self.tap(f"byp{mt}", [(self.bypB[:, :], 1024)])
            if u >= 4:
                self.mark(f"sim.{mt}")
                self.ph_sim_topk(mt)
                self.tap(f"idx{mt}", [(self.idxA[:, :], 64), (self.w8A[:, :], 64)])
            if u >= 5:
                self.mark(f"mem.{mt}")
                self.ph_mem(mt)
                self.tap(f"yp{mt}", [(self.CTX[:, :], 16384)])
                self.tap(f"z{mt}", [(self.LNB[:, 0:16384], 16384)])
                self.tap(f"xp{mt}", [(self.A32[:, :], 16384)])
            if u >= 6:
                self.mark(f"ln2.{mt}")
                self.ph_ln(mt, "acc")
                self.tap(f"ln2_{mt}", [(self.LNB[:, 0:16384], 16384)])
            if u >= 7:
                self.mark(f"ffn.{mt}")
                self.ph_ffn(mt)
                self.tap(f"ff{mt}", [(self.A32[:, :], 16384)])
            self.mark(f"store.{mt}")
            self.ph_store(mt)
        self.mark("final")
        self.ph_final()
        self.close_block()
        self.nc.finalize()
        return self.nc


_CACHE = {}


def _build_masks():
    r = np.arange(128)[:, None]
    c = np.arange(384)[None, :]
    band = (c >= r + 1) & (c <= r + 256)
    mg = np.where(band, 0.0, -1.0e30).astype(np.float32)
    m0 = np.where(band & (c >= 256), 0.0, -1.0e30).astype(np.float32)
    m1 = np.where(band & (c >= 128), 0.0, -1.0e30).astype(np.float32)
    return mg, m0, m1


def _chunked(wT, ncol):
    Dr, X = wT.shape
    nch = X // ncol
    return np.ascontiguousarray(
        wT.reshape(16, 128, nch, ncol).transpose(2, 1, 0, 3).reshape(nch * 128, 16 * ncol))


def _wd_chunked(wd):
    return np.ascontiguousarray(
        wd.reshape(16, 4, 128, 2048).transpose(0, 2, 1, 3).reshape(2048, 8192))


def _ffn_perm():
    perm = np.empty(I2, dtype=np.int64)
    idx = 0
    for sg in range(16):
        for j in range(4):
            for gw in range(128):
                perm[idx] = 4 * (sg * 128 + gw) + j
                idx += 1
    return perm


def kernel(L0_attn_hidden_states, L0_attn_wq, L0_attn_wk, L0_attn_wv, L0_attn_wo,
           L0_attn_gate_w, L0_attn_gate_b, L0_attn_mem_bank,
           L0_ffn_w_gate, L0_ffn_w_up, L0_ffn_w_down):
    x = np.asarray(L0_attn_hidden_states, dtype=np.float32)
    f32c = lambda a: np.ascontiguousarray(np.asarray(a, dtype=np.float32))
    wq, wk, wv, wo = map(f32c, (L0_attn_wq, L0_attn_wk, L0_attn_wv, L0_attn_wo))
    gate_w = f32c(L0_attn_gate_w)
    gate_b = f32c(L0_attn_gate_b).reshape(1, 1)
    mbank = f32c(L0_attn_mem_bank)
    wg_, wu_, wd_ = map(f32c, (L0_ffn_w_gate, L0_ffn_w_up, L0_ffn_w_down))

    perm = _ffn_perm()
    shared = {
        "wqP": _chunked(wq.T, 256), "wkP": _chunked(wk.T, 256),
        "wvP": _chunked(wv.T, 256), "woP": _chunked(wo.T, 256),
        "wkU": _chunked(wk, 256), "mbP": _chunked(mbank.T, 256),
        "mb": mbank,
        "gwT": np.ascontiguousarray(gate_w.reshape(16, 128).T),
        "gb": gate_b,
        "wgP": _chunked(wg_[:, perm], 128),
        "wuP": _chunked(wu_[:, perm], 128),
        "wdP": _wd_chunked(wd_[perm, :]),
    }
    mg, m0, m1 = _build_masks()
    masks_h0 = np.ascontiguousarray(np.concatenate([mg, m0, m1], axis=1))
    masks_h1 = np.ascontiguousarray(np.concatenate([mg, mg, mg], axis=1))

    in_maps = []
    for c in range(8):
        b, half = c // 2, c % 2
        xT = x[b].T
        if half == 0:
            xsl = np.concatenate([np.zeros((D, HALO), np.float32), xT[:, 0:2048]], axis=1)
        else:
            xsl = xT[:, 2048 - HALO:4096]
        m = dict(shared)
        m["xs"] = np.ascontiguousarray(xsl)
        m["masks"] = masks_h0 if half == 0 else masks_h1
        in_maps.append(m)

    if "nc" not in _CACHE:
        kb = KB()
        _CACHE["nc"] = kb.build()
    nc = _CACHE["nc"]
    res = run_bass_kernel_spmd(nc, in_maps, core_ids=list(range(8)))

    out = np.empty((4, 4096, D), dtype=np.float32)
    for c in range(8):
        b, half = c // 2, c % 2
        out[b, half * 2048:(half + 1) * 2048, :] = res.results[c]["out"].T
    return out
